# version 06
# baseline (speedup 1.0000x reference)
import os
import numpy as np
import concourse.bass as bass
import concourse.mybir as mybir
from concourse.bass_utils import run_bass_kernel_spmd

F32 = mybir.dt.float32
BF16 = mybir.dt.bfloat16
I32 = mybir.dt.int32
AF = mybir.ActivationFunctionType
ALU = mybir.AluOpType
AX = mybir.AxisListType

N_CORES = 8
B, T, C = 2, 16384, 1024
P_TOK = 4096
HW = 1024
HG = 512
EXT = HW + HG + P_TOK
CH = 512
NCH = EXT // CH
RND_C = 3.0 * 2.0 ** 22


_DT_SIZES = {}


def _dtsize(dt):
    s = _DT_SIZES.get(dt)
    if s is None:
        s = np.dtype(mybir.dt.np(dt)).itemsize
        _DT_SIZES[dt] = s
    return s


class _ApTensor:
    def __init__(self, apv):
        self._ap = apv

    def ap(self):
        return self._ap


class ASB:

    ENGINES = ("PE", "ACT", "DVE", "POOL", "SP")

    NQSEM = 4

    def __init__(self, nc):
        self.nc = nc
        self.eng = {"PE": nc.tensor, "ACT": nc.scalar, "DVE": nc.vector,
                    "POOL": nc.gpsimd, "SP": nc.sync}
        self.sem = {}
        self.cnt = {}
        self.incs = {}
        self.insts = {}
        for e in self.ENGINES:
            self.sem[e] = nc.alloc_semaphore(f"es_{e}")
            self.cnt[e] = 0
            self.incs[e] = []
            self.insts[e] = []
        self.q = {}
        for e in self.ENGINES:
            qn = "Q" + e
            self.q[qn] = {
                "sems": [nc.alloc_semaphore(f"qs_{e}{i}") for i in range(self.NQSEM)],
                "epoch": 0,
                "ecnt": 0,
                "etotal": [],
                "stot": [0] * self.NQSEM,
                "gated": [True] * self.NQSEM,
            }
        self.wm = {w: {} for w in self.ENGINES}
        self.reg = {}

    def _q_close(self, qn):
        st = self.q[qn]
        if st["ecnt"] == 0:
            return
        s = st["epoch"] % self.NQSEM
        st["stot"][s] += st["ecnt"]
        st["etotal"].append((s, st["stot"][s] * 16))
        st["epoch"] += 1
        st["ecnt"] = 0
        st["gated"][st["epoch"] % self.NQSEM] = False

    def _q_wait(self, waiter, qn, epoch):
        st = self.q[qn]
        if epoch == st["epoch"]:
            self._q_close(qn)
        s, val = st["etotal"][epoch]
        key = (qn, s)
        if self.wm[waiter].get(key, 0) < val:
            self.eng[waiter].wait_ge(st["sems"][s], val)
            self.wm[waiter][key] = val

    @staticmethod
    def _iv(ap):
        off = ap.offset
        if not isinstance(off, int):
            off = 0
        dims = list(ap.ap)
        step0, count0 = dims[0]
        esz = _dtsize(ap.dtype)
        span = 0
        for stp, cnt in dims[1:]:
            if stp > 0:
                span += stp * (cnt - 1)
        is_ps = "PSUM" in str(getattr(ap, "space", ""))
        if step0 > 0:
            p_lo = off // step0
            b_lo = off % step0
            if b_lo + span >= step0:
                return (ap.tensor.name, p_lo, p_lo + count0 + (b_lo + span) // step0,
                        0, step0 * esz)
            lo_b = b_lo * esz
            hi_b = (b_lo + span + 1) * esz
            if is_ps:
                lo_b = lo_b // 2048 * 2048
                hi_b = (hi_b + 2047) // 2048 * 2048
            return (ap.tensor.name, p_lo, p_lo + count0, lo_b, hi_b)
        return (ap.tensor.name, 0, count0, off * esz, (off + span + 1) * esz)

    def _edge(self, waiter, src, pos):
        if isinstance(src, tuple):
            self._q_wait(waiter, src[0], src[1])
            return
        incs = self.incs[src]
        val = None
        if incs and incs[-1][0] >= pos:
            lo, hi_i = 0, len(incs) - 1
            while lo < hi_i:
                mid = (lo + hi_i) // 2
                if incs[mid][0] >= pos:
                    hi_i = mid
                else:
                    lo = mid + 1
            val = incs[lo][1]
        else:
            inst = self.insts[src][pos]
            val = (incs[-1][1] if incs else 0) + 1
            inst.then_inc(self.sem[src], 1)
            incs.append((pos, val))
        if self.wm[waiter].get(src, 0) < val:
            self.eng[waiter].wait_ge(self.sem[src], val)
            self.wm[waiter][src] = val

    def _deps(self, waiter, reads, writes):
        need = {}
        for ap in reads:
            name, pl, ph, bl, bh = self._iv(ap)
            for rec in self.reg.get(name, ()):
                if rec[0] == "W" and rec[3] < ph and pl < rec[4] \
                        and rec[5] < bh and bl < rec[6]:
                    k = rec[1]
                    if need.get(k, -1) < rec[2]:
                        need[k] = rec[2]
        for ap in writes:
            name, pl, ph, bl, bh = self._iv(ap)
            for rec in self.reg.get(name, ()):
                if rec[3] < ph and pl < rec[4] and rec[5] < bh and bl < rec[6]:
                    k = rec[1]
                    if need.get(k, -1) < rec[2]:
                        need[k] = rec[2]
        for src, pos in need.items():
            self._edge(waiter, src, pos)

    def _record(self, src, pos, reads, writes):
        for ap in reads:
            name, pl, ph, bl, bh = self._iv(ap)
            self.reg.setdefault(name, []).append(("R", src, pos, pl, ph, bl, bh))
        for ap in writes:
            name, pl, ph, bl, bh = self._iv(ap)
            recs = self.reg.setdefault(name, [])
            recs[:] = [r for r in recs
                       if not (pl <= r[3] and r[4] <= ph and bl <= r[5] and r[6] <= bh)]
            recs.append(("W", src, pos, pl, ph, bl, bh))

    def op(self, e, emit, reads=(), writes=(), accum=False):
        if accum:
            self._deps(e, reads, ())
        else:
            self._deps(e, reads, writes)
        inst = emit(self.eng[e])
        pos = self.cnt[e]
        self.insts[e].append(inst)
        self.cnt[e] = pos + 1
        if accum:
            for ap in writes:
                name, pl, ph, bl, bh = self._iv(ap)
                for r in self.reg.get(name, ()):
                    if r[0] == "W" and r[3] == pl and r[4] == ph \
                            and r[5] == bl and r[6] == bh:
                        self.reg[name].remove(r)
                        break
                self.reg.setdefault(name, []).append(("W", e, pos, pl, ph, bl, bh))
            for ap in reads:
                name, pl, ph, bl, bh = self._iv(ap)
                self.reg.setdefault(name, []).append(("R", e, pos, pl, ph, bl, bh))
        else:
            self._record(e, pos, reads, writes)
        return inst

    def mm(self, out, lhsT, rhs, start, stop):
        return self.op("PE",
                       lambda t: t.matmul(out, lhsT, rhs, start=start, stop=stop),
                       reads=[lhsT, rhs], writes=[out], accum=not start)

    def dma(self, e, out, in_, **kw):
        qn = "Q" + e
        st = self.q[qn]
        self._deps(e, [in_], [out])
        s = st["epoch"] % self.NQSEM
        if st["ecnt"] == 0 and not st["gated"][s] and st["stot"][s] > 0:
            self.eng[e].wait_ge(st["sems"][s], st["stot"][s] * 16)
            st["gated"][s] = True
        inst = self.eng[e].dma_start(out, in_, **kw)
        self.cnt[e] += 1
        self.insts[e].append(inst)
        inst.then_inc(st["sems"][s], 16)
        st["ecnt"] += 1
        self._record((qn, st["epoch"]), 0, [in_], [out])
        return inst

    def finish(self, e="POOL"):
        for qn, st in self.q.items():
            self._q_close(qn)
            for s in range(self.NQSEM):
                if st["stot"][s] > 0:
                    key = (qn, s)
                    val = st["stot"][s] * 16
                    if self.wm[e].get(key, 0) < val:
                        self.eng[e].wait_ge(st["sems"][s], val)
                        self.wm[e][key] = val


def load_grid(a, nc, name, W_ap, nrows, ncol, st=None, st_dt=F32, wb=None, cw=1024,
              alloc=None):
    if alloc is None:
        alloc = nc.alloc_sbuf_tensor
    if wb is None:
        wb = alloc(name + "_b", [128, nrows * ncol], BF16)
    CW = min(ncol, cw)
    if st is None:
        st = alloc(name + "_st", [128, 2 * CW], st_dt)
    k = 0
    for ci in range(nrows):
        for c0 in range(0, ncol, CW):
            p = k % 2
            k += 1
            a.dma("SP", st.ap()[:, p * CW:(p + 1) * CW],
                  W_ap[128 * ci:128 * (ci + 1), c0:c0 + CW])
            dst = wb.ap()[:, ci * ncol + c0: ci * ncol + c0 + CW]
            eng = "DVE" if p == 0 else "ACT"
            if eng == "DVE":
                a.op("DVE", lambda e, d=dst, p=p: e.tensor_copy(
                    d, st.ap()[:, p * CW:(p + 1) * CW]),
                    reads=[st.ap()[:, p * CW:(p + 1) * CW]], writes=[dst])
            else:
                a.op("ACT", lambda e, d=dst, p=p: e.copy(
                    d, st.ap()[:, p * CW:(p + 1) * CW]),
                    reads=[st.ap()[:, p * CW:(p + 1) * CW]], writes=[dst])
    return wb


def load_w1024_grid(a, nc, name, W_ap, ncol=1024, st=None, alloc=None):
    return load_grid(a, nc, name, W_ap, 8, ncol, st=st, alloc=alloc)


def lhsT_tile(wb, ci, co, ncol=1024):
    return wb.ap()[:, ci * ncol + co * 128: ci * ncol + co * 128 + 128]


def rhs_slab(wb, ci, lo, hi, ncol=1024):
    return wb.ap()[:, ci * ncol + lo: ci * ncol + hi]


def layernorm_tiles(a, nc, bufs, xin_tiles, out_tiles, g_bc=None, b_bc=None,
                    tag="ln", stat_eng="ACT", norm_eng="DVE", width=C):
    s1 = bufs[f"{tag}_s1"]
    s2 = bufs[f"{tag}_s2"]
    junk = bufs[f"{tag}_junk"]
    eps = bufs["eps_col"]
    for i, (xi, oi) in enumerate(zip(xin_tiles, out_tiles)):
        c0 = s1.ap()[:, i:i + 1]
        c1 = s2.ap()[:, i:i + 1]
        a.op(stat_eng, lambda e, xi=xi, c0=c0: e.activation(
            junk.ap()[:, :width], xi, AF.Copy, accum_out=c0),
            reads=[xi], writes=[junk.ap()[:, :width], c0])
        a.op(stat_eng, lambda e, xi=xi, c1=c1: e.activation(
            junk.ap()[:, :width], xi, AF.Square, accum_out=c1),
            reads=[xi], writes=[junk.ap()[:, :width], c1])
        m = bufs[f"{tag}_m"].ap()[:, i:i + 1]
        v = bufs[f"{tag}_v"].ap()[:, i:i + 1]
        rs = bufs[f"{tag}_rs"].ap()[:, i:i + 1]
        nb = bufs[f"{tag}_nb"].ap()[:, i:i + 1]
        a.op("DVE", lambda e, c0=c0, m=m: e.tensor_scalar(
            m, c0, 1.0 / width, None, ALU.mult), reads=[c0], writes=[m])
        a.op("DVE", lambda e, c1=c1, m=m, v=v: e.scalar_tensor_tensor(
            v, m, -1.0, m, ALU.mult, ALU.mult), reads=[m], writes=[v])
        a.op("DVE", lambda e, c1=c1, v=v: e.scalar_tensor_tensor(
            v, c1, 1.0 / width, v, ALU.mult, ALU.add), reads=[c1, v], writes=[v])
        a.op("ACT", lambda e, v=v, rs=rs: e.activation(
            rs, v, AF.Sqrt, bias=eps.ap()[:, 0:1]), reads=[v, eps.ap()[:, 0:1]], writes=[rs])
        a.op("DVE", lambda e, rs=rs: e.reciprocal(rs, rs), reads=[rs], writes=[rs])
        a.op(norm_eng, lambda e, xi=xi, oi=oi, rs=rs, m=m: e.tensor_scalar(
            oi, xi, m, rs, ALU.subtract, ALU.mult), reads=[xi, rs, m], writes=[oi])
        if g_bc is not None:
            a.op("DVE", lambda e, oi=oi: e.tensor_mul(oi, oi, g_bc[i] if isinstance(g_bc, list) else g_bc),
                 reads=[oi, g_bc[i] if isinstance(g_bc, list) else g_bc], writes=[oi])
        if b_bc is not None:
            a.op("POOL", lambda e, oi=oi: e.tensor_add(oi, oi, b_bc[i] if isinstance(b_bc, list) else b_bc),
                 reads=[oi, b_bc[i] if isinstance(b_bc, list) else b_bc], writes=[oi])


def alloc_ln_bufs(a, nc, bufs, tag, n, alloc=None):
    if alloc is None:
        alloc = nc.alloc_sbuf_tensor
    for nm, w in [(f"{tag}_s1", n), (f"{tag}_s2", n), (f"{tag}_m", n),
                  (f"{tag}_v", n), (f"{tag}_rs", n), (f"{tag}_nb", n)]:
        bufs[nm] = alloc(nm, [128, w], F32)
    if "junk_shared" not in bufs:
        bufs["junk_shared"] = alloc("junk_shared", [128, C], F32)
    bufs[f"{tag}_junk"] = bufs["junk_shared"]


def _own_ctx():
    nc = bass.Bass("TRN2", target_bir_lowering=False)
    a = ASB(nc)

    def mk_dram(name, shape, dt, kind):
        return nc.dram_tensor(name, shape, dt, kind=kind).ap()

    def ps_raw(name, shape, dt=F32):
        return nc.alloc_psum_tensor(name, shape, dt)

    return nc, a, mk_dram, nc.alloc_sbuf_tensor, ps_raw



def build_ph1(nchunks=NCH, dbg=False, ctx=None):
    nc, a, mk_dram, sb_raw, ps_raw = ctx if ctx is not None else _own_ctx()
    ntok = nchunks * CH
    xd = mk_dram("x_ext", [ntok, C], F32, "ExternalInput")
    Wk = mk_dram("Wk", [C, C], F32, "ExternalInput")
    Wv = mk_dram("Wv", [C, C], F32, "ExternalInput")
    Wr = mk_dram("Wr", [C, C], F32, "ExternalInput")
    Wo = mk_dram("Wo", [C, C], F32, "ExternalInput")
    v_ln0g = mk_dram("ln0g", [1, C], F32, "ExternalInput")
    v_ln0b = mk_dram("ln0b", [1, C], F32, "ExternalInput")
    v_tmk = mk_dram("tmk", [128, 8], F32, "ExternalInput")
    v_tmv = mk_dram("tmv", [128, 8], F32, "ExternalInput")
    v_tmr = mk_dram("tmr", [128, 8], F32, "ExternalInput")
    v_td = mk_dram("td", [128, 8], F32, "ExternalInput")
    v_tf = mk_dram("tf", [128, 8], F32, "ExternalInput")
    v_mask = mk_dram("mask", [128, 1], F32, "ExternalInput")
    xi_out = mk_dram("x_inner", [max(ntok - CH, CH), C], F32, "ExternalOutput")
    xn_out = mk_dram("xn_s", [max(ntok - 2 * CH, CH), C], F32, "ExternalOutput")

    bufs = {}

    def sb(name, shape, dt=F32):
        t = sb_raw(name, shape, dt)
        bufs[name] = t
        return t

    wst = sb("wst", [128, 2 * 1024])
    wkb = load_w1024_grid(a, nc, "wk", Wk, st=wst, alloc=sb_raw)
    wvb = load_w1024_grid(a, nc, "wv", Wv, st=wst, alloc=sb_raw)
    wrb = load_w1024_grid(a, nc, "wr", Wr, st=wst, alloc=sb_raw)
    wob = load_w1024_grid(a, nc, "wo", Wo, st=wst, alloc=sb_raw)

    eps_col = sb("eps_col", [128, 1])
    a.op("DVE", lambda e: e.memset(eps_col.ap()[:, :], 1e-5),
         writes=[eps_col.ap()[:, :]])
    tmk8 = sb("tmk8", [128, 8]); tmv8 = sb("tmv8", [128, 8]); tmr8 = sb("tmr8", [128, 8])
    td8 = sb("td8", [128, 8]); tf8 = sb("tf8", [128, 8])
    mask8 = sb("mask8", [128, 1])
    for t, src in [(tmk8, v_tmk), (tmv8, v_tmv), (tmr8, v_tmr),
                   (td8, v_td), (tf8, v_tf), (mask8, v_mask)]:
        a.dma("SP", t.ap()[:, :], src)
    g_bc = sb("g_bc", [128, C]); b_bc = sb("b_bc", [128, C])
    a.dma("SP", g_bc.ap()[:, :], v_ln0g.partition_broadcast(128))
    a.dma("SP", b_bc.ap()[:, :], v_ln0b.partition_broadcast(128))
    lam8 = sb("lam8", [128, 8]); eu8 = sb("eu8", [128, 8])
    a.op("ACT", lambda e: e.activation(lam8.ap()[:, :], td8.ap()[:, :], AF.Exp),
         reads=[td8.ap()[:, :]], writes=[lam8.ap()[:, :]])
    a.op("ACT", lambda e: e.activation(lam8.ap()[:, :], lam8.ap()[:, :], AF.Exp, scale=-1.0),
         reads=[lam8.ap()[:, :]], writes=[lam8.ap()[:, :]])
    a.op("ACT", lambda e: e.activation(eu8.ap()[:, :], tf8.ap()[:, :], AF.Exp),
         reads=[tf8.ap()[:, :]], writes=[eu8.ap()[:, :]])

    NT = CH // 128
    xt = [sb("xt0", [128, NT * C])] * 2
    xn_t = sb("xn_t", [128, NT * C])
    xa_t = sb("xa_t", [128, NT * C], BF16)
    xaT = [sb(f"xaT{p}", [128, 8 * CH], BF16) for p in range(2)]
    pcol = sb("pcol", [128, 8], BF16)
    t1 = sb("t1", [128, 8 * CH], BF16)
    mkT = sb("mkT", [128, 8 * CH], BF16)
    mvT = sb("mvT", [128, 8 * CH], BF16)
    mrT = t1
    ekP = [sb(f"ek{i}", [128, CH]) for i in range(2)]
    ekvP = [sb(f"ekv{i}", [128, CH]) for i in range(2)]
    AarP = [sb(f"Aar{i}", [128, CH]) for i in range(2)]
    BarP = [sb(f"Bar{i}", [128, CH]) for i in range(2)]
    numP = [sb(f"num{i}", [128, CH]) for i in range(2)]
    denP = [sb(f"den{i}", [128, CH]) for i in range(2)]
    sigrP = [sb(f"sigr{i}", [128, CH]) for i in range(2)]
    rwkvT = sb("rwkvT", [128, 8 * CH], BF16)
    Acol = sb("Acol", [128, 8]); Bcol = sb("Bcol", [128, 8])
    a.op("DVE", lambda e: e.memset(Acol.ap()[:, :], 0.0), writes=[Acol.ap()[:, :]])
    a.op("DVE", lambda e: e.memset(Bcol.ap()[:, :], 0.0), writes=[Bcol.ap()[:, :]])
    a.op("DVE", lambda e: e.memset(pcol.ap()[:, :], 0.0), writes=[pcol.ap()[:, :]])

    alloc_ln_bufs(a, nc, bufs, "lnA", NT, alloc=sb_raw)
    alloc_ln_bufs(a, nc, bufs, "lnB", NT, alloc=sb_raw)

    ps_pool = [ps_raw(f"ps_p{i}", [128, CH], F32) for i in range(4)]
    ps_o = [ps_raw(f"ps_o{i}", [128, C], F32) for i in range(2)]

    for ch in range(nchunks):
        par = ch % 2
        full = ch >= 1
        x_c = xt[par]
        xaT_c = xaT[par]; xaT_p = xaT[1 - par]
        for tt in range(NT):
            a.dma("SP", x_c.ap()[:, tt * C:(tt + 1) * C],
                  xd[ch * CH + tt * 128: ch * CH + (tt + 1) * 128, :])
        x_tiles = [x_c.ap()[:, tt * C:(tt + 1) * C] for tt in range(NT)]
        xn_tiles = [xn_t.ap()[:, tt * C:(tt + 1) * C] for tt in range(NT)]
        xa_tiles = [xa_t.ap()[:, tt * C:(tt + 1) * C] for tt in range(NT)]
        layernorm_tiles(a, nc, bufs, x_tiles, xn_tiles, tag="lnA")
        if ch >= 2 and nchunks == NCH:
            for tt in range(NT):
                a.dma("POOL", xn_out[(ch - 2) * CH + tt * 128:(ch - 2) * CH + (tt + 1) * 128, :],
                      xn_tiles[tt])
        layernorm_tiles(a, nc, bufs, xn_tiles, xa_tiles, g_bc=g_bc.ap()[:, :],
                        b_bc=b_bc.ap()[:, :], tag="lnB")
        for tt in range(NT):
            for ci in range(8):
                a.dma("SP",
                      xaT_c.ap()[:, ci * CH + tt * 128: ci * CH + (tt + 1) * 128],
                      xa_t.ap()[:, tt * C + ci * 128: tt * C + (ci + 1) * 128],
                      transpose=True)
        for ci in range(8):
            xa_a = xaT_c.ap()[:, ci * CH + 1: ci * CH + CH]
            xs_a = xaT_c.ap()[:, ci * CH: ci * CH + CH - 1]
            xa_b = xaT_c.ap()[:, ci * CH: ci * CH + 1]
            xs_b = pcol.ap()[:, ci:ci + 1]
            t1_a = t1.ap()[:, ci * CH + 1:(ci + 1) * CH]
            t1_b = t1.ap()[:, ci * CH: ci * CH + 1]
            a.op("DVE", lambda e, o=t1_a, i0=xa_a, i1=xs_a: e.tensor_sub(o, i0, i1),
                 reads=[xa_a, xs_a], writes=[t1_a])
            a.op("DVE", lambda e, o=t1_b, i0=xa_b, i1=xs_b: e.tensor_sub(o, i0, i1),
                 reads=[xa_b, xs_b], writes=[t1_b])
            for arr, vcol, eng in [(mkT, tmk8, "DVE"), (mvT, tmv8, "DVE")] + \
                                  ([(mrT, tmr8, "DVE")] if full else []):
                o_a = arr.ap()[:, ci * CH + 1:(ci + 1) * CH]
                o_b = arr.ap()[:, ci * CH: ci * CH + 1]
                a.op(eng, lambda e, o=o_a, t=t1_a, v=vcol, xs=xs_a, ci=ci: e.scalar_tensor_tensor(
                    o, t, v.ap()[:, ci:ci + 1], xs, ALU.mult, ALU.add),
                    reads=[t1_a, vcol.ap()[:, ci:ci + 1], xs_a], writes=[o_a])
                a.op(eng, lambda e, o=o_b, t=t1_b, v=vcol, xs=xs_b, ci=ci: e.scalar_tensor_tensor(
                    o, t, v.ap()[:, ci:ci + 1], xs, ALU.mult, ALU.add),
                    reads=[t1_b, vcol.ap()[:, ci:ci + 1], xs_b], writes=[o_b])
        for ci in range(8):
            dst = pcol.ap()[:, ci:ci + 1]
            srcc = xaT_c.ap()[:, ci * CH + CH - 1: ci * CH + CH]
            if ch == 2:
                a.op("DVE", lambda e, d=dst, s=srcc: e.tensor_scalar_mul(
                    d, s, mask8.ap()[:, 0:1]),
                    reads=[srcc, mask8.ap()[:, 0:1]], writes=[dst])
            else:
                a.op("DVE", lambda e, d=dst, s=srcc: e.tensor_copy(d, s),
                     reads=[srcc], writes=[dst])
        for co in range(8):
            ek = ekP[co % 2]; ekv = ekvP[co % 2]
            Aar = AarP[co % 2]; Bar = BarP[co % 2]
            num = numP[co % 2]; den = denP[co % 2]; sigr = sigrP[co % 2]
            ps_k = ps_pool[(3 * co) % 4]
            ps_v = ps_pool[(3 * co + 1) % 4]
            ps_r = ps_pool[(3 * co + 2) % 4]
            for ci in range(8):
                a.mm(ps_k.ap()[:, :], lhsT_tile(wkb, ci, co),
                     mkT.ap()[:, ci * CH:(ci + 1) * CH], ci == 0, ci == 7)
            a.op("ACT", lambda e, co=co: e.activation(ek.ap()[:, :], ps_k.ap()[:, :], AF.Exp),
                 reads=[ps_k.ap()[:, :]], writes=[ek.ap()[:, :]])
            for ci in range(8):
                a.mm(ps_v.ap()[:, :], lhsT_tile(wvb, ci, co),
                     mvT.ap()[:, ci * CH:(ci + 1) * CH], ci == 0, ci == 7)
            a.op("DVE", lambda e: e.tensor_mul(ekv.ap()[:, :], ek.ap()[:, :], ps_v.ap()[:, :]),
                 reads=[ek.ap()[:, :], ps_v.ap()[:, :]], writes=[ekv.ap()[:, :]])
            lam_s = lam8.ap()[:, co:co + 1].broadcast_to((128, CH))
            a.op("DVE", lambda e, co=co: e.tensor_tensor_scan(
                Aar.ap()[:, :], lam_s, ekv.ap()[:, :], Acol.ap()[:, co:co + 1],
                ALU.mult, ALU.add),
                reads=[lam8.ap()[:, co:co + 1], ekv.ap()[:, :], Acol.ap()[:, co:co + 1]],
                writes=[Aar.ap()[:, :]])
            a.op("DVE", lambda e, co=co: e.tensor_tensor_scan(
                Bar.ap()[:, :], lam_s, ek.ap()[:, :], Bcol.ap()[:, co:co + 1],
                ALU.mult, ALU.add),
                reads=[lam8.ap()[:, co:co + 1], ek.ap()[:, :], Bcol.ap()[:, co:co + 1]],
                writes=[Bar.ap()[:, :]])
            if full:
                for ci in range(8):
                    a.mm(ps_r.ap()[:, :], lhsT_tile(wrb, ci, co),
                         mrT.ap()[:, ci * CH:(ci + 1) * CH], ci == 0, ci == 7)
                a.op("ACT", lambda e: e.activation(sigr.ap()[:, :], ps_r.ap()[:, :], AF.Sigmoid),
                     reads=[ps_r.ap()[:, :]], writes=[sigr.ap()[:, :]])
                eu_c = eu8.ap()[:, co:co + 1]
                a.op("DVE", lambda e: e.scalar_tensor_tensor(
                    num.ap()[:, 1:CH], ekv.ap()[:, 1:CH], eu_c, Aar.ap()[:, 0:CH - 1],
                    ALU.mult, ALU.add),
                    reads=[ekv.ap()[:, 1:CH], eu_c, Aar.ap()[:, 0:CH - 1]],
                    writes=[num.ap()[:, 1:CH]])
                a.op("DVE", lambda e, co=co: e.scalar_tensor_tensor(
                    num.ap()[:, 0:1], ekv.ap()[:, 0:1], eu_c, Acol.ap()[:, co:co + 1],
                    ALU.mult, ALU.add),
                    reads=[ekv.ap()[:, 0:1], eu_c, Acol.ap()[:, co:co + 1]],
                    writes=[num.ap()[:, 0:1]])
                a.op("DVE", lambda e: e.scalar_tensor_tensor(
                    den.ap()[:, 1:CH], ek.ap()[:, 1:CH], eu_c, Bar.ap()[:, 0:CH - 1],
                    ALU.mult, ALU.add),
                    reads=[ek.ap()[:, 1:CH], eu_c, Bar.ap()[:, 0:CH - 1]],
                    writes=[den.ap()[:, 1:CH]])
                a.op("DVE", lambda e, co=co: e.scalar_tensor_tensor(
                    den.ap()[:, 0:1], ek.ap()[:, 0:1], eu_c, Bcol.ap()[:, co:co + 1],
                    ALU.mult, ALU.add),
                    reads=[ek.ap()[:, 0:1], eu_c, Bcol.ap()[:, co:co + 1]],
                    writes=[den.ap()[:, 0:1]])
                a.op("DVE", lambda e: e.reciprocal(den.ap()[:, :], den.ap()[:, :]),
                    reads=[den.ap()[:, :]], writes=[den.ap()[:, :]])
                a.op("DVE", lambda e: e.tensor_mul(
                    num.ap()[:, :], num.ap()[:, :], den.ap()[:, :]),
                    reads=[num.ap()[:, :], den.ap()[:, :]], writes=[num.ap()[:, :]])
                o = rwkvT.ap()[:, co * CH:(co + 1) * CH]
                a.op("DVE", lambda e, o=o: e.tensor_mul(o, num.ap()[:, :], sigr.ap()[:, :]),
                     reads=[num.ap()[:, :], sigr.ap()[:, :]], writes=[o])
            msk = (ch == 2)
            for colbuf, arr in [(Acol, Aar), (Bcol, Bar)]:
                dst = colbuf.ap()[:, co:co + 1]
                src = arr.ap()[:, CH - 1:CH]
                if msk:
                    a.op("DVE", lambda e, d=dst, s=src: e.tensor_scalar_mul(
                        d, s, mask8.ap()[:, 0:1]),
                        reads=[src, mask8.ap()[:, 0:1]], writes=[dst])
                else:
                    a.op("DVE", lambda e, d=dst, s=src: e.tensor_copy(d, s),
                         reads=[src], writes=[dst])
        if full:
            for tt in range(NT):
                pso = ps_o[tt % 2]
                for half in range(2):
                    for ci in range(8):
                        a.mm(pso.ap()[:, half * 512:(half + 1) * 512],
                             rwkvT.ap()[:, ci * CH + tt * 128: ci * CH + (tt + 1) * 128],
                             rhs_slab(wob, ci, half * 512, (half + 1) * 512),
                             ci == 0, ci == 7)
                o = xn_tiles[tt]
                a.op("DVE", lambda e, o=o, pso=pso: e.tensor_add(
                    o, o, pso.ap()[:, :]),
                    reads=[o, pso.ap()[:, :]], writes=[o])
                a.dma("POOL", xi_out[(ch - 1) * CH + tt * 128:(ch - 1) * CH + (tt + 1) * 128, :], o)

    if ctx is None:
        a.finish()
    return nc


def _coltiles(v):
    return np.ascontiguousarray(np.asarray(v, np.float32).reshape(8, 128).T)


def _shard_x(x):
    shards = []
    for core in range(N_CORES):
        b, q = core // 4, core % 4
        start = q * P_TOK
        lo = start - (HW + HG)
        if lo < 0:
            xe = np.concatenate([np.zeros((-lo, C), np.float32),
                                 x[b, 0:start + P_TOK]], axis=0)
        else:
            xe = np.ascontiguousarray(x[b, lo:start + P_TOK])
        shards.append(xe)
    return shards


def _run_ph1(inp, nchunks=NCH):
    x = np.asarray(inp["L0_attn_x"], np.float32)
    shards = _shard_x(x)
    nc = build_ph1(nchunks)
    base = {
        "Wk": np.asarray(inp["L0_attn_Wk"], np.float32),
        "Wv": np.asarray(inp["L0_attn_Wv"], np.float32),
        "Wr": np.asarray(inp["L0_attn_Wr"], np.float32),
        "Wo": np.asarray(inp["L0_attn_Wo"], np.float32),
        "ln0g": np.asarray(inp["L0_attn_ln0_g"], np.float32).reshape(1, C),
        "ln0b": np.asarray(inp["L0_attn_ln0_b"], np.float32).reshape(1, C),
        "tmk": _coltiles(inp["L0_attn_tm_k"]),
        "tmv": _coltiles(inp["L0_attn_tm_v"]),
        "tmr": _coltiles(inp["L0_attn_tm_r"]),
        "td": _coltiles(inp["L0_attn_time_decay"]),
        "tf": _coltiles(inp["L0_attn_time_first"]),
    }
    in_maps = []
    for core in range(N_CORES):
        m = dict(base)
        m["x_ext"] = shards[core][:nchunks * CH]
        m["mask"] = np.full((128, 1), 0.0 if core % 4 == 0 else 1.0, np.float32)
        in_maps.append(m)
    res = run_bass_kernel_spmd(nc, in_maps, core_ids=list(range(N_CORES)))
    return res.results




_SCRATCH = {"x_inner", "xn_s", "y2", "x1", "attn1"}


def build_fused():
    nc = bass.Bass("TRN2", target_bir_lowering=False)
    a = ASB(nc)
    drams = {}

    def mk_dram(name, shape, dt, kind):
        if name in drams:
            return drams[name]
        k = "Internal" if name in _SCRATCH else kind
        ap = nc.dram_tensor(name, shape, dt, kind=k).ap()
        drams[name] = ap
        return ap

    arena_bytes = (nc.sbuf_bytes_remaining - 512) // 64 * 64
    arena = nc.alloc_sbuf_tensor("arena", [128, arena_bytes // 4], F32)
    ps_arena = nc.alloc_psum_tensor("ps_arena", [128, 4096], F32)
    st = {"off": 0, "ps": 0}

    def sb_raw(name, shape, dt=F32):
        esz = _dtsize(dt)
        off = (st["off"] + 63) // 64 * 64
        st["off"] = off + shape[1] * esz
        assert st["off"] <= arena_bytes, (name, st["off"], arena_bytes)
        v = arena.ap().bitcast(dt)[0:shape[0], off // esz: off // esz + shape[1]]
        return _ApTensor(v)

    def ps_raw(name, shape, dt=F32):
        assert dt is F32
        off = (st["ps"] + 2047) // 2048 * 2048
        st["ps"] = off + shape[1] * 4
        assert st["ps"] <= 16384, (name, st["ps"])
        v = ps_arena.ap()[0:shape[0], off // 4: off // 4 + shape[1]]
        return _ApTensor(v)

    ctx = (nc, a, mk_dram, sb_raw, ps_raw)
    for build in (build_ph1, build_ph2, build_ph3, build_ph4, build_ph5):
        st["off"] = 0
        st["ps"] = 0
        build(ctx=ctx)
    a.finish()
    return nc


_FUSED_NC = None


def kernel(**inp):
    global _FUSED_NC
    if _FUSED_NC is None:
        _FUSED_NC = build_fused()
    nc = _FUSED_NC
    x = np.asarray(inp["L0_attn_x"], np.float32)
    shards = _shard_x(x)
    base = {
        "Wk": np.asarray(inp["L0_attn_Wk"], np.float32),
        "Wv": np.asarray(inp["L0_attn_Wv"], np.float32),
        "Wr": np.asarray(inp["L0_attn_Wr"], np.float32),
        "Wo": np.asarray(inp["L0_attn_Wo"], np.float32),
        "ln0g": np.asarray(inp["L0_attn_ln0_g"], np.float32).reshape(1, C),
        "ln0b": np.asarray(inp["L0_attn_ln0_b"], np.float32).reshape(1, C),
        "tmk": _coltiles(inp["L0_attn_tm_k"]),
        "tmv": _coltiles(inp["L0_attn_tm_v"]),
        "tmr": _coltiles(inp["L0_attn_tm_r"]),
        "td": _coltiles(inp["L0_attn_time_decay"]),
        "tf": _coltiles(inp["L0_attn_time_first"]),
        "Wck": np.asarray(inp["L0_attn_Wck"], np.float32),
        "Wcv": np.asarray(inp["L0_attn_Wcv"], np.float32),
        "Wcr": np.asarray(inp["L0_attn_Wcr"], np.float32),
        "ln1g": np.asarray(inp["L0_attn_ln1_g"], np.float32).reshape(1, C),
        "ln1b": np.asarray(inp["L0_attn_ln1_b"], np.float32).reshape(1, C),
        "cmk": _coltiles(inp["L0_attn_cm_k"]),
        "cmr": _coltiles(inp["L0_attn_cm_r"]),
        "w_gate": np.asarray(inp["L0_ffn_w_gate"], np.float32),
        "w_up": np.asarray(inp["L0_ffn_w_up"], np.float32),
        "w_down": np.asarray(inp["L0_ffn_w_down"], np.float32),
        "Wq": np.asarray(inp["L1_attn_Wq"], np.float32),
        "Wk1": np.asarray(inp["L1_attn_Wk"], np.float32),
        "Wv1": np.asarray(inp["L1_attn_Wv"], np.float32),
        "Wo1": np.asarray(inp["L1_attn_Wo"], np.float32),
        "Wf1": np.asarray(inp["L1_attn_Wf1"], np.float32),
        "Wf2": np.asarray(inp["L1_attn_Wf2"], np.float32),
        "Wg1": np.asarray(inp["L1_attn_Wg1"], np.float32),
        "Wg2": np.asarray(inp["L1_attn_Wg2"], np.float32),
        "nw": _coltiles(inp["L1_attn_norm_w"]),
        "wgT": np.ascontiguousarray(np.asarray(inp["L1_ffn_w_gate"], np.int32).T),
        "wuT": np.ascontiguousarray(np.asarray(inp["L1_ffn_w_up"], np.int32).T),
        "wdT": np.ascontiguousarray(np.asarray(inp["L1_ffn_w_down"], np.int32).T),
    }
    in_maps = []
    for core in range(N_CORES):
        m = dict(base)
        m["x_ext"] = shards[core]
        m["mask"] = np.full((128, 1), 0.0 if core % 4 == 0 else 1.0, np.float32)
        in_maps.append(m)
    res = run_bass_kernel_spmd(nc, in_maps, core_ids=list(range(N_CORES)))
    out = np.empty((B, T, C), np.float32)
    for core in range(N_CORES):
        b, q = core // 4, core % 4
        out[b, q * P_TOK:(q + 1) * P_TOK, :] = res.results[core]["x2"]
    return out


def kernel_unfused(**inp):
    res1 = _run_ph1(inp)
    xi = [res1[c]["x_inner"] for c in range(N_CORES)]
    xn = [res1[c]["xn_s"] for c in range(N_CORES)]
    res2 = _run_ph2(inp, xi)
    y2 = [res2[c]["y2"] for c in range(N_CORES)]
    res3 = _run_ph3(inp, xn, y2)
    x1 = [res3[c]["x1"] for c in range(N_CORES)]
    res4 = _run_ph4(inp, x1)
    at = [res4[c]["attn1"] for c in range(N_CORES)]
    res5 = _run_ph5(inp, x1, at)
    out = np.empty((B, T, C), np.float32)
    for core in range(N_CORES):
        b, q = core // 4, core % 4
        out[b, q * P_TOK:(q + 1) * P_TOK, :] = res5[core]["x2"]
    return out


if __name__ == "__main__":
    print("kernel module (dev)")


def build_ph2(nchunks=NCH, ctx=None):
    nc, a, mk_dram, sb_raw, ps_raw = ctx if ctx is not None else _own_ctx()
    n_in = (nchunks - 1) * CH
    n_out = (nchunks - 2) * CH
    xi_d = mk_dram("x_inner", [n_in, C], F32, "ExternalInput")
    Wck = mk_dram("Wck", [C, 4 * C], F32, "ExternalInput")
    Wcv = mk_dram("Wcv", [4 * C, C], F32, "ExternalInput")
    Wcr = mk_dram("Wcr", [C, C], F32, "ExternalInput")
    v_ln1g = mk_dram("ln1g", [1, C], F32, "ExternalInput")
    v_ln1b = mk_dram("ln1b", [1, C], F32, "ExternalInput")
    v_cmk = mk_dram("cmk", [128, 8], F32, "ExternalInput")
    v_cmr = mk_dram("cmr", [128, 8], F32, "ExternalInput")
    v_mask = mk_dram("mask", [128, 1], F32, "ExternalInput")
    y2_d = mk_dram("y2", [n_out, C], F32, "ExternalOutput")
    wcv_sp = mk_dram("wcv_sp", [32 * 128, C], BF16, "Internal")

    bufs = {}

    def sb(name, shape, dt=F32):
        t = sb_raw(name, shape, dt)
        bufs[name] = t
        return t

    wst = sb("wst", [128, 2 * 1024])
    wckb = load_grid(a, nc, "wck", Wck, 8, 4096, st=wst, alloc=sb_raw)
    wcrb = load_grid(a, nc, "wcr", Wcr, 8, 1024, st=wst, alloc=sb_raw)
    wcv_stage = sb("wcv_stage", [128, 2 * 1024], BF16)
    for fo in range(32):
        p = fo % 2
        a.dma("SP", wst.ap()[:, p * 1024:(p + 1) * 1024],
              Wcv[128 * fo:128 * (fo + 1), :])
        dst = wcv_stage.ap()[:, p * 1024:(p + 1) * 1024]
        a.op("DVE", lambda e, d=dst, p=p: e.tensor_copy(
            d, wst.ap()[:, p * 1024:(p + 1) * 1024]),
            reads=[wst.ap()[:, p * 1024:(p + 1) * 1024]], writes=[dst])
        a.dma("POOL", wcv_sp[128 * fo:128 * (fo + 1), :], dst)

    eps_col = sb("eps_col", [128, 1])
    a.op("DVE", lambda e: e.memset(eps_col.ap()[:, :], 1e-5), writes=[eps_col.ap()[:, :]])
    cmk8 = sb("cmk8", [128, 8]); cmr8 = sb("cmr8", [128, 8]); mask8 = sb("mask8", [128, 1])
    for t, srcd in [(cmk8, v_cmk), (cmr8, v_cmr), (mask8, v_mask)]:
        a.dma("SP", t.ap()[:, :], srcd)
    g_bc = sb("g_bc", [128, C]); b_bc = sb("b_bc", [128, C])
    a.dma("SP", g_bc.ap()[:, :], v_ln1g.partition_broadcast(128))
    a.dma("SP", b_bc.ap()[:, :], v_ln1b.partition_broadcast(128))

    NT = CH // 128
    xi_c = sb("xi_c", [128, NT * C])
    xc_t = sb("xc_t", [128, NT * C], BF16)
    xcT_c = sb("xcT_c", [128, 8 * CH], BF16)
    pcol = sb("pcol", [128, 8], BF16)
    a.op("DVE", lambda e: e.memset(pcol.ap()[:, :], 0.0), writes=[pcol.ap()[:, :]])
    t1 = sb("t1", [128, 8 * CH], BF16)
    mkT = sb("mkT", [128, 8 * CH], BF16)
    mrT = t1
    kk2 = sb("kk2", [128, 32 * CH], BF16)
    rkP = [sb(f"rk{i}", [128, CH], BF16) for i in range(2)]
    rk = rkP[0]
    cmr_s = sb("cmr_s", [128, 8 * CH])
    wcv_sl = [sb(f"wcv_sl{p}", [128, C], BF16) for p in range(2)]
    ident = sb("ident", [128, 128])
    a.op("DVE", lambda e: e.memset(ident.ap()[:, :], 1.0), writes=[ident.ap()[:, :]])
    a.op("POOL", lambda e: e.affine_select(
        ident.ap()[:, :], ident.ap()[:, :], [[1, 128]], ALU.is_equal, 0.0,
        base=0, channel_multiplier=-1),
        reads=[ident.ap()[:, :]], writes=[ident.ap()[:, :]])
    alloc_ln_bufs(a, nc, bufs, "lnC", NT, alloc=sb_raw)

    ps_big = ps_raw("ps_big", [128, C], F32)
    ps_kkP = [_ApTensor(ps_big.ap()[:, 0:CH]), _ApTensor(ps_big.ap()[:, CH:2 * CH])]
    ps_cm = ps_raw("ps_cm", [128, 4 * CH], F32)
    ps_rr = ps_raw("ps_rr", [128, CH], F32)
    ps_ct = ps_big

    for ch in range(1, nchunks):
        full = ch >= 2
        for tt in range(NT):
            a.dma("SP", xi_c.ap()[:, tt * C:(tt + 1) * C],
                  xi_d[(ch - 1) * CH + tt * 128:(ch - 1) * CH + (tt + 1) * 128, :])
        xi_tiles = [xi_c.ap()[:, tt * C:(tt + 1) * C] for tt in range(NT)]
        xc_tiles = [xc_t.ap()[:, tt * C:(tt + 1) * C] for tt in range(NT)]
        layernorm_tiles(a, nc, bufs, xi_tiles, xc_tiles, g_bc=g_bc.ap()[:, :],
                        b_bc=b_bc.ap()[:, :], tag="lnC")
        for tt in range(NT):
            for ci in range(8):
                a.dma("SP",
                      xcT_c.ap()[:, ci * CH + tt * 128: ci * CH + (tt + 1) * 128],
                      xc_t.ap()[:, tt * C + ci * 128: tt * C + (ci + 1) * 128],
                      transpose=True)
        if full:
            for ci in range(8):
                xa_a = xcT_c.ap()[:, ci * CH + 1: ci * CH + CH]
                xs_a = xcT_c.ap()[:, ci * CH: ci * CH + CH - 1]
                xa_b = xcT_c.ap()[:, ci * CH: ci * CH + 1]
                xs_b = pcol.ap()[:, ci:ci + 1]
                t1_a = t1.ap()[:, ci * CH + 1:(ci + 1) * CH]
                t1_b = t1.ap()[:, ci * CH: ci * CH + 1]
                a.op("DVE", lambda e, o=t1_a, i0=xa_a, i1=xs_a: e.tensor_sub(o, i0, i1),
                     reads=[xa_a, xs_a], writes=[t1_a])
                a.op("DVE", lambda e, o=t1_b, i0=xa_b, i1=xs_b: e.tensor_sub(o, i0, i1),
                     reads=[xa_b, xs_b], writes=[t1_b])
                for arr, vcol in [(mkT, cmk8), (mrT, cmr8)]:
                    o_a = arr.ap()[:, ci * CH + 1:(ci + 1) * CH]
                    o_b = arr.ap()[:, ci * CH: ci * CH + 1]
                    a.op("DVE", lambda e, o=o_a, t=t1_a, v=vcol, xs=xs_a, ci=ci: e.scalar_tensor_tensor(
                        o, t, v.ap()[:, ci:ci + 1], xs, ALU.mult, ALU.add),
                        reads=[t1_a, vcol.ap()[:, ci:ci + 1], xs_a], writes=[o_a])
                    a.op("DVE", lambda e, o=o_b, t=t1_b, v=vcol, xs=xs_b, ci=ci: e.scalar_tensor_tensor(
                        o, t, v.ap()[:, ci:ci + 1], xs, ALU.mult, ALU.add),
                        reads=[t1_b, vcol.ap()[:, ci:ci + 1], xs_b], writes=[o_b])
        for ci in range(8):
            dst = pcol.ap()[:, ci:ci + 1]
            srcc = xcT_c.ap()[:, ci * CH + CH - 1: ci * CH + CH]
            if ch == 2:
                a.op("DVE", lambda e, d=dst, s=srcc: e.tensor_scalar_mul(
                    d, s, mask8.ap()[:, 0:1]),
                    reads=[srcc, mask8.ap()[:, 0:1]], writes=[dst])
            else:
                a.op("DVE", lambda e, d=dst, s=srcc: e.tensor_copy(d, s),
                     reads=[srcc], writes=[dst])
        if not full:
            continue
        for fo in range(32):
            ps_kk = ps_kkP[fo % 2]
            rk = rkP[fo % 2]
            for ci in range(8):
                a.mm(ps_kk.ap()[:, :], lhsT_tile(wckb, ci, fo, 4096),
                     mkT.ap()[:, ci * CH:(ci + 1) * CH], ci == 0, ci == 7)
            a.op("ACT", lambda e, ps_kk=ps_kk, rk=rk: e.activation(
                rk.ap()[:, :], ps_kk.ap()[:, :], AF.Relu),
                reads=[ps_kk.ap()[:, :]], writes=[rk.ap()[:, :]])
            o = kk2.ap()[:, fo * CH:(fo + 1) * CH]
            a.op("DVE", lambda e, o=o, rk=rk: e.tensor_mul(o, rk.ap()[:, :], rk.ap()[:, :]),
                 reads=[rk.ap()[:, :]], writes=[o])
        for cohalf in range(2):
            for fo in range(32):
                slp = wcv_sl[fo % 2]
                a.dma("SP", slp.ap()[:, :], wcv_sp[128 * fo:128 * (fo + 1), :])
                for cq in range(4):
                    co = cohalf * 4 + cq
                    a.mm(ps_cm.ap()[:, cq * CH:(cq + 1) * CH],
                         slp.ap()[:, co * 128:(co + 1) * 128],
                         kk2.ap()[:, fo * CH:(fo + 1) * CH],
                         fo == 0, fo == 31)
            for cq in range(4):
                co = cohalf * 4 + cq
                rk = rkP[cq % 2]
                for ci in range(8):
                    a.mm(ps_rr.ap()[:, :], lhsT_tile(wcrb, ci, co),
                         mrT.ap()[:, ci * CH:(ci + 1) * CH], ci == 0, ci == 7)
                a.op("ACT", lambda e, rk=rk: e.activation(
                    rk.ap()[:, :], ps_rr.ap()[:, :], AF.Sigmoid),
                     reads=[ps_rr.ap()[:, :]], writes=[rk.ap()[:, :]])
                dstc = cmr_s.ap()[:, co * CH:(co + 1) * CH]
                a.op("DVE", lambda e, cq=cq, dstc=dstc, rk=rk: e.tensor_mul(
                    dstc, rk.ap()[:, :], ps_cm.ap()[:, cq * CH:(cq + 1) * CH]),
                    reads=[rk.ap()[:, :], ps_cm.ap()[:, cq * CH:(cq + 1) * CH]],
                    writes=[dstc])
        for tt in range(NT):
            for co in range(8):
                srcp = cmr_s.ap()[:, co * CH + tt * 128: co * CH + (tt + 1) * 128]
                a.op("PE", lambda e, co=co, srcp=srcp: e.transpose(
                    ps_ct.ap()[:, co * 128:(co + 1) * 128], srcp, ident.ap()[:, :]),
                    reads=[srcp, ident.ap()[:, :]],
                    writes=[ps_ct.ap()[:, co * 128:(co + 1) * 128]])
            o = xi_tiles[tt]
            a.op("DVE", lambda e, o=o: e.tensor_add(o, o, ps_ct.ap()[:, :]),
                 reads=[o, ps_ct.ap()[:, :]], writes=[o])
            a.dma("POOL", y2_d[(ch - 2) * CH + tt * 128:(ch - 2) * CH + (tt + 1) * 128, :], o)

    if ctx is None:
        a.finish()
    return nc


def _run_ph2(inp, x_inner, nchunks=NCH):
    nc = build_ph2(nchunks)
    base = {
        "Wck": np.asarray(inp["L0_attn_Wck"], np.float32),
        "Wcv": np.asarray(inp["L0_attn_Wcv"], np.float32),
        "Wcr": np.asarray(inp["L0_attn_Wcr"], np.float32),
        "ln1g": np.asarray(inp["L0_attn_ln1_g"], np.float32).reshape(1, C),
        "ln1b": np.asarray(inp["L0_attn_ln1_b"], np.float32).reshape(1, C),
        "cmk": _coltiles(inp["L0_attn_cm_k"]),
        "cmr": _coltiles(inp["L0_attn_cm_r"]),
    }
    in_maps = []
    for core in range(N_CORES):
        m = dict(base)
        m["x_inner"] = x_inner[core]
        m["mask"] = np.full((128, 1), 0.0 if core % 4 == 0 else 1.0, np.float32)
        in_maps.append(m)
    res = run_bass_kernel_spmd(nc, in_maps, core_ids=list(range(N_CORES)))
    return res.results


CBF = 192.0


def emit_fp4_quant(a, nc, bufs, src_ap, dst_ap, W, ws=""):
    G = W // 16
    w0 = bufs["qw0" + ws].ap()[:, :W]
    w3 = bufs["qw3" + ws].ap()[:, :W]
    w1 = bufs["qw1" + ws].ap()[:, :W]
    w2 = bufs["qw2" + ws].ap()[:, :W]
    sc = bufs["qsc" + ws].ap()[:, :G]; rsc = bufs["qrsc" + ws].ap()[:, :G]
    src3 = src_ap.rearrange("p (g b) -> p g b", b=16)
    a.op("DVE", lambda e: e.tensor_reduce(sc.unsqueeze(-1), src3, AX.X, ALU.max,
                                          apply_absolute_value=True),
         reads=[src_ap], writes=[sc])
    a.op("DVE", lambda e: e.tensor_scalar(sc, sc, 1.0 / 6.0, 1e-12, ALU.mult, ALU.max),
         reads=[sc], writes=[sc])
    a.op("DVE", lambda e: e.reciprocal(rsc, sc), reads=[sc], writes=[rsc])
    a.op("DVE", lambda e: e.tensor_tensor(
        w0.rearrange("p (g b) -> p g b", b=16), src3,
        rsc.unsqueeze(-1).broadcast_to((128, G, 16)), ALU.mult),
        reads=[src_ap, rsc], writes=[w0])
    a.op("ACT", lambda e: e.activation(w1, w0, AF.Abs), reads=[w0], writes=[w1])
    a.op("DVE", lambda e: e.tensor_single_scalar(w2, w1, 2.0, ALU.is_ge),
         reads=[w1], writes=[w2])
    a.op("DVE", lambda e: e.tensor_single_scalar(w1, w1, 4.0, ALU.is_ge),
         reads=[w1], writes=[w1])
    a.op("DVE", lambda e: e.tensor_scalar(w3, w1, -0.5, 2.0, ALU.mult, ALU.add),
         reads=[w1], writes=[w3])
    a.op("DVE", lambda e: e.scalar_tensor_tensor(w3, w2, -1.0, w3, ALU.mult, ALU.add),
         reads=[w2, w3], writes=[w3])
    a.op("DVE", lambda e: e.tensor_mul(w0, w0, w3), reads=[w0, w3], writes=[w0])
    a.op("DVE", lambda e: e.scalar_tensor_tensor(w2, w2, 0.5, w1, ALU.mult, ALU.add),
         reads=[w2, w1], writes=[w2])
    a.op("ACT", lambda e: e.activation(w3, w0, AF.Copy, bias=RND_C),
         reads=[w0], writes=[w3])
    a.op("ACT", lambda e: e.activation(w3, w3, AF.Copy, bias=-RND_C),
         reads=[w3], writes=[w3])
    a.op("DVE", lambda e: e.tensor_mul(w2, w3, w2), reads=[w3, w2], writes=[w2])
    a.op("DVE", lambda e: e.scalar_tensor_tensor(w2, w3, 0.5, w2, ALU.mult, ALU.add),
         reads=[w3, w2], writes=[w2])
    a.op("DVE", lambda e: e.tensor_tensor(
        dst_ap.rearrange("p (g b) -> p g b", b=16),
        w2.rearrange("p (g b) -> p g b", b=16),
        sc.unsqueeze(-1).broadcast_to((128, G, 16)), ALU.mult),
        reads=[w2, sc], writes=[dst_ap])


def build_ph3(nchunks=NCH, ctx=None):
    nc, a, mk_dram, sb_raw, ps_raw = ctx if ctx is not None else _own_ctx()
    n_io = (nchunks - 2) * CH
    ntok = nchunks * CH
    xn_d = mk_dram("xn_s", [n_io, C], F32, "ExternalInput")
    x_d = mk_dram("x_ext", [ntok, C], F32, "ExternalInput")
    y2_d = mk_dram("y2", [n_io, C], F32, "ExternalInput")
    wg_d = mk_dram("w_gate", [2816, C], F32, "ExternalInput")
    wu_d = mk_dram("w_up", [2816, C], F32, "ExternalInput")
    wd_d = mk_dram("w_down", [C, 2816], F32, "ExternalInput")
    x1_d = mk_dram("x1", [n_io, C], F32, "ExternalOutput")
    wgT_sp = mk_dram("wgT_sp", [8 * 128, 2816], BF16, "Internal")
    wuT_sp = mk_dram("wuT_sp", [8 * 128, 2816], BF16, "Internal")

    bufs = {}

    def sb(name, shape, dt=F32):
        t = sb_raw(name, shape, dt)
        bufs[name] = t
        return t

    H3 = 2816
    WSETS = ("", "B", "C")
    for ws in WSETS:
        wdth = H3 if ws in ("", "B") else 1024
        for nm, wdt in [("qw0", F32), ("qw1", BF16), ("qw2", BF16), ("qw3", F32)]:
            bufs[nm + ws] = sb_raw(nm + ws, [128, wdth], wdt)
        bufs["qsc" + ws] = sb_raw("qsc" + ws, [128, wdth // 16], F32)
        bufs["qrsc" + ws] = sb_raw("qrsc" + ws, [128, wdth // 16], F32)
    wstP = ([sb(f"wst{i}", [128, 2816]) for i in range(2)] +
            [sb(f"wstn{i}", [128, 1024]) for i in range(2)])
    wqsP = [sb(f"wq_s{i}", [128, 2816], BF16) for i in range(4)]

    for wi, (wd_ap, sp) in enumerate([(wg_d, wgT_sp), (wu_d, wuT_sp)]):
        for hi in range(22):
            par = (wi * 22 + hi) % 4
            ws = WSETS[par % 3]
            wst = wstP[par]; wq_s = wqsP[par]
            a.dma("SP", wst.ap()[:, 0:1024], wd_ap[128 * hi:128 * (hi + 1), :])
            emit_fp4_quant(a, nc, bufs, wst.ap()[:, 0:1024], wq_s.ap()[:, 0:1024], 1024, ws=ws)
            for ci in range(8):
                a.dma("SP", bufs["qw1" + ws].ap()[:, ci * 128:(ci + 1) * 128],
                      wq_s.ap()[:, ci * 128:(ci + 1) * 128], transpose=True)
            for ci in range(8):
                a.dma("POOL", sp[ci * 128:(ci + 1) * 128, hi * 128:(hi + 1) * 128],
                      bufs["qw1" + ws].ap()[:, ci * 128:(ci + 1) * 128])
    wdT_sp = nc.dram_tensor("wdT_sp", [22 * 128, C], BF16, kind="Internal").ap()
    for ci in range(8):
        ws = WSETS[ci % 2]
        wst = wstP[ci % 2]; wq_s = wqsP[ci % 4]
        a.dma("SP", wst.ap()[:, :], wd_d[128 * ci:128 * (ci + 1), :])
        emit_fp4_quant(a, nc, bufs, wst.ap()[:, :], wq_s.ap()[:, :], 2816, ws=ws)
        for hi in range(22):
            stg = bufs["qw1" + ws].ap()[:, hi * 128:(hi + 1) * 128]
            a.dma("SP", stg, wq_s.ap()[:, hi * 128:(hi + 1) * 128], transpose=True)
            a.dma("POOL", wdT_sp[hi * 128:(hi + 1) * 128, ci * 128:(ci + 1) * 128], stg)
    wdslP = None

    NT = CH // 128
    xn_c = sb("xn_c", [128, NT * C])
    xq = sb("xq", [128, NT * C], BF16)
    xqT = sb("xqT", [128, 8 * CH], BF16)
    gsP = [sb("gs0", [128, H3], BF16)] * 2
    hhP = [sb("hh0", [128, H3], BF16)] * 2
    hqP = [sb("hq0", [128, H3], BF16)] * 2
    hqTP = [sb("hqT0", [128, 22 * 128], BF16)] * 2
    resP = [sb("res0", [128, C])] * 2
    wsl = [sb(f"wsl{p}", [128, H3], BF16) for p in range(2)]
    wdslP = [_ApTensor(wsl[0].ap()[:, 1024:2048]), _ApTensor(wsl[1].ap()[:, 1024:2048])]

    ps_g = ps_raw("ps_g", [128, H3], F32)
    ps_d = ps_raw("ps_d", [128, C], F32)

    HGRP = [(0, 512), (512, 1024), (1024, 1536), (1536, 2048), (2048, 2560), (2560, 2816)]

    for ch in range(2, nchunks):
        for tt in range(NT):
            a.dma("SP", xn_c.ap()[:, tt * C:(tt + 1) * C],
                  xn_d[(ch - 2) * CH + tt * 128:(ch - 2) * CH + (tt + 1) * 128, :])
        for tt in range(NT):
            emit_fp4_quant(a, nc, bufs, xn_c.ap()[:, tt * C:(tt + 1) * C],
                           xq.ap()[:, tt * C:(tt + 1) * C], C, ws="C")
        for tt in range(NT):
            for ci in range(8):
                a.dma("SP", xqT.ap()[:, ci * CH + tt * 128: ci * CH + (tt + 1) * 128],
                      xq.ap()[:, tt * C + ci * 128: tt * C + (ci + 1) * 128],
                      transpose=True)
        for tt in range(NT):
            gs = gsP[tt % 2]; hh = hhP[tt % 2]; hq = hqP[tt % 2]; hqT = hqTP[tt % 2]
            res_t = resP[tt % 2]
            for ci in range(8):
                slp = wsl[ci % 2]
                a.dma("SP", slp.ap()[:, :], wgT_sp[ci * 128:(ci + 1) * 128, :])
                lhs = xqT.ap()[:, ci * CH + tt * 128: ci * CH + (tt + 1) * 128]
                for (h0, h1) in HGRP:
                    a.mm(ps_g.ap()[:, h0:h1], lhs, slp.ap()[:, h0:h1], ci == 0, ci == 7)
            for (h0, h1) in HGRP:
                a.op("ACT", lambda e, h0=h0, h1=h1: e.activation(
                    gs.ap()[:, h0:h1], ps_g.ap()[:, h0:h1], AF.Silu),
                    reads=[ps_g.ap()[:, h0:h1]], writes=[gs.ap()[:, h0:h1]])
            for ci in range(8):
                slp = wsl[ci % 2]
                a.dma("SP", slp.ap()[:, :], wuT_sp[ci * 128:(ci + 1) * 128, :])
                lhs = xqT.ap()[:, ci * CH + tt * 128: ci * CH + (tt + 1) * 128]
                for (h0, h1) in HGRP:
                    a.mm(ps_g.ap()[:, h0:h1], lhs, slp.ap()[:, h0:h1], ci == 0, ci == 7)
            for (h0, h1) in HGRP:
                a.op("DVE", lambda e, h0=h0, h1=h1: e.tensor_tensor(
                    hh.ap()[:, h0:h1], gs.ap()[:, h0:h1], ps_g.ap()[:, h0:h1], ALU.mult),
                    reads=[gs.ap()[:, h0:h1], ps_g.ap()[:, h0:h1]], writes=[hh.ap()[:, h0:h1]])
            emit_fp4_quant(a, nc, bufs, hh.ap()[:, :], hq.ap()[:, :], H3,
                           ws=WSETS[tt % 2])
            for hi in range(22):
                a.dma("SP", hqT.ap()[:, hi * 128:(hi + 1) * 128],
                      hq.ap()[:, hi * 128:(hi + 1) * 128], transpose=True)
            for hi in range(22):
                slp = wdslP[hi % 2]
                a.dma("SP", slp.ap()[:, :], wdT_sp[hi * 128:(hi + 1) * 128, :])
                lhs = hqT.ap()[:, hi * 128:(hi + 1) * 128]
                a.mm(ps_d.ap()[:, 0:512], lhs, slp.ap()[:, 0:512], hi == 0, hi == 21)
                a.mm(ps_d.ap()[:, 512:1024], lhs, slp.ap()[:, 512:1024], hi == 0, hi == 21)
            row0 = (ch - 2) * CH + tt * 128
            a.dma("SP", res_t.ap()[:, :], y2_d[row0:row0 + 128, :])
            a.op("DVE", lambda e: e.tensor_add(res_t.ap()[:, :], res_t.ap()[:, :], ps_d.ap()[:, :]),
                 reads=[res_t.ap()[:, :], ps_d.ap()[:, :]], writes=[res_t.ap()[:, :]])
            a.dma("SP", xn_c.ap()[:, tt * C:(tt + 1) * C],
                  x_d[ch * CH + tt * 128: ch * CH + (tt + 1) * 128, :])
            a.op("DVE", lambda e, tt=tt: e.tensor_add(
                res_t.ap()[:, :], res_t.ap()[:, :], xn_c.ap()[:, tt * C:(tt + 1) * C]),
                reads=[res_t.ap()[:, :], xn_c.ap()[:, tt * C:(tt + 1) * C]],
                writes=[res_t.ap()[:, :]])
            a.dma("POOL", x1_d[row0:row0 + 128, :], res_t.ap()[:, :])

    if ctx is None:
        a.finish()
    return nc


def _run_ph3(inp, xn_s, y2, nchunks=NCH):
    x = np.asarray(inp["L0_attn_x"], np.float32)
    shards = _shard_x(x)
    nc = build_ph3(nchunks)
    base = {
        "w_gate": np.asarray(inp["L0_ffn_w_gate"], np.float32),
        "w_up": np.asarray(inp["L0_ffn_w_up"], np.float32),
        "w_down": np.asarray(inp["L0_ffn_w_down"], np.float32),
    }
    in_maps = []
    for core in range(N_CORES):
        m = dict(base)
        m["x_ext"] = shards[core][:nchunks * CH]
        m["xn_s"] = xn_s[core]
        m["y2"] = y2[core]
        in_maps.append(m)
    res = run_bass_kernel_spmd(nc, in_maps, core_ids=list(range(N_CORES)))
    return res.results


def build_ph4(nsc=9, ctx=None):
    nc, a, mk_dram, sb_raw, ps_raw = ctx if ctx is not None else _own_ctx()
    n_in = nsc * CH
    n_out = (nsc - 1) * CH
    H, M, DK = 8, 64, 128
    x1_d = mk_dram("x1", [n_in, C], F32, "ExternalInput")
    Wq_d = mk_dram("Wq", [C, C], F32, "ExternalInput")
    Wk_d = mk_dram("Wk1", [C, C], F32, "ExternalInput")
    Wv_d = mk_dram("Wv1", [C, C], F32, "ExternalInput")
    Wo_d = mk_dram("Wo1", [C, C], F32, "ExternalInput")
    Wf1_d = mk_dram("Wf1", [C, 128], F32, "ExternalInput")
    Wf2_d = mk_dram("Wf2", [128, 512], F32, "ExternalInput")
    Wg1_d = mk_dram("Wg1", [C, 128], F32, "ExternalInput")
    Wg2_d = mk_dram("Wg2", [128, C], F32, "ExternalInput")
    v_nw = mk_dram("nw", [128, 8], F32, "ExternalInput")
    v_mask = mk_dram("mask", [128, 1], F32, "ExternalInput")
    at_d = mk_dram("attn1", [n_out, C], F32, "ExternalOutput")

    bufs = {}

    def sb(name, shape, dt=F32):
        t = sb_raw(name, shape, dt)
        bufs[name] = t
        return t

    wst = sb("wst", [128, 2 * 1024])
    wqb = load_grid(a, nc, "wq", Wq_d, 8, 1024, st=wst, alloc=sb_raw)
    wkb = load_grid(a, nc, "wk1", Wk_d, 8, 1024, st=wst, alloc=sb_raw)
    wvb = load_grid(a, nc, "wv1", Wv_d, 8, 1024, st=wst, alloc=sb_raw)
    wob = load_grid(a, nc, "wo1", Wo_d, 8, 1024, st=wst, alloc=sb_raw)
    wf1b = load_grid(a, nc, "wf1", Wf1_d, 8, 128, st=None, alloc=sb_raw)
    wg1b = load_grid(a, nc, "wg1", Wg1_d, 8, 128, st=None, alloc=sb_raw)
    wf2b = sb("wf2b", [128, 512], BF16)
    wg2b = sb("wg2b", [128, C], BF16)
    a.dma("SP", wst.ap()[:, 0:512], Wf2_d)
    a.op("DVE", lambda e: e.tensor_copy(wf2b.ap()[:, :], wst.ap()[:, 0:512]),
         reads=[wst.ap()[:, 0:512]], writes=[wf2b.ap()[:, :]])
    a.dma("SP", wst.ap()[:, 0:1024], Wg2_d)
    a.op("DVE", lambda e: e.tensor_copy(wg2b.ap()[:, :], wst.ap()[:, 0:1024]),
         reads=[wst.ap()[:, 0:1024]], writes=[wg2b.ap()[:, :]])
    nw8 = sb("nw8", [128, 8]); mask8 = sb("mask8", [128, 1])
    a.dma("SP", nw8.ap()[:, :], v_nw)
    a.dma("SP", mask8.ap()[:, :], v_mask)

    maskT = sb("maskT", [128, 128], BF16)
    a.op("DVE", lambda e: e.memset(maskT.ap()[:, :], 1.0), writes=[maskT.ap()[:, :]])
    a.op("POOL", lambda e: e.affine_select(
        maskT.ap()[:, :], maskT.ap()[:, :], [[1, 128]], ALU.is_ge, 0.0,
        base=0, channel_multiplier=-1),
        reads=[maskT.ap()[:, :]], writes=[maskT.ap()[:, :]])
    maskT64 = sb("maskT64", [128, 128], BF16)
    triC = sb("triC", [128, 128], BF16)
    a.op("DVE", lambda e: e.tensor_scalar(maskT64.ap()[:, :], maskT.ap()[:, :],
                                          18446744073709551616.0, None, ALU.mult),
         reads=[maskT.ap()[:, :]], writes=[maskT64.ap()[:, :]])
    m63 = sb("m63", [128, 128], BF16)
    a.op("DVE", lambda e: e.memset(m63.ap()[0:64, :], 1.0), writes=[m63.ap()[0:64, :]])
    a.op("DVE", lambda e: e.memset(m63.ap()[64:128, :], 0.0), writes=[m63.ap()[64:128, :]])
    a.op("DVE", lambda e: e.tensor_sub(triC.ap()[:, :], maskT.ap()[:, :], m63.ap()[:, :]),
         reads=[maskT.ap()[:, :], m63.ap()[:, :]], writes=[triC.ap()[:, :]])
    onescA = sb("onescA", [128, 1], BF16)
    onescH = sb("onescH", [128, 1], BF16)
    a.op("DVE", lambda e: e.memset(onescA.ap()[:, :], 1.0), writes=[onescA.ap()[:, :]])
    a.op("DVE", lambda e: e.memset(onescH.ap()[0:64, :], 0.0), writes=[onescH.ap()[0:64, :]])
    a.op("DVE", lambda e: e.memset(onescH.ap()[64:128, :], 1.0), writes=[onescH.ap()[64:128, :]])
    onesf = sb("onesf", [128, 1])
    a.op("DVE", lambda e: e.memset(onesf.ap()[:, :], 1.0), writes=[onesf.ap()[:, :]])
    ones1 = sb("ones1", [1, 128], BF16)
    a.op("DVE", lambda e: e.memset(ones1.ap()[:, :], 1.0), writes=[ones1.ap()[:, :]])

    NT = CH // 128
    x1_c = sb("x1_c", [128, NT * C])
    xn1 = sb("xn1", [128, NT * C], BF16)
    xn1T = sb("xn1T", [128, 8 * CH], BF16)
    qTs = sb("qTs", [128, 8 * CH], BF16)
    kTs = sb("kTs", [128, 8 * CH], BF16)
    kNT = sb("kNT", [128, NT * C], BF16)
    vv = sb("vv", [128, NT * C], BF16)
    f1T = sb("f1T", [128, CH], BF16)
    g1T = sb("g1T", [128, CH], BF16)
    sT = sb("sT", [128, 4 * CH], BF16)
    logf = sb("logf", [128, NT * CH], BF16)
    s_tm = sb("s_tm", [128, NT * CH], BF16)
    sgT = sb("sgT", [128, 8 * CH], BF16)
    wkA = sb("wkA", [128, CH])
    wkB = sb("wkB", [128, CH])
    P_bf = sb("P_bf", [128, CH], BF16)
    eGc = sb("eGc", [128, CH], BF16)
    eGcN = sb("eGcN", [128, CH], BF16)
    eGcTN = sb("eGcTN", [128, 4 * 128], BF16)
    Bt = sb("Bt", [128, CH], BF16)
    AmT = sb("AmT", [128, 8 * 128], BF16)
    okf = sb("okf", [128, CH])
    mxs = sb("mxs", [128, 8]); nmx = sb("nmx", [128, 8])
    ssum = sb("ssum", [128, 8]); rsum = sb("rsum", [128, 8])
    qvU = sb("qvU", [128, CH])
    qvt = sb("qvt", [128, CH], BF16)
    qvT = sb("qvT", [128, 4 * 128], BF16)
    qhT = sb("qhT", [128, 4 * 128], BF16)
    BtT = sb("BtT", [128, 4 * 128], BF16)
    W2m = sb("W2m", [128, 8 * 128], BF16)
    og = sb("og", [128, 8 * 128])
    outN = sb("outN", [128, 8 * 128], BF16)
    att_t = sb("att_t", [128, C])
    eGLr = sb("eGLr", [1, 512], BF16)
    eGcLr = sb("eGcLr", [1, 512], BF16)
    PLb_s = sb("PLb_s", [128, CH], BF16)
    ELb_s = sb("ELb_s", [128, CH], BF16)
    ELcol = sb("ELcol", [128, 4])
    PLcol = sb("PLcol", [128, 4])
    hk = sb("hk", [128, 512], BF16)
    hv = sb("hv", [128, 4 * 128], BF16)
    a.op("DVE", lambda e: e.memset(hk.ap()[:, :], 0.0), writes=[hk.ap()[:, :]])
    a.op("DVE", lambda e: e.memset(hv.ap()[:, :], 0.0), writes=[hv.ap()[:, :]])
    bufs["junk_shared"] = wst
    alloc_ln_bufs(a, nc, bufs, "lnD", NT, alloc=sb_raw)
    eps_col = sb("eps_col", [128, 1])
    a.op("DVE", lambda e: e.memset(eps_col.ap()[:, :], 1e-5), writes=[eps_col.ap()[:, :]])

    ps1 = ps_raw("ps1", [128, CH], F32)
    ps2 = ps_raw("ps2", [128, CH], F32)
    ps3 = ps_raw("ps3", [128, 2 * CH], F32)
    psP = ps_raw("psP", [128, CH], F32)
    psV = ps_raw("psV", [128, 2 * CH], F32)
    psR = ps_raw("psR", [1, CH], F32)

    for sc in range(nsc):
        out_sc = sc >= 1
        for tt in range(NT):
            a.dma("SP", x1_c.ap()[:, tt * C:(tt + 1) * C],
                  x1_d[sc * CH + tt * 128: sc * CH + (tt + 1) * 128, :])
        x1_tiles = [x1_c.ap()[:, tt * C:(tt + 1) * C] for tt in range(NT)]
        xn_tiles = [xn1.ap()[:, tt * C:(tt + 1) * C] for tt in range(NT)]
        layernorm_tiles(a, nc, bufs, x1_tiles, xn_tiles, tag="lnD")
        for tt in range(NT):
            for ci in range(8):
                a.dma("SP", xn1T.ap()[:, ci * CH + tt * 128: ci * CH + (tt + 1) * 128],
                      xn1.ap()[:, tt * C + ci * 128: tt * C + (ci + 1) * 128],
                      transpose=True)
        for h in range(H):
            for ci in range(8):
                a.mm(psP.ap()[:, :], lhsT_tile(wqb, ci, h),
                     xn1T.ap()[:, ci * CH:(ci + 1) * CH], ci == 0, ci == 7)
            a.op("ACT", lambda e, h=h: e.activation(
                qTs.ap()[:, h * CH:(h + 1) * CH], psP.ap()[:, :], AF.Silu),
                reads=[psP.ap()[:, :]], writes=[qTs.ap()[:, h * CH:(h + 1) * CH]])
            for ci in range(8):
                a.mm(psP.ap()[:, :], lhsT_tile(wkb, ci, h),
                     xn1T.ap()[:, ci * CH:(ci + 1) * CH], ci == 0, ci == 7)
            a.op("ACT", lambda e, h=h: e.activation(
                kTs.ap()[:, h * CH:(h + 1) * CH], psP.ap()[:, :], AF.Silu),
                reads=[psP.ap()[:, :]], writes=[kTs.ap()[:, h * CH:(h + 1) * CH]])
        for h in range(H):
            for lc in range(NT):
                a.dma("SP", kNT.ap()[:, lc * C + h * 128: lc * C + (h + 1) * 128],
                      kTs.ap()[:, h * CH + lc * 128: h * CH + (lc + 1) * 128],
                      transpose=True)
        for tt in range(NT):
            for half in range(2):
                for ci in range(8):
                    a.mm(psV.ap()[:, half * CH:(half + 1) * CH],
                         xn1T.ap()[:, ci * CH + tt * 128: ci * CH + (tt + 1) * 128],
                         rhs_slab(wvb, ci, half * CH, (half + 1) * CH),
                         ci == 0, ci == 7)
            a.op("DVE", lambda e, tt=tt: e.tensor_copy(
                vv.ap()[:, tt * C:(tt + 1) * C], psV.ap()[:, :]),
                reads=[psV.ap()[:, :]], writes=[vv.ap()[:, tt * C:(tt + 1) * C]])
        for ci in range(8):
            a.mm(psP.ap()[:, :], lhsT_tile(wf1b, ci, 0, 128),
                 xn1T.ap()[:, ci * CH:(ci + 1) * CH], ci == 0, ci == 7)
        a.op("DVE", lambda e: e.tensor_copy(f1T.ap()[:, :], psP.ap()[:, :]),
             reads=[psP.ap()[:, :]], writes=[f1T.ap()[:, :]])
        for mt in range(4):
            a.mm(ps1.ap()[:, :], wf2b.ap()[:, mt * 128:(mt + 1) * 128], f1T.ap()[:, :],
                 True, True)
            a.op("ACT", lambda e, mt=mt: e.activation(
                sT.ap()[:, mt * CH:(mt + 1) * CH], ps1.ap()[:, :], AF.Sigmoid, scale=-1.0),
                reads=[ps1.ap()[:, :]], writes=[sT.ap()[:, mt * CH:(mt + 1) * CH]])
        for tt in range(NT):
            a.mm(ps1.ap()[:, :], f1T.ap()[:, tt * 128:(tt + 1) * 128], wf2b.ap()[:, :],
                 True, True)
            a.op("ACT", lambda e, tt=tt: e.activation(
                s_tm.ap()[:, tt * CH:(tt + 1) * CH], ps1.ap()[:, :], AF.Sigmoid, scale=-1.0),
                reads=[ps1.ap()[:, :]],
                writes=[s_tm.ap()[:, tt * CH:(tt + 1) * CH]])
            a.op("ACT", lambda e: e.activation(wkA.ap()[:, :], ps1.ap()[:, :], AF.Exp, scale=-1.0),
                 reads=[ps1.ap()[:, :]], writes=[wkA.ap()[:, :]])
            a.op("ACT", lambda e: e.activation(wkB.ap()[:, :], wkA.ap()[:, :], AF.Ln,
                                               bias=onesf.ap()[:, 0:1]),
                 reads=[wkA.ap()[:, :], onesf.ap()[:, 0:1]], writes=[wkB.ap()[:, :]])
            a.op("DVE", lambda e, tt=tt: e.tensor_scalar(
                logf.ap()[:, tt * CH:(tt + 1) * CH], wkB.ap()[:, :], -1.0, None, ALU.mult),
                reads=[wkB.ap()[:, :]], writes=[logf.ap()[:, tt * CH:(tt + 1) * CH]])
        for ci in range(8):
            a.mm(psP.ap()[:, :], lhsT_tile(wg1b, ci, 0, 128),
                 xn1T.ap()[:, ci * CH:(ci + 1) * CH], ci == 0, ci == 7)
        a.op("DVE", lambda e: e.tensor_copy(g1T.ap()[:, :], psP.ap()[:, :]),
             reads=[psP.ap()[:, :]], writes=[g1T.ap()[:, :]])
        if out_sc:
            for co in range(8):
                a.mm(ps1.ap()[:, :], wg2b.ap()[:, co * 128:(co + 1) * 128], g1T.ap()[:, :],
                     True, True)
                a.op("ACT", lambda e, co=co: e.activation(
                    sgT.ap()[:, co * CH:(co + 1) * CH], ps1.ap()[:, :], AF.Sigmoid),
                    reads=[ps1.ap()[:, :]], writes=[sgT.ap()[:, co * CH:(co + 1) * CH]])
        for lc in range(NT):
            lf = logf.ap()[:, lc * CH:(lc + 1) * CH]
            a.mm(ps1.ap()[:, :], maskT.ap()[:, :], lf, True, True)
            a.op("ACT", lambda e: e.activation(P_bf.ap()[:, :], ps1.ap()[:, :], AF.Exp),
                 reads=[ps1.ap()[:, :]], writes=[P_bf.ap()[:, :]])
            a.mm(ps2.ap()[:, :], triC.ap()[:, :], lf, True, True)
            a.op("ACT", lambda e: e.activation(eGc.ap()[:, :], ps2.ap()[:, :], AF.Exp),
                 reads=[ps2.ap()[:, :]], writes=[eGc.ap()[:, :]])
            a.op("ACT", lambda e: e.activation(eGcN.ap()[:, :], ps2.ap()[:, :], AF.Exp, scale=-1.0),
                 reads=[ps2.ap()[:, :]], writes=[eGcN.ap()[:, :]])
            for mt in range(4):
                a.mm(ps1.ap()[:, mt * 128:(mt + 1) * 128],
                     lf[:, mt * 128:(mt + 1) * 128], triC.ap()[:, :], True, True)
            a.op("ACT", lambda e: e.activation(eGcTN.ap()[:, :], ps1.ap()[:, :], AF.Exp, scale=-1.0),
                 reads=[ps1.ap()[:, :]], writes=[eGcTN.ap()[:, :]])
            for mt in range(4):
                a.op("ACT", lambda e, mt=mt: e.activation(
                    ELcol.ap()[:, mt:mt + 1], ps1.ap()[:, mt * 128 + 127: mt * 128 + 128],
                    AF.Exp),
                    reads=[ps1.ap()[:, mt * 128 + 127: mt * 128 + 128]],
                    writes=[ELcol.ap()[:, mt:mt + 1]])
            a.mm(psR.ap()[0:1, 0:CH], onescA.ap()[:, :], lf, True, True)
            a.op("ACT", lambda e: e.activation(eGLr.ap()[:, :], psR.ap()[0:1, 0:CH], AF.Exp),
                 reads=[psR.ap()[0:1, 0:CH]], writes=[eGLr.ap()[:, :]])
            a.mm(psR.ap()[0:1, 0:CH], onescH.ap()[:, :], lf, True, True)
            a.op("ACT", lambda e: e.activation(eGcLr.ap()[:, :], psR.ap()[0:1, 0:CH], AF.Exp),
                 reads=[psR.ap()[0:1, 0:CH]], writes=[eGcLr.ap()[:, :]])
            a.mm(ps2.ap()[:, :], ones1.ap()[:, :], eGLr.ap()[:, :], True, True)
            a.op("ACT", lambda e: e.copy(PLb_s.ap()[:, :], ps2.ap()[:, :]),
                 reads=[ps2.ap()[:, :]], writes=[PLb_s.ap()[:, :]])
            a.mm(ps2.ap()[:, :], ones1.ap()[:, :], eGcLr.ap()[:, :], True, True)
            a.op("ACT", lambda e: e.copy(ELb_s.ap()[:, :], ps2.ap()[:, :]),
                 reads=[ps2.ap()[:, :]], writes=[ELb_s.ap()[:, :]])
            for mt in range(4):
                a.mm(ps2.ap()[:, mt:mt + 1], lf[:, mt * 128:(mt + 1) * 128],
                     onescA.ap()[:, :], True, True)
            a.op("ACT", lambda e: e.activation(PLcol.ap()[:, :], ps2.ap()[:, 0:4], AF.Exp),
                 reads=[ps2.ap()[:, 0:4]], writes=[PLcol.ap()[:, :]])
            a.op("DVE", lambda e: e.tensor_mul(
                Bt.ap()[:, :], s_tm.ap()[:, lc * CH:(lc + 1) * CH], eGcN.ap()[:, :]),
                reads=[s_tm.ap()[:, lc * CH:(lc + 1) * CH], eGcN.ap()[:, :]],
                writes=[Bt.ap()[:, :]])
            for mt in range(4):
                a.op("DVE", lambda e, mt=mt: e.tensor_mul(
                    BtT.ap()[:, mt * 128:(mt + 1) * 128],
                    sT.ap()[:, mt * CH + lc * 128: mt * CH + (lc + 1) * 128],
                    eGcTN.ap()[:, mt * 128:(mt + 1) * 128]),
                    reads=[sT.ap()[:, mt * CH + lc * 128: mt * CH + (lc + 1) * 128],
                           eGcTN.ap()[:, mt * 128:(mt + 1) * 128]],
                    writes=[BtT.ap()[:, mt * 128:(mt + 1) * 128]])
            for h in range(H):
                a.mm(ps3.ap()[:, h * 128:(h + 1) * 128],
                     kTs.ap()[:, h * CH + lc * 128: h * CH + (lc + 1) * 128],
                     qTs.ap()[:, h * CH + lc * 128: h * CH + (lc + 1) * 128],
                     True, True)
            for h in range(H):
                a.op("DVE", lambda e, h=h: e.tensor_mul(
                    AmT.ap()[:, h * 128:(h + 1) * 128],
                    ps3.ap()[:, h * 128:(h + 1) * 128], maskT.ap()[:, :]),
                    reads=[ps3.ap()[:, h * 128:(h + 1) * 128], maskT.ap()[:, :]],
                    writes=[AmT.ap()[:, h * 128:(h + 1) * 128]])
            for h in range(H):
                a.mm(ps1.ap()[:, h * 64:(h + 1) * 64],
                     AmT.ap()[:, h * 128:(h + 1) * 128], Bt.ap()[:, h * 64:(h + 1) * 64],
                     True, True)
                a.mm(ps2.ap()[:, h * 64:(h + 1) * 64],
                     qTs.ap()[:, h * CH + lc * 128: h * CH + (lc + 1) * 128],
                     hk.ap()[:, h * 64:(h + 1) * 64], True, True)
            a.op("DVE", lambda e: e.tensor_mul(okf.ap()[:, :], ps1.ap()[:, :], eGc.ap()[:, :]),
                 reads=[ps1.ap()[:, :], eGc.ap()[:, :]], writes=[okf.ap()[:, :]])
            a.op("DVE", lambda e: e.tensor_mul(qvU.ap()[:, :], ps2.ap()[:, :], P_bf.ap()[:, :]),
                 reads=[ps2.ap()[:, :], P_bf.ap()[:, :]], writes=[qvU.ap()[:, :]])
            a.op("DVE", lambda e: e.tensor_add(okf.ap()[:, :], okf.ap()[:, :], qvU.ap()[:, :]),
                 reads=[okf.ap()[:, :], qvU.ap()[:, :]], writes=[okf.ap()[:, :]])
            a.op("DVE", lambda e: e.tensor_reduce(
                mxs.ap()[:, :].unsqueeze(-1), okf.ap()[:, :].rearrange("p (h m) -> p h m", m=64),
                AX.X, ALU.max),
                reads=[okf.ap()[:, :]], writes=[mxs.ap()[:, :]])
            a.op("DVE", lambda e: e.tensor_scalar(nmx.ap()[:, :], mxs.ap()[:, :], -1.0, None, ALU.mult),
                 reads=[mxs.ap()[:, :]], writes=[nmx.ap()[:, :]])
            for h in range(H):
                a.op("ACT", lambda e, h=h: e.activation(
                    qvU.ap()[:, h * 64:(h + 1) * 64], okf.ap()[:, h * 64:(h + 1) * 64],
                    AF.Exp, bias=nmx.ap()[:, h:h + 1],
                    accum_out=ssum.ap()[:, h:h + 1]),
                    reads=[okf.ap()[:, h * 64:(h + 1) * 64], nmx.ap()[:, h:h + 1]],
                    writes=[qvU.ap()[:, h * 64:(h + 1) * 64], ssum.ap()[:, h:h + 1]])
            a.op("DVE", lambda e: e.reciprocal(rsum.ap()[:, :], ssum.ap()[:, :]),
                 reads=[ssum.ap()[:, :]], writes=[rsum.ap()[:, :]])
            a.op("DVE", lambda e: e.tensor_scalar(rsum.ap()[:, :], rsum.ap()[:, :],
                                                  5.421010862427522e-20, None, ALU.mult),
                 reads=[rsum.ap()[:, :]], writes=[rsum.ap()[:, :]])
            a.op("DVE", lambda e: e.tensor_mul(qvU.ap()[:, :], qvU.ap()[:, :], eGc.ap()[:, :]),
                 reads=[qvU.ap()[:, :], eGc.ap()[:, :]], writes=[qvU.ap()[:, :]])
            for h in range(H):
                a.op("DVE", lambda e, h=h: e.tensor_scalar_mul(
                    qvt.ap()[:, h * 64:(h + 1) * 64], qvU.ap()[:, h * 64:(h + 1) * 64],
                    rsum.ap()[:, h:h + 1]),
                    reads=[qvU.ap()[:, h * 64:(h + 1) * 64], rsum.ap()[:, h:h + 1]],
                    writes=[qvt.ap()[:, h * 64:(h + 1) * 64]])
            for mt in range(4):
                a.dma("SP", qvT.ap()[:, mt * 128:(mt + 1) * 128],
                      qvt.ap()[:, mt * 128:(mt + 1) * 128], transpose=True)
            a.op("DVE", lambda e: e.reciprocal(nmx.ap()[:, 0:4], ELcol.ap()[:, :]),
                 reads=[ELcol.ap()[:, :]], writes=[nmx.ap()[:, 0:4]])
            a.op("DVE", lambda e: e.tensor_mul(nmx.ap()[:, 0:4], PLcol.ap()[:, :], nmx.ap()[:, 0:4]),
                 reads=[PLcol.ap()[:, :], nmx.ap()[:, 0:4]], writes=[nmx.ap()[:, 0:4]])
            a.op("DVE", lambda e: e.tensor_scalar(nmx.ap()[:, 0:4], nmx.ap()[:, 0:4],
                                                  18446744073709551616.0, None, ALU.mult),
                 reads=[nmx.ap()[:, 0:4]], writes=[nmx.ap()[:, 0:4]])
            for mt in range(4):
                a.op("DVE", lambda e, mt=mt: e.tensor_scalar_mul(
                    qhT.ap()[:, mt * 128:(mt + 1) * 128], qvT.ap()[:, mt * 128:(mt + 1) * 128],
                    nmx.ap()[:, mt:mt + 1]),
                    reads=[qvT.ap()[:, mt * 128:(mt + 1) * 128], nmx.ap()[:, mt:mt + 1]],
                    writes=[qhT.ap()[:, mt * 128:(mt + 1) * 128]])
            for h in range(H):
                mt, poff = h // 2, 64 * (h % 2)
                a.mm(ps3.ap()[:, h * 128:(h + 1) * 128],
                     BtT.ap()[poff:poff + 64, mt * 128:(mt + 1) * 128],
                     qvT.ap()[poff:poff + 64, mt * 128:(mt + 1) * 128],
                     True, True)
            for h in range(H):
                a.op("DVE", lambda e, h=h: e.tensor_mul(
                    W2m.ap()[:, h * 128:(h + 1) * 128],
                    ps3.ap()[:, h * 128:(h + 1) * 128], maskT64.ap()[:, :]),
                    reads=[ps3.ap()[:, h * 128:(h + 1) * 128], maskT64.ap()[:, :]],
                    writes=[W2m.ap()[:, h * 128:(h + 1) * 128]])
            for h in range(H):
                mt, poff = h // 2, 64 * (h % 2)
                a.mm(ps3.ap()[:, h * 128:(h + 1) * 128],
                     vv.ap()[:, lc * C + h * 128: lc * C + (h + 1) * 128],
                     W2m.ap()[:, h * 128:(h + 1) * 128], True, False)
                a.mm(ps3.ap()[:, h * 128:(h + 1) * 128],
                     hv.ap()[poff:poff + 64, mt * 128:(mt + 1) * 128],
                     qhT.ap()[poff:poff + 64, mt * 128:(mt + 1) * 128], False, True)
            if out_sc:
                for h in range(H):
                    a.op("DVE", lambda e, h=h: e.tensor_mul(
                        og.ap()[:, h * 128:(h + 1) * 128],
                        ps3.ap()[:, h * 128:(h + 1) * 128],
                        sgT.ap()[:, h * CH + lc * 128: h * CH + (lc + 1) * 128]),
                        reads=[ps3.ap()[:, h * 128:(h + 1) * 128],
                               sgT.ap()[:, h * CH + lc * 128: h * CH + (lc + 1) * 128]],
                        writes=[og.ap()[:, h * 128:(h + 1) * 128]])
            for h in range(H):
                a.mm(ps1.ap()[:, h * 64:(h + 1) * 64],
                     kNT.ap()[:, lc * C + h * 128: lc * C + (h + 1) * 128],
                     Bt.ap()[:, h * 64:(h + 1) * 64], True, True)
            for h in range(H):
                mt, poff = h // 2, 64 * (h % 2)
                a.mm(ps2.ap()[poff:poff + 64, mt * 128:(mt + 1) * 128],
                     Bt.ap()[:, h * 64:(h + 1) * 64],
                     vv.ap()[:, lc * C + h * 128: lc * C + (h + 1) * 128], True, True)
            a.op("DVE", lambda e: e.tensor_mul(wkA.ap()[:, :], ps1.ap()[:, :], ELb_s.ap()[:, :]),
                 reads=[ps1.ap()[:, :], ELb_s.ap()[:, :]], writes=[wkA.ap()[:, :]])
            a.op("DVE", lambda e: e.tensor_mul(wkB.ap()[:, :], hk.ap()[:, :], PLb_s.ap()[:, :]),
                 reads=[hk.ap()[:, :], PLb_s.ap()[:, :]], writes=[wkB.ap()[:, :]])
            a.op("DVE", lambda e: e.tensor_add(hk.ap()[:, :], wkA.ap()[:, :], wkB.ap()[:, :]),
                 reads=[wkA.ap()[:, :], wkB.ap()[:, :]], writes=[hk.ap()[:, :]])
            for mt in range(4):
                a.op("DVE", lambda e, mt=mt: e.tensor_scalar_mul(
                    wkA.ap()[:, mt * 128:(mt + 1) * 128], ps2.ap()[:, mt * 128:(mt + 1) * 128],
                    ELcol.ap()[:, mt:mt + 1]),
                    reads=[ps2.ap()[:, mt * 128:(mt + 1) * 128], ELcol.ap()[:, mt:mt + 1]],
                    writes=[wkA.ap()[:, mt * 128:(mt + 1) * 128]])
                a.op("DVE", lambda e, mt=mt: e.scalar_tensor_tensor(
                    hv.ap()[:, mt * 128:(mt + 1) * 128], hv.ap()[:, mt * 128:(mt + 1) * 128],
                    PLcol.ap()[:, mt:mt + 1], wkA.ap()[:, mt * 128:(mt + 1) * 128],
                    ALU.mult, ALU.add),
                    reads=[hv.ap()[:, mt * 128:(mt + 1) * 128], PLcol.ap()[:, mt:mt + 1],
                           wkA.ap()[:, mt * 128:(mt + 1) * 128]],
                    writes=[hv.ap()[:, mt * 128:(mt + 1) * 128]])
            if out_sc:
                a.op("ACT", lambda e: e.activation(okf.ap()[:, :], og.ap()[:, 0:CH], AF.Square),
                     reads=[og.ap()[:, 0:CH]], writes=[okf.ap()[:, :]])
                a.op("ACT", lambda e: e.activation(qvU.ap()[:, :], og.ap()[:, CH:2 * CH], AF.Square),
                     reads=[og.ap()[:, CH:2 * CH]], writes=[qvU.ap()[:, :]])
                for half in range(2):
                    for hh4 in range(4):
                        h = half * 4 + hh4
                        srcq = (okf.ap()[:, h * 128:(h + 1) * 128] if h < 4
                                else qvU.ap()[:, (h - 4) * 128:(h - 3) * 128])
                        a.mm(psR.ap()[0:1, hh4 * 128:(hh4 + 1) * 128],
                             onesf.ap()[:, :], srcq, True, True)
                    a.op("ACT", lambda e, half=half: e.activation(
                        att_t.ap()[0:1, half * CH:(half + 1) * CH], psR.ap()[0:1, :],
                        AF.Ln, scale=1.0 / 128.0, bias=eps_col.ap()[0:1, 0:1]),
                        reads=[psR.ap()[0:1, :], eps_col.ap()[0:1, 0:1]],
                        writes=[att_t.ap()[0:1, half * CH:(half + 1) * CH]])
                a.op("ACT", lambda e: e.activation(att_t.ap()[0:1, :], att_t.ap()[0:1, :],
                                                   AF.Exp, scale=-0.5),
                     reads=[att_t.ap()[0:1, :]], writes=[att_t.ap()[0:1, :]])
                a.op("DVE", lambda e: e.tensor_copy(eGLr.ap()[:, :], att_t.ap()[0:1, 0:512]),
                     reads=[att_t.ap()[0:1, 0:512]], writes=[eGLr.ap()[:, :]])
                a.op("DVE", lambda e: e.tensor_copy(eGcLr.ap()[:, :], att_t.ap()[0:1, 512:1024]),
                     reads=[att_t.ap()[0:1, 512:1024]], writes=[eGcLr.ap()[:, :]])
                a.mm(ps3.ap()[:, 0:CH], ones1.ap()[:, :], eGLr.ap()[:, :], True, True)
                a.mm(ps3.ap()[:, CH:2 * CH], ones1.ap()[:, :], eGcLr.ap()[:, :], True, True)
                a.op("DVE", lambda e: e.tensor_mul(og.ap()[:, :], og.ap()[:, :], ps3.ap()[:, :]),
                     reads=[og.ap()[:, :], ps3.ap()[:, :]], writes=[og.ap()[:, :]])
                for co in range(8):
                    a.op("DVE", lambda e, co=co: e.tensor_scalar_mul(
                        outN.ap()[:, co * 128:(co + 1) * 128], og.ap()[:, co * 128:(co + 1) * 128],
                        nw8.ap()[:, co:co + 1]),
                        reads=[og.ap()[:, co * 128:(co + 1) * 128], nw8.ap()[:, co:co + 1]],
                        writes=[outN.ap()[:, co * 128:(co + 1) * 128]])
                for half in range(2):
                    for ci in range(8):
                        a.mm(psV.ap()[:, half * CH:(half + 1) * CH],
                             outN.ap()[:, ci * 128:(ci + 1) * 128],
                             rhs_slab(wob, ci, half * CH, (half + 1) * CH),
                             ci == 0, ci == 7)
                a.op("DVE", lambda e: e.tensor_copy(att_t.ap()[:, :], psV.ap()[:, :]),
                     reads=[psV.ap()[:, :]], writes=[att_t.ap()[:, :]])
                row0 = (sc - 1) * CH + lc * 128
                a.dma("POOL", at_d[row0:row0 + 128, :], att_t.ap()[:, :])
        if sc == 0:
            a.op("DVE", lambda e: e.tensor_scalar_mul(hk.ap()[:, :], hk.ap()[:, :], mask8.ap()[:, 0:1]),
                 reads=[hk.ap()[:, :], mask8.ap()[:, 0:1]], writes=[hk.ap()[:, :]])
            a.op("DVE", lambda e: e.tensor_scalar_mul(hv.ap()[:, :], hv.ap()[:, :], mask8.ap()[:, 0:1]),
                 reads=[hv.ap()[:, :], mask8.ap()[:, 0:1]], writes=[hv.ap()[:, :]])

    if ctx is None:
        a.finish()
    return nc


def _run_ph4(inp, x1, nsc=9):
    nc = build_ph4(nsc)
    base = {
        "Wq": np.asarray(inp["L1_attn_Wq"], np.float32),
        "Wk1": np.asarray(inp["L1_attn_Wk"], np.float32),
        "Wv1": np.asarray(inp["L1_attn_Wv"], np.float32),
        "Wo1": np.asarray(inp["L1_attn_Wo"], np.float32),
        "Wf1": np.asarray(inp["L1_attn_Wf1"], np.float32),
        "Wf2": np.asarray(inp["L1_attn_Wf2"], np.float32),
        "Wg1": np.asarray(inp["L1_attn_Wg1"], np.float32),
        "Wg2": np.asarray(inp["L1_attn_Wg2"], np.float32),
        "nw": _coltiles(inp["L1_attn_norm_w"]),
    }
    in_maps = []
    for core in range(N_CORES):
        m = dict(base)
        m["x1"] = x1[core]
        m["mask"] = np.full((128, 1), 0.0 if core % 4 == 0 else 1.0, np.float32)
        in_maps.append(m)
    res = run_bass_kernel_spmd(nc, in_maps, core_ids=list(range(N_CORES)))
    return res.results


def build_ph5(nch5=8, ctx=None):
    nc, a, mk_dram, sb_raw, ps_raw = ctx if ctx is not None else _own_ctx()
    HID = 3072
    n_out = nch5 * CH
    x1_d = mk_dram("x1", [n_out + CH, C], F32, "ExternalInput")
    at_d = mk_dram("attn1", [n_out, C], F32, "ExternalInput")
    wgT_d = mk_dram("wgT", [C, HID], I32, "ExternalInput")
    wuT_d = mk_dram("wuT", [C, HID], I32, "ExternalInput")
    wdT_d = mk_dram("wdT", [HID, C], I32, "ExternalInput")
    out_d = mk_dram("x2", [n_out, C], F32, "ExternalOutput")

    bufs = {}

    def sb(name, shape, dt=F32):
        t = sb_raw(name, shape, dt)
        bufs[name] = t
        return t

    wst = sb("wst", [128, 2 * 512], I32)
    wgb = load_grid(a, nc, "wg1t", wgT_d, 8, HID, st=wst, cw=512, alloc=sb_raw)
    wub = load_grid(a, nc, "wu1t", wuT_d, 8, HID, st=wst, cw=512, alloc=sb_raw)
    wd_sp = mk_dram("wd_sp", [24 * 128, C], BF16, "Internal")
    wd_stage = sb("wd_stage", [128, C], BF16)
    for hi in range(24):
        for half in range(2):
            p = half
            a.dma("SP", wst.ap()[:, p * 512:(p + 1) * 512],
                  wdT_d[128 * hi:128 * (hi + 1), half * 512:(half + 1) * 512])
            dsts = wd_stage.ap()[:, half * 512:(half + 1) * 512]
            a.op("DVE", lambda e, d=dsts, p=p: e.tensor_copy(
                d, wst.ap()[:, p * 512:(p + 1) * 512]),
                reads=[wst.ap()[:, p * 512:(p + 1) * 512]], writes=[dsts])
        a.dma("POOL", wd_sp[128 * hi:128 * (hi + 1), :], wd_stage.ap()[:, :])
    wd_sl = [sb(f"wd_sl{p}", [128, C], BF16) for p in range(2)]

    NT = CH // 128
    x1_c = sb("x1_c", [128, NT * C])
    xn1 = sb("xn1", [128, NT * C], BF16)
    xn1T = sb("xn1T", [128, 8 * CH], BF16)
    sgP = [sb(f"sg{i}", [128, CH], BF16) for i in range(2)]
    hT = sb("hT", [128, 24 * CH], BF16)
    resP = [sb(f"res{i}", [128, C]) for i in range(2)]
    class _TView:
        def __init__(self, apv):
            self._ap = apv

        def ap(self):
            return self._ap
    bufs["junk_shared"] = _TView(wst.ap().bitcast(F32)[:, 0:C])
    bufs["lnE_junk"] = bufs["junk_shared"]
    alloc_ln_bufs(a, nc, bufs, "lnE", NT, alloc=sb_raw)
    eps_col = sb("eps_col", [128, 1])
    a.op("DVE", lambda e: e.memset(eps_col.ap()[:, :], 1e-5), writes=[eps_col.ap()[:, :]])

    ps_gP = [ps_raw(f"ps_g{i}", [128, CH], F32) for i in range(2)]
    ps_uP = [ps_raw(f"ps_u{i}", [128, CH], F32) for i in range(2)]
    ps_dP = [ps_raw(f"ps_d{i}", [128, C], F32) for i in range(2)]

    for ch in range(nch5):
        for tt in range(NT):
            a.dma("SP", x1_c.ap()[:, tt * C:(tt + 1) * C],
                  x1_d[CH + ch * CH + tt * 128: CH + ch * CH + (tt + 1) * 128, :])
        x1_tiles = [x1_c.ap()[:, tt * C:(tt + 1) * C] for tt in range(NT)]
        xn_tiles = [xn1.ap()[:, tt * C:(tt + 1) * C] for tt in range(NT)]
        layernorm_tiles(a, nc, bufs, x1_tiles, xn_tiles, tag="lnE")
        for tt in range(NT):
            for ci in range(8):
                a.dma("SP", xn1T.ap()[:, ci * CH + tt * 128: ci * CH + (tt + 1) * 128],
                      xn1.ap()[:, tt * C + ci * 128: tt * C + (ci + 1) * 128],
                      transpose=True)
        for ht in range(24):
            ps_g = ps_gP[ht % 2]; ps_u = ps_uP[ht % 2]; sg = sgP[ht % 2]
            for ci in range(8):
                a.mm(ps_g.ap()[:, :], lhsT_tile(wgb, ci, ht, HID),
                     xn1T.ap()[:, ci * CH:(ci + 1) * CH], ci == 0, ci == 7)
            for ci in range(8):
                a.mm(ps_u.ap()[:, :], lhsT_tile(wub, ci, ht, HID),
                     xn1T.ap()[:, ci * CH:(ci + 1) * CH], ci == 0, ci == 7)
            a.op("ACT", lambda e, ps_g=ps_g, sg=sg: e.activation(
                sg.ap()[:, :], ps_g.ap()[:, :], AF.Silu, scale=1.0 / 127.0),
                 reads=[ps_g.ap()[:, :]], writes=[sg.ap()[:, :]])
            o = hT.ap()[:, ht * CH:(ht + 1) * CH]
            a.op("DVE", lambda e, o=o, sg=sg, ps_u=ps_u: e.tensor_mul(
                o, sg.ap()[:, :], ps_u.ap()[:, :]),
                 reads=[sg.ap()[:, :], ps_u.ap()[:, :]], writes=[o])
        for tt in range(NT):
            ps_d = ps_dP[tt % 2]; res_t = resP[tt % 2]
            for hi in range(24):
                slp = wd_sl[hi % 2]
                a.dma("SP", slp.ap()[:, :], wd_sp[128 * hi:128 * (hi + 1), :])
                lhs = hT.ap()[:, hi * CH + tt * 128: hi * CH + (tt + 1) * 128]
                a.mm(ps_d.ap()[:, 0:512], lhs, slp.ap()[:, 0:512],
                     hi == 0, hi == 23)
                a.mm(ps_d.ap()[:, 512:1024], lhs, slp.ap()[:, 512:1024],
                     hi == 0, hi == 23)
            row0 = ch * CH + tt * 128
            a.dma("SP", res_t.ap()[:, :], at_d[row0:row0 + 128, :])
            a.op("DVE", lambda e, ps_d=ps_d, res_t=res_t: e.scalar_tensor_tensor(
                res_t.ap()[:, :], ps_d.ap()[:, :], 1.0 / 16129.0,
                res_t.ap()[:, :], ALU.mult, ALU.add),
                reads=[ps_d.ap()[:, :], res_t.ap()[:, :]],
                writes=[res_t.ap()[:, :]])
            a.op("DVE", lambda e, tt=tt, res_t=res_t: e.tensor_add(
                res_t.ap()[:, :], res_t.ap()[:, :], x1_c.ap()[:, tt * C:(tt + 1) * C]),
                 reads=[res_t.ap()[:, :], x1_c.ap()[:, tt * C:(tt + 1) * C]],
                 writes=[res_t.ap()[:, :]])
            a.dma("POOL", out_d[row0:row0 + 128, :], res_t.ap()[:, :])

    if ctx is None:
        a.finish()
    return nc


def _run_ph5(inp, x1, attn1, nch5=8):
    nc = build_ph5(nch5)
    base = {
        "wgT": np.ascontiguousarray(np.asarray(inp["L1_ffn_w_gate"], np.int32).T),
        "wuT": np.ascontiguousarray(np.asarray(inp["L1_ffn_w_up"], np.int32).T),
        "wdT": np.ascontiguousarray(np.asarray(inp["L1_ffn_w_down"], np.int32).T),
    }
    in_maps = []
    for core in range(N_CORES):
        m = dict(base)
        m["x1"] = x1[core]
        m["attn1"] = attn1[core]
        in_maps.append(m)
    res = run_bass_kernel_spmd(nc, in_maps, core_ids=list(range(N_CORES)))
    return res.results


# version 07 (final)
# speedup vs baseline: 1.0318x; 1.0318x over previous
import os
import numpy as np
import concourse.bass as bass
import concourse.mybir as mybir
from concourse.bass_utils import run_bass_kernel_spmd

F32 = mybir.dt.float32
BF16 = mybir.dt.bfloat16
I32 = mybir.dt.int32
AF = mybir.ActivationFunctionType
ALU = mybir.AluOpType
AX = mybir.AxisListType

N_CORES = 8
B, T, C = 2, 16384, 1024
P_TOK = 4096
HW = 1024
HG = 512
EXT = HW + HG + P_TOK
CH = 512
NCH = EXT // CH
RND_C = 3.0 * 2.0 ** 22


_DT_SIZES = {}


def _dtsize(dt):
    s = _DT_SIZES.get(dt)
    if s is None:
        s = np.dtype(mybir.dt.np(dt)).itemsize
        _DT_SIZES[dt] = s
    return s


class _ApTensor:
    def __init__(self, apv):
        self._ap = apv

    def ap(self):
        return self._ap


class ASB:

    ENGINES = ("PE", "ACT", "DVE", "POOL", "SP")

    NQSEM = 4

    def __init__(self, nc):
        self.nc = nc
        self.eng = {"PE": nc.tensor, "ACT": nc.scalar, "DVE": nc.vector,
                    "POOL": nc.gpsimd, "SP": nc.sync}
        self.sem = {}
        self.cnt = {}
        self.incs = {}
        self.insts = {}
        for e in self.ENGINES:
            self.sem[e] = nc.alloc_semaphore(f"es_{e}")
            self.cnt[e] = 0
            self.incs[e] = []
            self.insts[e] = []
        self.q = {}
        for e in self.ENGINES:
            qn = "Q" + e
            self.q[qn] = {
                "sems": [nc.alloc_semaphore(f"qs_{e}{i}") for i in range(self.NQSEM)],
                "epoch": 0,
                "ecnt": 0,
                "etotal": [],
                "stot": [0] * self.NQSEM,
                "gated": [True] * self.NQSEM,
            }
        self.wm = {w: {} for w in self.ENGINES}
        self.reg = {}

    def _q_close(self, qn):
        st = self.q[qn]
        if st["ecnt"] == 0:
            return
        s = st["epoch"] % self.NQSEM
        st["stot"][s] += st["ecnt"]
        st["etotal"].append((s, st["stot"][s] * 16))
        st["epoch"] += 1
        st["ecnt"] = 0
        st["gated"][st["epoch"] % self.NQSEM] = False

    def _q_wait(self, waiter, qn, epoch):
        st = self.q[qn]
        if epoch == st["epoch"]:
            self._q_close(qn)
        s, val = st["etotal"][epoch]
        key = (qn, s)
        if self.wm[waiter].get(key, 0) < val:
            self.eng[waiter].wait_ge(st["sems"][s], val)
            self.wm[waiter][key] = val

    @staticmethod
    def _iv(ap):
        off = ap.offset
        if not isinstance(off, int):
            off = 0
        dims = list(ap.ap)
        step0, count0 = dims[0]
        esz = _dtsize(ap.dtype)
        span = 0
        for stp, cnt in dims[1:]:
            if stp > 0:
                span += stp * (cnt - 1)
        is_ps = "PSUM" in str(getattr(ap, "space", ""))
        if step0 > 0:
            p_lo = off // step0
            b_lo = off % step0
            if b_lo + span >= step0:
                return (ap.tensor.name, p_lo, p_lo + count0 + (b_lo + span) // step0,
                        0, step0 * esz)
            lo_b = b_lo * esz
            hi_b = (b_lo + span + 1) * esz
            if is_ps:
                lo_b = lo_b // 2048 * 2048
                hi_b = (hi_b + 2047) // 2048 * 2048
            return (ap.tensor.name, p_lo, p_lo + count0, lo_b, hi_b)
        return (ap.tensor.name, 0, count0, off * esz, (off + span + 1) * esz)

    def _edge(self, waiter, src, pos):
        if isinstance(src, tuple):
            self._q_wait(waiter, src[0], src[1])
            return
        incs = self.incs[src]
        val = None
        if incs and incs[-1][0] >= pos:
            lo, hi_i = 0, len(incs) - 1
            while lo < hi_i:
                mid = (lo + hi_i) // 2
                if incs[mid][0] >= pos:
                    hi_i = mid
                else:
                    lo = mid + 1
            val = incs[lo][1]
        else:
            inst = self.insts[src][pos]
            val = (incs[-1][1] if incs else 0) + 1
            inst.then_inc(self.sem[src], 1)
            incs.append((pos, val))
        if self.wm[waiter].get(src, 0) < val:
            self.eng[waiter].wait_ge(self.sem[src], val)
            self.wm[waiter][src] = val

    def _deps(self, waiter, reads, writes):
        need = {}
        for ap in reads:
            name, pl, ph, bl, bh = self._iv(ap)
            for rec in self.reg.get(name, ()):
                if rec[0] == "W" and rec[3] < ph and pl < rec[4] \
                        and rec[5] < bh and bl < rec[6]:
                    k = rec[1]
                    if need.get(k, -1) < rec[2]:
                        need[k] = rec[2]
        for ap in writes:
            name, pl, ph, bl, bh = self._iv(ap)
            for rec in self.reg.get(name, ()):
                if rec[3] < ph and pl < rec[4] and rec[5] < bh and bl < rec[6]:
                    k = rec[1]
                    if need.get(k, -1) < rec[2]:
                        need[k] = rec[2]
        for src, pos in need.items():
            self._edge(waiter, src, pos)

    def _record(self, src, pos, reads, writes):
        for ap in reads:
            name, pl, ph, bl, bh = self._iv(ap)
            self.reg.setdefault(name, []).append(("R", src, pos, pl, ph, bl, bh))
        for ap in writes:
            name, pl, ph, bl, bh = self._iv(ap)
            recs = self.reg.setdefault(name, [])
            recs[:] = [r for r in recs
                       if not (pl <= r[3] and r[4] <= ph and bl <= r[5] and r[6] <= bh)]
            recs.append(("W", src, pos, pl, ph, bl, bh))

    def op(self, e, emit, reads=(), writes=(), accum=False):
        if accum:
            self._deps(e, reads, ())
        else:
            self._deps(e, reads, writes)
        inst = emit(self.eng[e])
        pos = self.cnt[e]
        self.insts[e].append(inst)
        self.cnt[e] = pos + 1
        if accum:
            for ap in writes:
                name, pl, ph, bl, bh = self._iv(ap)
                for r in self.reg.get(name, ()):
                    if r[0] == "W" and r[3] == pl and r[4] == ph \
                            and r[5] == bl and r[6] == bh:
                        self.reg[name].remove(r)
                        break
                self.reg.setdefault(name, []).append(("W", e, pos, pl, ph, bl, bh))
            for ap in reads:
                name, pl, ph, bl, bh = self._iv(ap)
                self.reg.setdefault(name, []).append(("R", e, pos, pl, ph, bl, bh))
        else:
            self._record(e, pos, reads, writes)
        return inst

    def mm(self, out, lhsT, rhs, start, stop):
        return self.op("PE",
                       lambda t: t.matmul(out, lhsT, rhs, start=start, stop=stop),
                       reads=[lhsT, rhs], writes=[out], accum=not start)

    def dma(self, e, out, in_, **kw):
        qn = "Q" + e
        st = self.q[qn]
        self._deps(e, [in_], [out])
        s = st["epoch"] % self.NQSEM
        if st["ecnt"] == 0 and not st["gated"][s] and st["stot"][s] > 0:
            self.eng[e].wait_ge(st["sems"][s], st["stot"][s] * 16)
            st["gated"][s] = True
        inst = self.eng[e].dma_start(out, in_, **kw)
        self.cnt[e] += 1
        self.insts[e].append(inst)
        inst.then_inc(st["sems"][s], 16)
        st["ecnt"] += 1
        self._record((qn, st["epoch"]), 0, [in_], [out])
        return inst

    def finish(self, e="POOL"):
        for qn, st in self.q.items():
            self._q_close(qn)
            for s in range(self.NQSEM):
                if st["stot"][s] > 0:
                    key = (qn, s)
                    val = st["stot"][s] * 16
                    if self.wm[e].get(key, 0) < val:
                        self.eng[e].wait_ge(st["sems"][s], val)
                        self.wm[e][key] = val


def load_grid(a, nc, name, W_ap, nrows, ncol, st=None, st_dt=F32, wb=None, cw=1024,
              alloc=None):
    if alloc is None:
        alloc = nc.alloc_sbuf_tensor
    if wb is None:
        wb = alloc(name + "_b", [128, nrows * ncol], BF16)
    CW = min(ncol, cw)
    if st is None:
        st = alloc(name + "_st", [128, 2 * CW], st_dt)
    k = 0
    for ci in range(nrows):
        for c0 in range(0, ncol, CW):
            p = k % 2
            k += 1
            a.dma("SP", st.ap()[:, p * CW:(p + 1) * CW],
                  W_ap[128 * ci:128 * (ci + 1), c0:c0 + CW])
            dst = wb.ap()[:, ci * ncol + c0: ci * ncol + c0 + CW]
            eng = "DVE" if p == 0 else "ACT"
            if eng == "DVE":
                a.op("DVE", lambda e, d=dst, p=p: e.tensor_copy(
                    d, st.ap()[:, p * CW:(p + 1) * CW]),
                    reads=[st.ap()[:, p * CW:(p + 1) * CW]], writes=[dst])
            else:
                a.op("ACT", lambda e, d=dst, p=p: e.copy(
                    d, st.ap()[:, p * CW:(p + 1) * CW]),
                    reads=[st.ap()[:, p * CW:(p + 1) * CW]], writes=[dst])
    return wb


def load_w1024_grid(a, nc, name, W_ap, ncol=1024, st=None, alloc=None):
    return load_grid(a, nc, name, W_ap, 8, ncol, st=st, alloc=alloc)


def lhsT_tile(wb, ci, co, ncol=1024):
    return wb.ap()[:, ci * ncol + co * 128: ci * ncol + co * 128 + 128]


def rhs_slab(wb, ci, lo, hi, ncol=1024):
    return wb.ap()[:, ci * ncol + lo: ci * ncol + hi]


def layernorm_tiles(a, nc, bufs, xin_tiles, out_tiles, g_bc=None, b_bc=None,
                    tag="ln", stat_eng="ACT", norm_eng="DVE", width=C):
    s1 = bufs[f"{tag}_s1"]
    s2 = bufs[f"{tag}_s2"]
    junk = bufs[f"{tag}_junk"]
    eps = bufs["eps_col"]
    for i, (xi, oi) in enumerate(zip(xin_tiles, out_tiles)):
        c0 = s1.ap()[:, i:i + 1]
        c1 = s2.ap()[:, i:i + 1]
        a.op(stat_eng, lambda e, xi=xi, c0=c0: e.activation(
            junk.ap()[:, :width], xi, AF.Copy, accum_out=c0),
            reads=[xi], writes=[junk.ap()[:, :width], c0])
        a.op(stat_eng, lambda e, xi=xi, c1=c1: e.activation(
            junk.ap()[:, :width], xi, AF.Square, accum_out=c1),
            reads=[xi], writes=[junk.ap()[:, :width], c1])
        m = bufs[f"{tag}_m"].ap()[:, i:i + 1]
        v = bufs[f"{tag}_v"].ap()[:, i:i + 1]
        rs = bufs[f"{tag}_rs"].ap()[:, i:i + 1]
        nb = bufs[f"{tag}_nb"].ap()[:, i:i + 1]
        a.op("DVE", lambda e, c0=c0, m=m: e.tensor_scalar(
            m, c0, 1.0 / width, None, ALU.mult), reads=[c0], writes=[m])
        a.op("DVE", lambda e, c1=c1, m=m, v=v: e.scalar_tensor_tensor(
            v, m, -1.0, m, ALU.mult, ALU.mult), reads=[m], writes=[v])
        a.op("DVE", lambda e, c1=c1, v=v: e.scalar_tensor_tensor(
            v, c1, 1.0 / width, v, ALU.mult, ALU.add), reads=[c1, v], writes=[v])
        a.op("ACT", lambda e, v=v, rs=rs: e.activation(
            rs, v, AF.Sqrt, bias=eps.ap()[:, 0:1]), reads=[v, eps.ap()[:, 0:1]], writes=[rs])
        a.op("DVE", lambda e, rs=rs: e.reciprocal(rs, rs), reads=[rs], writes=[rs])
        a.op(norm_eng, lambda e, xi=xi, oi=oi, rs=rs, m=m: e.tensor_scalar(
            oi, xi, m, rs, ALU.subtract, ALU.mult), reads=[xi, rs, m], writes=[oi])
        if g_bc is not None:
            a.op("DVE", lambda e, oi=oi: e.tensor_mul(oi, oi, g_bc[i] if isinstance(g_bc, list) else g_bc),
                 reads=[oi, g_bc[i] if isinstance(g_bc, list) else g_bc], writes=[oi])
        if b_bc is not None:
            a.op("POOL", lambda e, oi=oi: e.tensor_add(oi, oi, b_bc[i] if isinstance(b_bc, list) else b_bc),
                 reads=[oi, b_bc[i] if isinstance(b_bc, list) else b_bc], writes=[oi])


def alloc_ln_bufs(a, nc, bufs, tag, n, alloc=None):
    if alloc is None:
        alloc = nc.alloc_sbuf_tensor
    for nm, w in [(f"{tag}_s1", n), (f"{tag}_s2", n), (f"{tag}_m", n),
                  (f"{tag}_v", n), (f"{tag}_rs", n), (f"{tag}_nb", n)]:
        bufs[nm] = alloc(nm, [128, w], F32)
    if "junk_shared" not in bufs:
        bufs["junk_shared"] = alloc("junk_shared", [128, C], F32)
    bufs[f"{tag}_junk"] = bufs["junk_shared"]


def _own_ctx():
    nc = bass.Bass("TRN2", target_bir_lowering=False)
    a = ASB(nc)

    def mk_dram(name, shape, dt, kind):
        return nc.dram_tensor(name, shape, dt, kind=kind).ap()

    def ps_raw(name, shape, dt=F32):
        return nc.alloc_psum_tensor(name, shape, dt)

    return nc, a, mk_dram, nc.alloc_sbuf_tensor, ps_raw



def build_ph1(nchunks=NCH, dbg=False, ctx=None):
    nc, a, mk_dram, sb_raw, ps_raw = ctx if ctx is not None else _own_ctx()
    ntok = nchunks * CH
    xd = mk_dram("x_ext", [ntok, C], F32, "ExternalInput")
    Wk = mk_dram("Wk", [C, C], F32, "ExternalInput")
    Wv = mk_dram("Wv", [C, C], F32, "ExternalInput")
    Wr = mk_dram("Wr", [C, C], F32, "ExternalInput")
    Wo = mk_dram("Wo", [C, C], F32, "ExternalInput")
    v_ln0g = mk_dram("ln0g", [1, C], F32, "ExternalInput")
    v_ln0b = mk_dram("ln0b", [1, C], F32, "ExternalInput")
    v_tmk = mk_dram("tmk", [128, 8], F32, "ExternalInput")
    v_tmv = mk_dram("tmv", [128, 8], F32, "ExternalInput")
    v_tmr = mk_dram("tmr", [128, 8], F32, "ExternalInput")
    v_td = mk_dram("td", [128, 8], F32, "ExternalInput")
    v_tf = mk_dram("tf", [128, 8], F32, "ExternalInput")
    v_mask = mk_dram("mask", [128, 1], F32, "ExternalInput")
    xi_out = mk_dram("x_inner", [max(ntok - CH, CH), C], F32, "ExternalOutput")
    xn_out = mk_dram("xn_s", [max(ntok - 2 * CH, CH), C], F32, "ExternalOutput")

    bufs = {}

    def sb(name, shape, dt=F32):
        t = sb_raw(name, shape, dt)
        bufs[name] = t
        return t

    wst = sb("wst", [128, 2 * 1024])
    wkb = load_w1024_grid(a, nc, "wk", Wk, st=wst, alloc=sb_raw)
    wvb = load_w1024_grid(a, nc, "wv", Wv, st=wst, alloc=sb_raw)
    wrb = load_w1024_grid(a, nc, "wr", Wr, st=wst, alloc=sb_raw)
    wob = load_w1024_grid(a, nc, "wo", Wo, st=wst, alloc=sb_raw)

    eps_col = sb("eps_col", [128, 1])
    a.op("DVE", lambda e: e.memset(eps_col.ap()[:, :], 1e-5),
         writes=[eps_col.ap()[:, :]])
    tmk8 = sb("tmk8", [128, 8]); tmv8 = sb("tmv8", [128, 8]); tmr8 = sb("tmr8", [128, 8])
    td8 = sb("td8", [128, 8]); tf8 = sb("tf8", [128, 8])
    mask8 = sb("mask8", [128, 1])
    for t, src in [(tmk8, v_tmk), (tmv8, v_tmv), (tmr8, v_tmr),
                   (td8, v_td), (tf8, v_tf), (mask8, v_mask)]:
        a.dma("SP", t.ap()[:, :], src)
    g_bc = sb("g_bc", [128, C]); b_bc = sb("b_bc", [128, C])
    a.dma("SP", g_bc.ap()[:, :], v_ln0g.partition_broadcast(128))
    a.dma("SP", b_bc.ap()[:, :], v_ln0b.partition_broadcast(128))
    lam8 = sb("lam8", [128, 8]); eu8 = sb("eu8", [128, 8])
    a.op("ACT", lambda e: e.activation(lam8.ap()[:, :], td8.ap()[:, :], AF.Exp),
         reads=[td8.ap()[:, :]], writes=[lam8.ap()[:, :]])
    a.op("ACT", lambda e: e.activation(lam8.ap()[:, :], lam8.ap()[:, :], AF.Exp, scale=-1.0),
         reads=[lam8.ap()[:, :]], writes=[lam8.ap()[:, :]])
    a.op("ACT", lambda e: e.activation(eu8.ap()[:, :], tf8.ap()[:, :], AF.Exp),
         reads=[tf8.ap()[:, :]], writes=[eu8.ap()[:, :]])

    NT = CH // 128
    xt = [sb("xt0", [128, NT * C])] * 2
    xn_t = sb("xn_t", [128, NT * C])
    xa_t = sb("xa_t", [128, NT * C], BF16)
    xaT = [sb(f"xaT{p}", [128, 8 * CH], BF16) for p in range(2)]
    pcol = sb("pcol", [128, 8], BF16)
    t1 = sb("t1", [128, 8 * CH], BF16)
    mkT = sb("mkT", [128, 8 * CH], BF16)
    mvT = sb("mvT", [128, 8 * CH], BF16)
    mrT = t1
    ekP = [sb(f"ek{i}", [128, CH]) for i in range(2)]
    ekvP = [sb(f"ekv{i}", [128, CH]) for i in range(2)]
    AarP = [sb(f"Aar{i}", [128, CH]) for i in range(2)]
    BarP = [sb(f"Bar{i}", [128, CH]) for i in range(2)]
    numP = [sb(f"num{i}", [128, CH]) for i in range(2)]
    denP = [sb(f"den{i}", [128, CH]) for i in range(2)]
    sigrP = [sb(f"sigr{i}", [128, CH]) for i in range(2)]
    rwkvT = sb("rwkvT", [128, 8 * CH], BF16)
    Acol = sb("Acol", [128, 8]); Bcol = sb("Bcol", [128, 8])
    a.op("DVE", lambda e: e.memset(Acol.ap()[:, :], 0.0), writes=[Acol.ap()[:, :]])
    a.op("DVE", lambda e: e.memset(Bcol.ap()[:, :], 0.0), writes=[Bcol.ap()[:, :]])
    a.op("DVE", lambda e: e.memset(pcol.ap()[:, :], 0.0), writes=[pcol.ap()[:, :]])

    alloc_ln_bufs(a, nc, bufs, "lnA", NT, alloc=sb_raw)
    alloc_ln_bufs(a, nc, bufs, "lnB", NT, alloc=sb_raw)

    ps_pool = [ps_raw(f"ps_p{i}", [128, CH], F32) for i in range(4)]
    ps_o = [ps_raw(f"ps_o{i}", [128, C], F32) for i in range(2)]

    for ch in range(nchunks):
        par = ch % 2
        full = ch >= 1
        x_c = xt[par]
        xaT_c = xaT[par]; xaT_p = xaT[1 - par]
        for tt in range(NT):
            a.dma("SP", x_c.ap()[:, tt * C:(tt + 1) * C],
                  xd[ch * CH + tt * 128: ch * CH + (tt + 1) * 128, :])
        x_tiles = [x_c.ap()[:, tt * C:(tt + 1) * C] for tt in range(NT)]
        xn_tiles = [xn_t.ap()[:, tt * C:(tt + 1) * C] for tt in range(NT)]
        xa_tiles = [xa_t.ap()[:, tt * C:(tt + 1) * C] for tt in range(NT)]
        layernorm_tiles(a, nc, bufs, x_tiles, xn_tiles, tag="lnA")
        if ch >= 2 and nchunks == NCH:
            for tt in range(NT):
                a.dma("POOL", xn_out[(ch - 2) * CH + tt * 128:(ch - 2) * CH + (tt + 1) * 128, :],
                      xn_tiles[tt])
        layernorm_tiles(a, nc, bufs, xn_tiles, xa_tiles, g_bc=g_bc.ap()[:, :],
                        b_bc=b_bc.ap()[:, :], tag="lnB")
        for tt in range(NT):
            for ci in range(8):
                a.dma("SP",
                      xaT_c.ap()[:, ci * CH + tt * 128: ci * CH + (tt + 1) * 128],
                      xa_t.ap()[:, tt * C + ci * 128: tt * C + (ci + 1) * 128],
                      transpose=True)
        for ci in range(8):
            xa_a = xaT_c.ap()[:, ci * CH + 1: ci * CH + CH]
            xs_a = xaT_c.ap()[:, ci * CH: ci * CH + CH - 1]
            xa_b = xaT_c.ap()[:, ci * CH: ci * CH + 1]
            xs_b = pcol.ap()[:, ci:ci + 1]
            t1_a = t1.ap()[:, ci * CH + 1:(ci + 1) * CH]
            t1_b = t1.ap()[:, ci * CH: ci * CH + 1]
            a.op("POOL", lambda e, o=t1_a, i0=xa_a, i1=xs_a: e.tensor_sub(o, i0, i1),
                 reads=[xa_a, xs_a], writes=[t1_a])
            a.op("DVE", lambda e, o=t1_b, i0=xa_b, i1=xs_b: e.tensor_sub(o, i0, i1),
                 reads=[xa_b, xs_b], writes=[t1_b])
            for arr, vcol, eng in [(mkT, tmk8, "DVE"), (mvT, tmv8, "DVE")] + \
                                  ([(mrT, tmr8, "DVE")] if full else []):
                o_a = arr.ap()[:, ci * CH + 1:(ci + 1) * CH]
                o_b = arr.ap()[:, ci * CH: ci * CH + 1]
                a.op(eng, lambda e, o=o_a, t=t1_a, v=vcol, xs=xs_a, ci=ci: e.scalar_tensor_tensor(
                    o, t, v.ap()[:, ci:ci + 1], xs, ALU.mult, ALU.add),
                    reads=[t1_a, vcol.ap()[:, ci:ci + 1], xs_a], writes=[o_a])
                a.op(eng, lambda e, o=o_b, t=t1_b, v=vcol, xs=xs_b, ci=ci: e.scalar_tensor_tensor(
                    o, t, v.ap()[:, ci:ci + 1], xs, ALU.mult, ALU.add),
                    reads=[t1_b, vcol.ap()[:, ci:ci + 1], xs_b], writes=[o_b])
        for ci in range(8):
            dst = pcol.ap()[:, ci:ci + 1]
            srcc = xaT_c.ap()[:, ci * CH + CH - 1: ci * CH + CH]
            if ch == 2:
                a.op("DVE", lambda e, d=dst, s=srcc: e.tensor_scalar_mul(
                    d, s, mask8.ap()[:, 0:1]),
                    reads=[srcc, mask8.ap()[:, 0:1]], writes=[dst])
            else:
                a.op("DVE", lambda e, d=dst, s=srcc: e.tensor_copy(d, s),
                     reads=[srcc], writes=[dst])
        for co in range(8):
            ek = ekP[co % 2]; ekv = ekvP[co % 2]
            Aar = AarP[co % 2]; Bar = BarP[co % 2]
            num = numP[co % 2]; den = denP[co % 2]; sigr = sigrP[co % 2]
            ps_k = ps_pool[(3 * co) % 4]
            ps_v = ps_pool[(3 * co + 1) % 4]
            ps_r = ps_pool[(3 * co + 2) % 4]
            for ci in range(8):
                a.mm(ps_k.ap()[:, :], lhsT_tile(wkb, ci, co),
                     mkT.ap()[:, ci * CH:(ci + 1) * CH], ci == 0, ci == 7)
            a.op("ACT", lambda e, co=co: e.activation(ek.ap()[:, :], ps_k.ap()[:, :], AF.Exp),
                 reads=[ps_k.ap()[:, :]], writes=[ek.ap()[:, :]])
            for ci in range(8):
                a.mm(ps_v.ap()[:, :], lhsT_tile(wvb, ci, co),
                     mvT.ap()[:, ci * CH:(ci + 1) * CH], ci == 0, ci == 7)
            a.op("DVE", lambda e: e.tensor_mul(ekv.ap()[:, :], ek.ap()[:, :], ps_v.ap()[:, :]),
                 reads=[ek.ap()[:, :], ps_v.ap()[:, :]], writes=[ekv.ap()[:, :]])
            lam_s = lam8.ap()[:, co:co + 1].broadcast_to((128, CH))
            a.op("DVE", lambda e, co=co: e.tensor_tensor_scan(
                Aar.ap()[:, :], lam_s, ekv.ap()[:, :], Acol.ap()[:, co:co + 1],
                ALU.mult, ALU.add),
                reads=[lam8.ap()[:, co:co + 1], ekv.ap()[:, :], Acol.ap()[:, co:co + 1]],
                writes=[Aar.ap()[:, :]])
            a.op("DVE", lambda e, co=co: e.tensor_tensor_scan(
                Bar.ap()[:, :], lam_s, ek.ap()[:, :], Bcol.ap()[:, co:co + 1],
                ALU.mult, ALU.add),
                reads=[lam8.ap()[:, co:co + 1], ek.ap()[:, :], Bcol.ap()[:, co:co + 1]],
                writes=[Bar.ap()[:, :]])
            if full:
                for ci in range(8):
                    a.mm(ps_r.ap()[:, :], lhsT_tile(wrb, ci, co),
                         mrT.ap()[:, ci * CH:(ci + 1) * CH], ci == 0, ci == 7)
                a.op("ACT", lambda e: e.activation(sigr.ap()[:, :], ps_r.ap()[:, :], AF.Sigmoid),
                     reads=[ps_r.ap()[:, :]], writes=[sigr.ap()[:, :]])
                eu_c = eu8.ap()[:, co:co + 1]
                a.op("DVE", lambda e: e.scalar_tensor_tensor(
                    num.ap()[:, 1:CH], ekv.ap()[:, 1:CH], eu_c, Aar.ap()[:, 0:CH - 1],
                    ALU.mult, ALU.add),
                    reads=[ekv.ap()[:, 1:CH], eu_c, Aar.ap()[:, 0:CH - 1]],
                    writes=[num.ap()[:, 1:CH]])
                a.op("DVE", lambda e, co=co: e.scalar_tensor_tensor(
                    num.ap()[:, 0:1], ekv.ap()[:, 0:1], eu_c, Acol.ap()[:, co:co + 1],
                    ALU.mult, ALU.add),
                    reads=[ekv.ap()[:, 0:1], eu_c, Acol.ap()[:, co:co + 1]],
                    writes=[num.ap()[:, 0:1]])
                a.op("DVE", lambda e: e.scalar_tensor_tensor(
                    den.ap()[:, 1:CH], ek.ap()[:, 1:CH], eu_c, Bar.ap()[:, 0:CH - 1],
                    ALU.mult, ALU.add),
                    reads=[ek.ap()[:, 1:CH], eu_c, Bar.ap()[:, 0:CH - 1]],
                    writes=[den.ap()[:, 1:CH]])
                a.op("DVE", lambda e, co=co: e.scalar_tensor_tensor(
                    den.ap()[:, 0:1], ek.ap()[:, 0:1], eu_c, Bcol.ap()[:, co:co + 1],
                    ALU.mult, ALU.add),
                    reads=[ek.ap()[:, 0:1], eu_c, Bcol.ap()[:, co:co + 1]],
                    writes=[den.ap()[:, 0:1]])
                a.op("DVE", lambda e: e.reciprocal(den.ap()[:, :], den.ap()[:, :]),
                    reads=[den.ap()[:, :]], writes=[den.ap()[:, :]])
                a.op("DVE", lambda e: e.tensor_mul(
                    num.ap()[:, :], num.ap()[:, :], den.ap()[:, :]),
                    reads=[num.ap()[:, :], den.ap()[:, :]], writes=[num.ap()[:, :]])
                o = rwkvT.ap()[:, co * CH:(co + 1) * CH]
                a.op("POOL", lambda e, o=o, num=num, sigr=sigr: e.tensor_mul(
                    o, num.ap()[:, :], sigr.ap()[:, :]),
                     reads=[num.ap()[:, :], sigr.ap()[:, :]], writes=[o])
            msk = (ch == 2)
            for colbuf, arr in [(Acol, Aar), (Bcol, Bar)]:
                dst = colbuf.ap()[:, co:co + 1]
                src = arr.ap()[:, CH - 1:CH]
                if msk:
                    a.op("DVE", lambda e, d=dst, s=src: e.tensor_scalar_mul(
                        d, s, mask8.ap()[:, 0:1]),
                        reads=[src, mask8.ap()[:, 0:1]], writes=[dst])
                else:
                    a.op("DVE", lambda e, d=dst, s=src: e.tensor_copy(d, s),
                         reads=[src], writes=[dst])
        if full:
            for tt in range(NT):
                pso = ps_o[tt % 2]
                for half in range(2):
                    for ci in range(8):
                        a.mm(pso.ap()[:, half * 512:(half + 1) * 512],
                             rwkvT.ap()[:, ci * CH + tt * 128: ci * CH + (tt + 1) * 128],
                             rhs_slab(wob, ci, half * 512, (half + 1) * 512),
                             ci == 0, ci == 7)
                o = xn_tiles[tt]
                a.op("DVE", lambda e, o=o, pso=pso: e.tensor_add(
                    o, o, pso.ap()[:, :]),
                    reads=[o, pso.ap()[:, :]], writes=[o])
                a.dma("POOL", xi_out[(ch - 1) * CH + tt * 128:(ch - 1) * CH + (tt + 1) * 128, :], o)

    if ctx is None:
        a.finish()
    return nc


def _coltiles(v):
    return np.ascontiguousarray(np.asarray(v, np.float32).reshape(8, 128).T)


def _shard_x(x):
    shards = []
    for core in range(N_CORES):
        b, q = core // 4, core % 4
        start = q * P_TOK
        lo = start - (HW + HG)
        if lo < 0:
            xe = np.concatenate([np.zeros((-lo, C), np.float32),
                                 x[b, 0:start + P_TOK]], axis=0)
        else:
            xe = np.ascontiguousarray(x[b, lo:start + P_TOK])
        shards.append(xe)
    return shards


def _run_ph1(inp, nchunks=NCH):
    x = np.asarray(inp["L0_attn_x"], np.float32)
    shards = _shard_x(x)
    nc = build_ph1(nchunks)
    base = {
        "Wk": np.asarray(inp["L0_attn_Wk"], np.float32),
        "Wv": np.asarray(inp["L0_attn_Wv"], np.float32),
        "Wr": np.asarray(inp["L0_attn_Wr"], np.float32),
        "Wo": np.asarray(inp["L0_attn_Wo"], np.float32),
        "ln0g": np.asarray(inp["L0_attn_ln0_g"], np.float32).reshape(1, C),
        "ln0b": np.asarray(inp["L0_attn_ln0_b"], np.float32).reshape(1, C),
        "tmk": _coltiles(inp["L0_attn_tm_k"]),
        "tmv": _coltiles(inp["L0_attn_tm_v"]),
        "tmr": _coltiles(inp["L0_attn_tm_r"]),
        "td": _coltiles(inp["L0_attn_time_decay"]),
        "tf": _coltiles(inp["L0_attn_time_first"]),
    }
    in_maps = []
    for core in range(N_CORES):
        m = dict(base)
        m["x_ext"] = shards[core][:nchunks * CH]
        m["mask"] = np.full((128, 1), 0.0 if core % 4 == 0 else 1.0, np.float32)
        in_maps.append(m)
    res = run_bass_kernel_spmd(nc, in_maps, core_ids=list(range(N_CORES)))
    return res.results




_SCRATCH = {"x_inner", "xn_s", "y2", "x1", "attn1"}


def build_fused():
    nc = bass.Bass("TRN2", target_bir_lowering=False)
    a = ASB(nc)
    drams = {}

    def mk_dram(name, shape, dt, kind):
        if name in drams:
            return drams[name]
        k = "Internal" if name in _SCRATCH else kind
        ap = nc.dram_tensor(name, shape, dt, kind=k).ap()
        drams[name] = ap
        return ap

    arena_bytes = (nc.sbuf_bytes_remaining - 512) // 64 * 64
    arena = nc.alloc_sbuf_tensor("arena", [128, arena_bytes // 4], F32)
    ps_arena = nc.alloc_psum_tensor("ps_arena", [128, 4096], F32)
    st = {"off": 0, "ps": 0}

    def sb_raw(name, shape, dt=F32):
        esz = _dtsize(dt)
        off = (st["off"] + 63) // 64 * 64
        st["off"] = off + shape[1] * esz
        assert st["off"] <= arena_bytes, (name, st["off"], arena_bytes)
        v = arena.ap().bitcast(dt)[0:shape[0], off // esz: off // esz + shape[1]]
        return _ApTensor(v)

    def ps_raw(name, shape, dt=F32):
        assert dt is F32
        off = (st["ps"] + 2047) // 2048 * 2048
        st["ps"] = off + shape[1] * 4
        assert st["ps"] <= 16384, (name, st["ps"])
        v = ps_arena.ap()[0:shape[0], off // 4: off // 4 + shape[1]]
        return _ApTensor(v)

    ctx = (nc, a, mk_dram, sb_raw, ps_raw)
    for build in (build_ph1, build_ph2, build_ph3, build_ph4, build_ph5):
        st["off"] = 0
        st["ps"] = 0
        build(ctx=ctx)
    a.finish()
    return nc


_FUSED_NC = None


def kernel(**inp):
    global _FUSED_NC
    if _FUSED_NC is None:
        _FUSED_NC = build_fused()
    nc = _FUSED_NC
    x = np.asarray(inp["L0_attn_x"], np.float32)
    shards = _shard_x(x)
    base = {
        "Wk": np.asarray(inp["L0_attn_Wk"], np.float32),
        "Wv": np.asarray(inp["L0_attn_Wv"], np.float32),
        "Wr": np.asarray(inp["L0_attn_Wr"], np.float32),
        "Wo": np.asarray(inp["L0_attn_Wo"], np.float32),
        "ln0g": np.asarray(inp["L0_attn_ln0_g"], np.float32).reshape(1, C),
        "ln0b": np.asarray(inp["L0_attn_ln0_b"], np.float32).reshape(1, C),
        "tmk": _coltiles(inp["L0_attn_tm_k"]),
        "tmv": _coltiles(inp["L0_attn_tm_v"]),
        "tmr": _coltiles(inp["L0_attn_tm_r"]),
        "td": _coltiles(inp["L0_attn_time_decay"]),
        "tf": _coltiles(inp["L0_attn_time_first"]),
        "Wck": np.asarray(inp["L0_attn_Wck"], np.float32),
        "Wcv": np.asarray(inp["L0_attn_Wcv"], np.float32),
        "Wcr": np.asarray(inp["L0_attn_Wcr"], np.float32),
        "ln1g": np.asarray(inp["L0_attn_ln1_g"], np.float32).reshape(1, C),
        "ln1b": np.asarray(inp["L0_attn_ln1_b"], np.float32).reshape(1, C),
        "cmk": _coltiles(inp["L0_attn_cm_k"]),
        "cmr": _coltiles(inp["L0_attn_cm_r"]),
        "w_gate": np.asarray(inp["L0_ffn_w_gate"], np.float32),
        "w_up": np.asarray(inp["L0_ffn_w_up"], np.float32),
        "w_down": np.asarray(inp["L0_ffn_w_down"], np.float32),
        "Wq": np.asarray(inp["L1_attn_Wq"], np.float32),
        "Wk1": np.asarray(inp["L1_attn_Wk"], np.float32),
        "Wv1": np.asarray(inp["L1_attn_Wv"], np.float32),
        "Wo1": np.asarray(inp["L1_attn_Wo"], np.float32),
        "Wf1": np.asarray(inp["L1_attn_Wf1"], np.float32),
        "Wf2": np.asarray(inp["L1_attn_Wf2"], np.float32),
        "Wg1": np.asarray(inp["L1_attn_Wg1"], np.float32),
        "Wg2": np.asarray(inp["L1_attn_Wg2"], np.float32),
        "nw": _coltiles(inp["L1_attn_norm_w"]),
        "wgT": np.ascontiguousarray(np.asarray(inp["L1_ffn_w_gate"], np.int32).T),
        "wuT": np.ascontiguousarray(np.asarray(inp["L1_ffn_w_up"], np.int32).T),
        "wdT": np.ascontiguousarray(np.asarray(inp["L1_ffn_w_down"], np.int32).T),
    }
    in_maps = []
    for core in range(N_CORES):
        m = dict(base)
        m["x_ext"] = shards[core]
        m["mask"] = np.full((128, 1), 0.0 if core % 4 == 0 else 1.0, np.float32)
        in_maps.append(m)
    res = run_bass_kernel_spmd(nc, in_maps, core_ids=list(range(N_CORES)))
    out = np.empty((B, T, C), np.float32)
    for core in range(N_CORES):
        b, q = core // 4, core % 4
        out[b, q * P_TOK:(q + 1) * P_TOK, :] = res.results[core]["x2"]
    return out


def kernel_unfused(**inp):
    res1 = _run_ph1(inp)
    xi = [res1[c]["x_inner"] for c in range(N_CORES)]
    xn = [res1[c]["xn_s"] for c in range(N_CORES)]
    res2 = _run_ph2(inp, xi)
    y2 = [res2[c]["y2"] for c in range(N_CORES)]
    res3 = _run_ph3(inp, xn, y2)
    x1 = [res3[c]["x1"] for c in range(N_CORES)]
    res4 = _run_ph4(inp, x1)
    at = [res4[c]["attn1"] for c in range(N_CORES)]
    res5 = _run_ph5(inp, x1, at)
    out = np.empty((B, T, C), np.float32)
    for core in range(N_CORES):
        b, q = core // 4, core % 4
        out[b, q * P_TOK:(q + 1) * P_TOK, :] = res5[core]["x2"]
    return out


if __name__ == "__main__":
    print("kernel module (dev)")


def build_ph2(nchunks=NCH, ctx=None):
    nc, a, mk_dram, sb_raw, ps_raw = ctx if ctx is not None else _own_ctx()
    n_in = (nchunks - 1) * CH
    n_out = (nchunks - 2) * CH
    xi_d = mk_dram("x_inner", [n_in, C], F32, "ExternalInput")
    Wck = mk_dram("Wck", [C, 4 * C], F32, "ExternalInput")
    Wcv = mk_dram("Wcv", [4 * C, C], F32, "ExternalInput")
    Wcr = mk_dram("Wcr", [C, C], F32, "ExternalInput")
    v_ln1g = mk_dram("ln1g", [1, C], F32, "ExternalInput")
    v_ln1b = mk_dram("ln1b", [1, C], F32, "ExternalInput")
    v_cmk = mk_dram("cmk", [128, 8], F32, "ExternalInput")
    v_cmr = mk_dram("cmr", [128, 8], F32, "ExternalInput")
    v_mask = mk_dram("mask", [128, 1], F32, "ExternalInput")
    y2_d = mk_dram("y2", [n_out, C], F32, "ExternalOutput")
    wcv_sp = mk_dram("wcv_sp", [32 * 128, C], BF16, "Internal")

    bufs = {}

    def sb(name, shape, dt=F32):
        t = sb_raw(name, shape, dt)
        bufs[name] = t
        return t

    wst = sb("wst", [128, 2 * 1024])
    wckb = load_grid(a, nc, "wck", Wck, 8, 4096, st=wst, alloc=sb_raw)
    wcrb = load_grid(a, nc, "wcr", Wcr, 8, 1024, st=wst, alloc=sb_raw)
    wcv_stage = sb("wcv_stage", [128, 2 * 1024], BF16)
    for fo in range(32):
        p = fo % 2
        a.dma("SP", wst.ap()[:, p * 1024:(p + 1) * 1024],
              Wcv[128 * fo:128 * (fo + 1), :])
        dst = wcv_stage.ap()[:, p * 1024:(p + 1) * 1024]
        a.op("DVE", lambda e, d=dst, p=p: e.tensor_copy(
            d, wst.ap()[:, p * 1024:(p + 1) * 1024]),
            reads=[wst.ap()[:, p * 1024:(p + 1) * 1024]], writes=[dst])
        a.dma("POOL", wcv_sp[128 * fo:128 * (fo + 1), :], dst)

    eps_col = sb("eps_col", [128, 1])
    a.op("DVE", lambda e: e.memset(eps_col.ap()[:, :], 1e-5), writes=[eps_col.ap()[:, :]])
    cmk8 = sb("cmk8", [128, 8]); cmr8 = sb("cmr8", [128, 8]); mask8 = sb("mask8", [128, 1])
    for t, srcd in [(cmk8, v_cmk), (cmr8, v_cmr), (mask8, v_mask)]:
        a.dma("SP", t.ap()[:, :], srcd)
    g_bc = sb("g_bc", [128, C]); b_bc = sb("b_bc", [128, C])
    a.dma("SP", g_bc.ap()[:, :], v_ln1g.partition_broadcast(128))
    a.dma("SP", b_bc.ap()[:, :], v_ln1b.partition_broadcast(128))

    NT = CH // 128
    xi_c = sb("xi_c", [128, NT * C])
    xc_t = sb("xc_t", [128, NT * C], BF16)
    xcT_c = sb("xcT_c", [128, 8 * CH], BF16)
    pcol = sb("pcol", [128, 8], BF16)
    a.op("DVE", lambda e: e.memset(pcol.ap()[:, :], 0.0), writes=[pcol.ap()[:, :]])
    t1 = sb("t1", [128, 8 * CH], BF16)
    mkT = sb("mkT", [128, 8 * CH], BF16)
    mrT = t1
    kk2 = sb("kk2", [128, 32 * CH], BF16)
    rkP = [sb(f"rk{i}", [128, CH], BF16) for i in range(2)]
    rk = rkP[0]
    cmr_s = sb("cmr_s", [128, 8 * CH])
    wcv_sl = [sb(f"wcv_sl{p}", [128, C], BF16) for p in range(2)]
    ident = sb("ident", [128, 128])
    a.op("DVE", lambda e: e.memset(ident.ap()[:, :], 1.0), writes=[ident.ap()[:, :]])
    a.op("POOL", lambda e: e.affine_select(
        ident.ap()[:, :], ident.ap()[:, :], [[1, 128]], ALU.is_equal, 0.0,
        base=0, channel_multiplier=-1),
        reads=[ident.ap()[:, :]], writes=[ident.ap()[:, :]])
    alloc_ln_bufs(a, nc, bufs, "lnC", NT, alloc=sb_raw)

    ps_big = ps_raw("ps_big", [128, C], F32)
    ps_kkP = [_ApTensor(ps_big.ap()[:, 0:CH]), _ApTensor(ps_big.ap()[:, CH:2 * CH])]
    ps_cm = ps_raw("ps_cm", [128, 4 * CH], F32)
    ps_rr = ps_raw("ps_rr", [128, CH], F32)
    ps_ct = ps_big

    for ch in range(1, nchunks):
        full = ch >= 2
        for tt in range(NT):
            a.dma("SP", xi_c.ap()[:, tt * C:(tt + 1) * C],
                  xi_d[(ch - 1) * CH + tt * 128:(ch - 1) * CH + (tt + 1) * 128, :])
        xi_tiles = [xi_c.ap()[:, tt * C:(tt + 1) * C] for tt in range(NT)]
        xc_tiles = [xc_t.ap()[:, tt * C:(tt + 1) * C] for tt in range(NT)]
        layernorm_tiles(a, nc, bufs, xi_tiles, xc_tiles, g_bc=g_bc.ap()[:, :],
                        b_bc=b_bc.ap()[:, :], tag="lnC")
        for tt in range(NT):
            for ci in range(8):
                a.dma("SP",
                      xcT_c.ap()[:, ci * CH + tt * 128: ci * CH + (tt + 1) * 128],
                      xc_t.ap()[:, tt * C + ci * 128: tt * C + (ci + 1) * 128],
                      transpose=True)
        if full:
            for ci in range(8):
                xa_a = xcT_c.ap()[:, ci * CH + 1: ci * CH + CH]
                xs_a = xcT_c.ap()[:, ci * CH: ci * CH + CH - 1]
                xa_b = xcT_c.ap()[:, ci * CH: ci * CH + 1]
                xs_b = pcol.ap()[:, ci:ci + 1]
                t1_a = t1.ap()[:, ci * CH + 1:(ci + 1) * CH]
                t1_b = t1.ap()[:, ci * CH: ci * CH + 1]
                a.op("POOL", lambda e, o=t1_a, i0=xa_a, i1=xs_a: e.tensor_sub(o, i0, i1),
                     reads=[xa_a, xs_a], writes=[t1_a])
                a.op("DVE", lambda e, o=t1_b, i0=xa_b, i1=xs_b: e.tensor_sub(o, i0, i1),
                     reads=[xa_b, xs_b], writes=[t1_b])
                for arr, vcol in [(mkT, cmk8), (mrT, cmr8)]:
                    o_a = arr.ap()[:, ci * CH + 1:(ci + 1) * CH]
                    o_b = arr.ap()[:, ci * CH: ci * CH + 1]
                    a.op("DVE", lambda e, o=o_a, t=t1_a, v=vcol, xs=xs_a, ci=ci: e.scalar_tensor_tensor(
                        o, t, v.ap()[:, ci:ci + 1], xs, ALU.mult, ALU.add),
                        reads=[t1_a, vcol.ap()[:, ci:ci + 1], xs_a], writes=[o_a])
                    a.op("DVE", lambda e, o=o_b, t=t1_b, v=vcol, xs=xs_b, ci=ci: e.scalar_tensor_tensor(
                        o, t, v.ap()[:, ci:ci + 1], xs, ALU.mult, ALU.add),
                        reads=[t1_b, vcol.ap()[:, ci:ci + 1], xs_b], writes=[o_b])
        for ci in range(8):
            dst = pcol.ap()[:, ci:ci + 1]
            srcc = xcT_c.ap()[:, ci * CH + CH - 1: ci * CH + CH]
            if ch == 2:
                a.op("DVE", lambda e, d=dst, s=srcc: e.tensor_scalar_mul(
                    d, s, mask8.ap()[:, 0:1]),
                    reads=[srcc, mask8.ap()[:, 0:1]], writes=[dst])
            else:
                a.op("DVE", lambda e, d=dst, s=srcc: e.tensor_copy(d, s),
                     reads=[srcc], writes=[dst])
        if not full:
            continue
        for fo in range(32):
            ps_kk = ps_kkP[fo % 2]
            rk = rkP[fo % 2]
            for ci in range(8):
                a.mm(ps_kk.ap()[:, :], lhsT_tile(wckb, ci, fo, 4096),
                     mkT.ap()[:, ci * CH:(ci + 1) * CH], ci == 0, ci == 7)
            a.op("ACT", lambda e, ps_kk=ps_kk, rk=rk: e.activation(
                rk.ap()[:, :], ps_kk.ap()[:, :], AF.Relu),
                reads=[ps_kk.ap()[:, :]], writes=[rk.ap()[:, :]])
            o = kk2.ap()[:, fo * CH:(fo + 1) * CH]
            a.op("DVE", lambda e, o=o, rk=rk: e.tensor_mul(o, rk.ap()[:, :], rk.ap()[:, :]),
                 reads=[rk.ap()[:, :]], writes=[o])
        for cohalf in range(2):
            for fo in range(32):
                slp = wcv_sl[fo % 2]
                a.dma("SP", slp.ap()[:, :], wcv_sp[128 * fo:128 * (fo + 1), :])
                for cq in range(4):
                    co = cohalf * 4 + cq
                    a.mm(ps_cm.ap()[:, cq * CH:(cq + 1) * CH],
                         slp.ap()[:, co * 128:(co + 1) * 128],
                         kk2.ap()[:, fo * CH:(fo + 1) * CH],
                         fo == 0, fo == 31)
            for cq in range(4):
                co = cohalf * 4 + cq
                rk = rkP[cq % 2]
                for ci in range(8):
                    a.mm(ps_rr.ap()[:, :], lhsT_tile(wcrb, ci, co),
                         mrT.ap()[:, ci * CH:(ci + 1) * CH], ci == 0, ci == 7)
                a.op("ACT", lambda e, rk=rk: e.activation(
                    rk.ap()[:, :], ps_rr.ap()[:, :], AF.Sigmoid),
                     reads=[ps_rr.ap()[:, :]], writes=[rk.ap()[:, :]])
                dstc = cmr_s.ap()[:, co * CH:(co + 1) * CH]
                a.op("DVE", lambda e, cq=cq, dstc=dstc, rk=rk: e.tensor_mul(
                    dstc, rk.ap()[:, :], ps_cm.ap()[:, cq * CH:(cq + 1) * CH]),
                    reads=[rk.ap()[:, :], ps_cm.ap()[:, cq * CH:(cq + 1) * CH]],
                    writes=[dstc])
        for tt in range(NT):
            for co in range(8):
                srcp = cmr_s.ap()[:, co * CH + tt * 128: co * CH + (tt + 1) * 128]
                a.op("PE", lambda e, co=co, srcp=srcp: e.transpose(
                    ps_ct.ap()[:, co * 128:(co + 1) * 128], srcp, ident.ap()[:, :]),
                    reads=[srcp, ident.ap()[:, :]],
                    writes=[ps_ct.ap()[:, co * 128:(co + 1) * 128]])
            o = xi_tiles[tt]
            a.op("DVE", lambda e, o=o: e.tensor_add(o, o, ps_ct.ap()[:, :]),
                 reads=[o, ps_ct.ap()[:, :]], writes=[o])
            a.dma("POOL", y2_d[(ch - 2) * CH + tt * 128:(ch - 2) * CH + (tt + 1) * 128, :], o)

    if ctx is None:
        a.finish()
    return nc


def _run_ph2(inp, x_inner, nchunks=NCH):
    nc = build_ph2(nchunks)
    base = {
        "Wck": np.asarray(inp["L0_attn_Wck"], np.float32),
        "Wcv": np.asarray(inp["L0_attn_Wcv"], np.float32),
        "Wcr": np.asarray(inp["L0_attn_Wcr"], np.float32),
        "ln1g": np.asarray(inp["L0_attn_ln1_g"], np.float32).reshape(1, C),
        "ln1b": np.asarray(inp["L0_attn_ln1_b"], np.float32).reshape(1, C),
        "cmk": _coltiles(inp["L0_attn_cm_k"]),
        "cmr": _coltiles(inp["L0_attn_cm_r"]),
    }
    in_maps = []
    for core in range(N_CORES):
        m = dict(base)
        m["x_inner"] = x_inner[core]
        m["mask"] = np.full((128, 1), 0.0 if core % 4 == 0 else 1.0, np.float32)
        in_maps.append(m)
    res = run_bass_kernel_spmd(nc, in_maps, core_ids=list(range(N_CORES)))
    return res.results


CBF = 192.0


def emit_fp4_quant(a, nc, bufs, src_ap, dst_ap, W, ws=""):
    G = W // 16
    w0 = bufs["qw0" + ws].ap()[:, :W]
    w3 = bufs["qw3" + ws].ap()[:, :W]
    w1 = bufs["qw1" + ws].ap()[:, :W]
    w2 = bufs["qw2" + ws].ap()[:, :W]
    sc = bufs["qsc" + ws].ap()[:, :G]; rsc = bufs["qrsc" + ws].ap()[:, :G]
    src3 = src_ap.rearrange("p (g b) -> p g b", b=16)
    a.op("DVE", lambda e: e.tensor_reduce(sc.unsqueeze(-1), src3, AX.X, ALU.max,
                                          apply_absolute_value=True),
         reads=[src_ap], writes=[sc])
    a.op("DVE", lambda e: e.tensor_scalar(sc, sc, 1.0 / 6.0, 1e-12, ALU.mult, ALU.max),
         reads=[sc], writes=[sc])
    a.op("DVE", lambda e: e.reciprocal(rsc, sc), reads=[sc], writes=[rsc])
    a.op("DVE", lambda e: e.tensor_tensor(
        w0.rearrange("p (g b) -> p g b", b=16), src3,
        rsc.unsqueeze(-1).broadcast_to((128, G, 16)), ALU.mult),
        reads=[src_ap, rsc], writes=[w0])
    a.op("ACT", lambda e: e.activation(w1, w0, AF.Abs), reads=[w0], writes=[w1])
    a.op("DVE", lambda e: e.tensor_single_scalar(w2, w1, 2.0, ALU.is_ge),
         reads=[w1], writes=[w2])
    a.op("DVE", lambda e: e.tensor_single_scalar(w1, w1, 4.0, ALU.is_ge),
         reads=[w1], writes=[w1])
    a.op("DVE", lambda e: e.tensor_scalar(w3, w1, -0.5, 2.0, ALU.mult, ALU.add),
         reads=[w1], writes=[w3])
    a.op("DVE", lambda e: e.scalar_tensor_tensor(w3, w2, -1.0, w3, ALU.mult, ALU.add),
         reads=[w2, w3], writes=[w3])
    a.op("DVE", lambda e: e.tensor_mul(w0, w0, w3), reads=[w0, w3], writes=[w0])
    a.op("DVE", lambda e: e.scalar_tensor_tensor(w2, w2, 0.5, w1, ALU.mult, ALU.add),
         reads=[w2, w1], writes=[w2])
    a.op("ACT", lambda e: e.activation(w3, w0, AF.Copy, bias=RND_C),
         reads=[w0], writes=[w3])
    a.op("ACT", lambda e: e.activation(w3, w3, AF.Copy, bias=-RND_C),
         reads=[w3], writes=[w3])
    a.op("DVE", lambda e: e.tensor_mul(w2, w3, w2), reads=[w3, w2], writes=[w2])
    a.op("DVE", lambda e: e.scalar_tensor_tensor(w2, w3, 0.5, w2, ALU.mult, ALU.add),
         reads=[w3, w2], writes=[w2])
    a.op("DVE", lambda e: e.tensor_tensor(
        dst_ap.rearrange("p (g b) -> p g b", b=16),
        w2.rearrange("p (g b) -> p g b", b=16),
        sc.unsqueeze(-1).broadcast_to((128, G, 16)), ALU.mult),
        reads=[w2, sc], writes=[dst_ap])


def build_ph3(nchunks=NCH, ctx=None):
    nc, a, mk_dram, sb_raw, ps_raw = ctx if ctx is not None else _own_ctx()
    n_io = (nchunks - 2) * CH
    ntok = nchunks * CH
    xn_d = mk_dram("xn_s", [n_io, C], F32, "ExternalInput")
    x_d = mk_dram("x_ext", [ntok, C], F32, "ExternalInput")
    y2_d = mk_dram("y2", [n_io, C], F32, "ExternalInput")
    wg_d = mk_dram("w_gate", [2816, C], F32, "ExternalInput")
    wu_d = mk_dram("w_up", [2816, C], F32, "ExternalInput")
    wd_d = mk_dram("w_down", [C, 2816], F32, "ExternalInput")
    x1_d = mk_dram("x1", [n_io, C], F32, "ExternalOutput")
    wgT_sp = mk_dram("wgT_sp", [8 * 128, 2816], BF16, "Internal")
    wuT_sp = mk_dram("wuT_sp", [8 * 128, 2816], BF16, "Internal")

    bufs = {}

    def sb(name, shape, dt=F32):
        t = sb_raw(name, shape, dt)
        bufs[name] = t
        return t

    H3 = 2816
    for ws in ("", "B"):
        for nm, wdt in [("qw0", F32), ("qw1", BF16), ("qw2", BF16), ("qw3", F32)]:
            bufs[nm + ws] = sb_raw(nm + ws, [128, H3], wdt)
        bufs["qsc" + ws] = sb_raw("qsc" + ws, [128, H3 // 16], F32)
        bufs["qrsc" + ws] = sb_raw("qrsc" + ws, [128, H3 // 16], F32)
    wstP = [sb(f"wst{i}", [128, 2816]) for i in range(2)]
    wqsP = [sb(f"wq_s{i}", [128, 2816], BF16) for i in range(2)]

    for wi, (wd_ap, sp) in enumerate([(wg_d, wgT_sp), (wu_d, wuT_sp)]):
        for hi in range(22):
            par = (wi * 22 + hi) % 2
            ws = "" if par == 0 else "B"
            wst = wstP[par]; wq_s = wqsP[par]
            a.dma("SP", wst.ap()[:, 0:1024], wd_ap[128 * hi:128 * (hi + 1), :])
            emit_fp4_quant(a, nc, bufs, wst.ap()[:, 0:1024], wq_s.ap()[:, 0:1024], 1024, ws=ws)
            for ci in range(8):
                a.dma("SP", bufs["qw1" + ws].ap()[:, ci * 128:(ci + 1) * 128],
                      wq_s.ap()[:, ci * 128:(ci + 1) * 128], transpose=True)
            for ci in range(8):
                a.dma("POOL", sp[ci * 128:(ci + 1) * 128, hi * 128:(hi + 1) * 128],
                      bufs["qw1" + ws].ap()[:, ci * 128:(ci + 1) * 128])
    wdT_sp = nc.dram_tensor("wdT_sp", [22 * 128, C], BF16, kind="Internal").ap()
    for ci in range(8):
        par = ci % 2
        ws = "" if par == 0 else "B"
        wst = wstP[par]; wq_s = wqsP[par]
        a.dma("SP", wst.ap()[:, :], wd_d[128 * ci:128 * (ci + 1), :])
        emit_fp4_quant(a, nc, bufs, wst.ap()[:, :], wq_s.ap()[:, :], 2816, ws=ws)
        for hi in range(22):
            stg = bufs["qw1" + ws].ap()[:, hi * 128:(hi + 1) * 128]
            a.dma("SP", stg, wq_s.ap()[:, hi * 128:(hi + 1) * 128], transpose=True)
            a.dma("POOL", wdT_sp[hi * 128:(hi + 1) * 128, ci * 128:(ci + 1) * 128], stg)
    wdslP = [sb(f"wdsl{i}", [128, C], BF16) for i in range(2)]

    NT = CH // 128
    xn_c = sb("xn_c", [128, NT * C])
    xq = sb("xq", [128, NT * C], BF16)
    xqT = sb("xqT", [128, 8 * CH], BF16)
    gsP = [sb(f"gs{i}", [128, H3], BF16) for i in range(2)]
    hhP = [sb(f"hh{i}", [128, H3], BF16) for i in range(2)]
    hqP = [sb(f"hq{i}", [128, H3], BF16) for i in range(2)]
    hqTP = [sb(f"hqT{i}", [128, 22 * 128], BF16) for i in range(2)]
    resP = [sb(f"res{i}", [128, C]) for i in range(2)]
    wsl = [sb(f"wsl{p}", [128, H3], BF16) for p in range(2)]

    ps_g = ps_raw("ps_g", [128, H3], F32)
    ps_d = ps_raw("ps_d", [128, C], F32)

    HGRP = [(0, 512), (512, 1024), (1024, 1536), (1536, 2048), (2048, 2560), (2560, 2816)]

    for ch in range(2, nchunks):
        for tt in range(NT):
            a.dma("SP", xn_c.ap()[:, tt * C:(tt + 1) * C],
                  xn_d[(ch - 2) * CH + tt * 128:(ch - 2) * CH + (tt + 1) * 128, :])
        for tt in range(NT):
            emit_fp4_quant(a, nc, bufs, xn_c.ap()[:, tt * C:(tt + 1) * C],
                           xq.ap()[:, tt * C:(tt + 1) * C], C,
                           ws=("" if tt % 2 == 0 else "B"))
        for tt in range(NT):
            for ci in range(8):
                a.dma("SP", xqT.ap()[:, ci * CH + tt * 128: ci * CH + (tt + 1) * 128],
                      xq.ap()[:, tt * C + ci * 128: tt * C + (ci + 1) * 128],
                      transpose=True)
        for tt in range(NT):
            gs = gsP[tt % 2]; hh = hhP[tt % 2]; hq = hqP[tt % 2]; hqT = hqTP[tt % 2]
            res_t = resP[tt % 2]
            for ci in range(8):
                slp = wsl[ci % 2]
                a.dma("SP", slp.ap()[:, :], wgT_sp[ci * 128:(ci + 1) * 128, :])
                lhs = xqT.ap()[:, ci * CH + tt * 128: ci * CH + (tt + 1) * 128]
                for (h0, h1) in HGRP:
                    a.mm(ps_g.ap()[:, h0:h1], lhs, slp.ap()[:, h0:h1], ci == 0, ci == 7)
            for (h0, h1) in HGRP:
                a.op("ACT", lambda e, h0=h0, h1=h1: e.activation(
                    gs.ap()[:, h0:h1], ps_g.ap()[:, h0:h1], AF.Silu),
                    reads=[ps_g.ap()[:, h0:h1]], writes=[gs.ap()[:, h0:h1]])
            for ci in range(8):
                slp = wsl[ci % 2]
                a.dma("SP", slp.ap()[:, :], wuT_sp[ci * 128:(ci + 1) * 128, :])
                lhs = xqT.ap()[:, ci * CH + tt * 128: ci * CH + (tt + 1) * 128]
                for (h0, h1) in HGRP:
                    a.mm(ps_g.ap()[:, h0:h1], lhs, slp.ap()[:, h0:h1], ci == 0, ci == 7)
            for (h0, h1) in HGRP:
                a.op("DVE", lambda e, h0=h0, h1=h1: e.tensor_tensor(
                    hh.ap()[:, h0:h1], gs.ap()[:, h0:h1], ps_g.ap()[:, h0:h1], ALU.mult),
                    reads=[gs.ap()[:, h0:h1], ps_g.ap()[:, h0:h1]], writes=[hh.ap()[:, h0:h1]])
            emit_fp4_quant(a, nc, bufs, hh.ap()[:, :], hq.ap()[:, :], H3,
                           ws=("" if tt % 2 == 0 else "B"))
            for hi in range(22):
                a.dma("SP", hqT.ap()[:, hi * 128:(hi + 1) * 128],
                      hq.ap()[:, hi * 128:(hi + 1) * 128], transpose=True)
            for hi in range(22):
                slp = wdslP[hi % 2]
                a.dma("SP", slp.ap()[:, :], wdT_sp[hi * 128:(hi + 1) * 128, :])
                lhs = hqT.ap()[:, hi * 128:(hi + 1) * 128]
                a.mm(ps_d.ap()[:, 0:512], lhs, slp.ap()[:, 0:512], hi == 0, hi == 21)
                a.mm(ps_d.ap()[:, 512:1024], lhs, slp.ap()[:, 512:1024], hi == 0, hi == 21)
            row0 = (ch - 2) * CH + tt * 128
            a.dma("SP", res_t.ap()[:, :], y2_d[row0:row0 + 128, :])
            a.op("DVE", lambda e: e.tensor_add(res_t.ap()[:, :], res_t.ap()[:, :], ps_d.ap()[:, :]),
                 reads=[res_t.ap()[:, :], ps_d.ap()[:, :]], writes=[res_t.ap()[:, :]])
            a.dma("SP", xn_c.ap()[:, tt * C:(tt + 1) * C],
                  x_d[ch * CH + tt * 128: ch * CH + (tt + 1) * 128, :])
            a.op("DVE", lambda e, tt=tt: e.tensor_add(
                res_t.ap()[:, :], res_t.ap()[:, :], xn_c.ap()[:, tt * C:(tt + 1) * C]),
                reads=[res_t.ap()[:, :], xn_c.ap()[:, tt * C:(tt + 1) * C]],
                writes=[res_t.ap()[:, :]])
            a.dma("POOL", x1_d[row0:row0 + 128, :], res_t.ap()[:, :])

    if ctx is None:
        a.finish()
    return nc


def _run_ph3(inp, xn_s, y2, nchunks=NCH):
    x = np.asarray(inp["L0_attn_x"], np.float32)
    shards = _shard_x(x)
    nc = build_ph3(nchunks)
    base = {
        "w_gate": np.asarray(inp["L0_ffn_w_gate"], np.float32),
        "w_up": np.asarray(inp["L0_ffn_w_up"], np.float32),
        "w_down": np.asarray(inp["L0_ffn_w_down"], np.float32),
    }
    in_maps = []
    for core in range(N_CORES):
        m = dict(base)
        m["x_ext"] = shards[core][:nchunks * CH]
        m["xn_s"] = xn_s[core]
        m["y2"] = y2[core]
        in_maps.append(m)
    res = run_bass_kernel_spmd(nc, in_maps, core_ids=list(range(N_CORES)))
    return res.results


def build_ph4(nsc=9, ctx=None):
    nc, a, mk_dram, sb_raw, ps_raw = ctx if ctx is not None else _own_ctx()
    n_in = nsc * CH
    n_out = (nsc - 1) * CH
    H, M, DK = 8, 64, 128
    x1_d = mk_dram("x1", [n_in, C], F32, "ExternalInput")
    Wq_d = mk_dram("Wq", [C, C], F32, "ExternalInput")
    Wk_d = mk_dram("Wk1", [C, C], F32, "ExternalInput")
    Wv_d = mk_dram("Wv1", [C, C], F32, "ExternalInput")
    Wo_d = mk_dram("Wo1", [C, C], F32, "ExternalInput")
    Wf1_d = mk_dram("Wf1", [C, 128], F32, "ExternalInput")
    Wf2_d = mk_dram("Wf2", [128, 512], F32, "ExternalInput")
    Wg1_d = mk_dram("Wg1", [C, 128], F32, "ExternalInput")
    Wg2_d = mk_dram("Wg2", [128, C], F32, "ExternalInput")
    v_nw = mk_dram("nw", [128, 8], F32, "ExternalInput")
    v_mask = mk_dram("mask", [128, 1], F32, "ExternalInput")
    at_d = mk_dram("attn1", [n_out, C], F32, "ExternalOutput")

    bufs = {}

    def sb(name, shape, dt=F32):
        t = sb_raw(name, shape, dt)
        bufs[name] = t
        return t

    wst = sb("wst", [128, 2 * 1024])
    wqb = load_grid(a, nc, "wq", Wq_d, 8, 1024, st=wst, alloc=sb_raw)
    wkb = load_grid(a, nc, "wk1", Wk_d, 8, 1024, st=wst, alloc=sb_raw)
    wvb = load_grid(a, nc, "wv1", Wv_d, 8, 1024, st=wst, alloc=sb_raw)
    wob = load_grid(a, nc, "wo1", Wo_d, 8, 1024, st=wst, alloc=sb_raw)
    wf1b = load_grid(a, nc, "wf1", Wf1_d, 8, 128, st=None, alloc=sb_raw)
    wg1b = load_grid(a, nc, "wg1", Wg1_d, 8, 128, st=None, alloc=sb_raw)
    wf2b = sb("wf2b", [128, 512], BF16)
    wg2b = sb("wg2b", [128, C], BF16)
    a.dma("SP", wst.ap()[:, 0:512], Wf2_d)
    a.op("DVE", lambda e: e.tensor_copy(wf2b.ap()[:, :], wst.ap()[:, 0:512]),
         reads=[wst.ap()[:, 0:512]], writes=[wf2b.ap()[:, :]])
    a.dma("SP", wst.ap()[:, 0:1024], Wg2_d)
    a.op("DVE", lambda e: e.tensor_copy(wg2b.ap()[:, :], wst.ap()[:, 0:1024]),
         reads=[wst.ap()[:, 0:1024]], writes=[wg2b.ap()[:, :]])
    nw8 = sb("nw8", [128, 8]); mask8 = sb("mask8", [128, 1])
    a.dma("SP", nw8.ap()[:, :], v_nw)
    a.dma("SP", mask8.ap()[:, :], v_mask)

    maskT = sb("maskT", [128, 128], BF16)
    a.op("DVE", lambda e: e.memset(maskT.ap()[:, :], 1.0), writes=[maskT.ap()[:, :]])
    a.op("POOL", lambda e: e.affine_select(
        maskT.ap()[:, :], maskT.ap()[:, :], [[1, 128]], ALU.is_ge, 0.0,
        base=0, channel_multiplier=-1),
        reads=[maskT.ap()[:, :]], writes=[maskT.ap()[:, :]])
    maskT64 = sb("maskT64", [128, 128], BF16)
    triC = sb("triC", [128, 128], BF16)
    a.op("DVE", lambda e: e.tensor_scalar(maskT64.ap()[:, :], maskT.ap()[:, :],
                                          18446744073709551616.0, None, ALU.mult),
         reads=[maskT.ap()[:, :]], writes=[maskT64.ap()[:, :]])
    m63 = sb("m63", [128, 128], BF16)
    a.op("DVE", lambda e: e.memset(m63.ap()[0:64, :], 1.0), writes=[m63.ap()[0:64, :]])
    a.op("DVE", lambda e: e.memset(m63.ap()[64:128, :], 0.0), writes=[m63.ap()[64:128, :]])
    a.op("DVE", lambda e: e.tensor_sub(triC.ap()[:, :], maskT.ap()[:, :], m63.ap()[:, :]),
         reads=[maskT.ap()[:, :], m63.ap()[:, :]], writes=[triC.ap()[:, :]])
    onescA = sb("onescA", [128, 1], BF16)
    onescH = sb("onescH", [128, 1], BF16)
    a.op("DVE", lambda e: e.memset(onescA.ap()[:, :], 1.0), writes=[onescA.ap()[:, :]])
    a.op("DVE", lambda e: e.memset(onescH.ap()[0:64, :], 0.0), writes=[onescH.ap()[0:64, :]])
    a.op("DVE", lambda e: e.memset(onescH.ap()[64:128, :], 1.0), writes=[onescH.ap()[64:128, :]])
    onesf = sb("onesf", [128, 1])
    a.op("DVE", lambda e: e.memset(onesf.ap()[:, :], 1.0), writes=[onesf.ap()[:, :]])
    ones1 = sb("ones1", [1, 128], BF16)
    a.op("DVE", lambda e: e.memset(ones1.ap()[:, :], 1.0), writes=[ones1.ap()[:, :]])

    NT = CH // 128
    x1_c = sb("x1_c", [128, NT * C])
    xn1 = sb("xn1", [128, NT * C], BF16)
    xn1T = sb("xn1T", [128, 8 * CH], BF16)
    qTs = sb("qTs", [128, 8 * CH], BF16)
    kTs = sb("kTs", [128, 8 * CH], BF16)
    kNT = sb("kNT", [128, NT * C], BF16)
    vv = sb("vv", [128, NT * C], BF16)
    f1T = sb("f1T", [128, CH], BF16)
    g1T = sb("g1T", [128, CH], BF16)
    sT = sb("sT", [128, 4 * CH], BF16)
    logf = sb("logf", [128, NT * CH], BF16)
    s_tm = sb("s_tm", [128, NT * CH], BF16)
    sgT = sb("sgT", [128, 8 * CH], BF16)
    wkA = sb("wkA", [128, CH])
    wkB = sb("wkB", [128, CH])
    P_bf = sb("P_bf", [128, CH], BF16)
    eGc = sb("eGc", [128, CH], BF16)
    eGcN = sb("eGcN", [128, CH], BF16)
    eGcTN = sb("eGcTN", [128, 4 * 128], BF16)
    Bt = sb("Bt", [128, CH], BF16)
    AmT = sb("AmT", [128, 8 * 128], BF16)
    okf = sb("okf", [128, CH])
    mxs = sb("mxs", [128, 8]); nmx = sb("nmx", [128, 8])
    ssum = sb("ssum", [128, 8]); rsum = sb("rsum", [128, 8])
    qvU = sb("qvU", [128, CH])
    qvt = sb("qvt", [128, CH], BF16)
    qvT = sb("qvT", [128, 4 * 128], BF16)
    qhT = sb("qhT", [128, 4 * 128], BF16)
    BtT = sb("BtT", [128, 4 * 128], BF16)
    W2m = sb("W2m", [128, 8 * 128], BF16)
    og = sb("og", [128, 8 * 128])
    outN = sb("outN", [128, 8 * 128], BF16)
    att_t = sb("att_t", [128, C])
    eGLr = sb("eGLr", [1, 512], BF16)
    eGcLr = sb("eGcLr", [1, 512], BF16)
    PLb_s = sb("PLb_s", [128, CH], BF16)
    ELb_s = sb("ELb_s", [128, CH], BF16)
    ELcol = sb("ELcol", [128, 4])
    PLcol = sb("PLcol", [128, 4])
    hk = sb("hk", [128, 512], BF16)
    hv = sb("hv", [128, 4 * 128], BF16)
    a.op("DVE", lambda e: e.memset(hk.ap()[:, :], 0.0), writes=[hk.ap()[:, :]])
    a.op("DVE", lambda e: e.memset(hv.ap()[:, :], 0.0), writes=[hv.ap()[:, :]])
    bufs["junk_shared"] = wst
    alloc_ln_bufs(a, nc, bufs, "lnD", NT, alloc=sb_raw)
    eps_col = sb("eps_col", [128, 1])
    a.op("DVE", lambda e: e.memset(eps_col.ap()[:, :], 1e-5), writes=[eps_col.ap()[:, :]])

    ps1 = ps_raw("ps1", [128, CH], F32)
    ps2 = ps_raw("ps2", [128, CH], F32)
    ps3 = ps_raw("ps3", [128, 2 * CH], F32)
    psP = ps_raw("psP", [128, CH], F32)
    psV = ps_raw("psV", [128, 2 * CH], F32)
    psR = ps_raw("psR", [1, CH], F32)

    for sc in range(nsc):
        out_sc = sc >= 1
        for tt in range(NT):
            a.dma("SP", x1_c.ap()[:, tt * C:(tt + 1) * C],
                  x1_d[sc * CH + tt * 128: sc * CH + (tt + 1) * 128, :])
        x1_tiles = [x1_c.ap()[:, tt * C:(tt + 1) * C] for tt in range(NT)]
        xn_tiles = [xn1.ap()[:, tt * C:(tt + 1) * C] for tt in range(NT)]
        layernorm_tiles(a, nc, bufs, x1_tiles, xn_tiles, tag="lnD")
        for tt in range(NT):
            for ci in range(8):
                a.dma("SP", xn1T.ap()[:, ci * CH + tt * 128: ci * CH + (tt + 1) * 128],
                      xn1.ap()[:, tt * C + ci * 128: tt * C + (ci + 1) * 128],
                      transpose=True)
        for h in range(H):
            for ci in range(8):
                a.mm(psP.ap()[:, :], lhsT_tile(wqb, ci, h),
                     xn1T.ap()[:, ci * CH:(ci + 1) * CH], ci == 0, ci == 7)
            a.op("ACT", lambda e, h=h: e.activation(
                qTs.ap()[:, h * CH:(h + 1) * CH], psP.ap()[:, :], AF.Silu),
                reads=[psP.ap()[:, :]], writes=[qTs.ap()[:, h * CH:(h + 1) * CH]])
            for ci in range(8):
                a.mm(psP.ap()[:, :], lhsT_tile(wkb, ci, h),
                     xn1T.ap()[:, ci * CH:(ci + 1) * CH], ci == 0, ci == 7)
            a.op("ACT", lambda e, h=h: e.activation(
                kTs.ap()[:, h * CH:(h + 1) * CH], psP.ap()[:, :], AF.Silu),
                reads=[psP.ap()[:, :]], writes=[kTs.ap()[:, h * CH:(h + 1) * CH]])
        for h in range(H):
            for lc in range(NT):
                a.dma("SP", kNT.ap()[:, lc * C + h * 128: lc * C + (h + 1) * 128],
                      kTs.ap()[:, h * CH + lc * 128: h * CH + (lc + 1) * 128],
                      transpose=True)
        for tt in range(NT):
            for half in range(2):
                for ci in range(8):
                    a.mm(psV.ap()[:, half * CH:(half + 1) * CH],
                         xn1T.ap()[:, ci * CH + tt * 128: ci * CH + (tt + 1) * 128],
                         rhs_slab(wvb, ci, half * CH, (half + 1) * CH),
                         ci == 0, ci == 7)
            a.op("DVE", lambda e, tt=tt: e.tensor_copy(
                vv.ap()[:, tt * C:(tt + 1) * C], psV.ap()[:, :]),
                reads=[psV.ap()[:, :]], writes=[vv.ap()[:, tt * C:(tt + 1) * C]])
        for ci in range(8):
            a.mm(psP.ap()[:, :], lhsT_tile(wf1b, ci, 0, 128),
                 xn1T.ap()[:, ci * CH:(ci + 1) * CH], ci == 0, ci == 7)
        a.op("DVE", lambda e: e.tensor_copy(f1T.ap()[:, :], psP.ap()[:, :]),
             reads=[psP.ap()[:, :]], writes=[f1T.ap()[:, :]])
        for mt in range(4):
            a.mm(ps1.ap()[:, :], wf2b.ap()[:, mt * 128:(mt + 1) * 128], f1T.ap()[:, :],
                 True, True)
            a.op("ACT", lambda e, mt=mt: e.activation(
                sT.ap()[:, mt * CH:(mt + 1) * CH], ps1.ap()[:, :], AF.Sigmoid, scale=-1.0),
                reads=[ps1.ap()[:, :]], writes=[sT.ap()[:, mt * CH:(mt + 1) * CH]])
        for tt in range(NT):
            a.mm(ps1.ap()[:, :], f1T.ap()[:, tt * 128:(tt + 1) * 128], wf2b.ap()[:, :],
                 True, True)
            a.op("ACT", lambda e, tt=tt: e.activation(
                s_tm.ap()[:, tt * CH:(tt + 1) * CH], ps1.ap()[:, :], AF.Sigmoid, scale=-1.0),
                reads=[ps1.ap()[:, :]],
                writes=[s_tm.ap()[:, tt * CH:(tt + 1) * CH]])
            a.op("ACT", lambda e: e.activation(wkA.ap()[:, :], ps1.ap()[:, :], AF.Exp, scale=-1.0),
                 reads=[ps1.ap()[:, :]], writes=[wkA.ap()[:, :]])
            a.op("ACT", lambda e: e.activation(wkB.ap()[:, :], wkA.ap()[:, :], AF.Ln,
                                               bias=onesf.ap()[:, 0:1]),
                 reads=[wkA.ap()[:, :], onesf.ap()[:, 0:1]], writes=[wkB.ap()[:, :]])
            a.op("DVE", lambda e, tt=tt: e.tensor_scalar(
                logf.ap()[:, tt * CH:(tt + 1) * CH], wkB.ap()[:, :], -1.0, None, ALU.mult),
                reads=[wkB.ap()[:, :]], writes=[logf.ap()[:, tt * CH:(tt + 1) * CH]])
        for ci in range(8):
            a.mm(psP.ap()[:, :], lhsT_tile(wg1b, ci, 0, 128),
                 xn1T.ap()[:, ci * CH:(ci + 1) * CH], ci == 0, ci == 7)
        a.op("DVE", lambda e: e.tensor_copy(g1T.ap()[:, :], psP.ap()[:, :]),
             reads=[psP.ap()[:, :]], writes=[g1T.ap()[:, :]])
        if out_sc:
            for co in range(8):
                a.mm(ps1.ap()[:, :], wg2b.ap()[:, co * 128:(co + 1) * 128], g1T.ap()[:, :],
                     True, True)
                a.op("ACT", lambda e, co=co: e.activation(
                    sgT.ap()[:, co * CH:(co + 1) * CH], ps1.ap()[:, :], AF.Sigmoid),
                    reads=[ps1.ap()[:, :]], writes=[sgT.ap()[:, co * CH:(co + 1) * CH]])
        for lc in range(NT):
            lf = logf.ap()[:, lc * CH:(lc + 1) * CH]
            a.mm(ps1.ap()[:, :], maskT.ap()[:, :], lf, True, True)
            a.op("ACT", lambda e: e.activation(P_bf.ap()[:, :], ps1.ap()[:, :], AF.Exp),
                 reads=[ps1.ap()[:, :]], writes=[P_bf.ap()[:, :]])
            a.mm(ps2.ap()[:, :], triC.ap()[:, :], lf, True, True)
            a.op("ACT", lambda e: e.activation(eGc.ap()[:, :], ps2.ap()[:, :], AF.Exp),
                 reads=[ps2.ap()[:, :]], writes=[eGc.ap()[:, :]])
            a.op("ACT", lambda e: e.activation(eGcN.ap()[:, :], ps2.ap()[:, :], AF.Exp, scale=-1.0),
                 reads=[ps2.ap()[:, :]], writes=[eGcN.ap()[:, :]])
            for mt in range(4):
                a.mm(ps1.ap()[:, mt * 128:(mt + 1) * 128],
                     lf[:, mt * 128:(mt + 1) * 128], triC.ap()[:, :], True, True)
            a.op("ACT", lambda e: e.activation(eGcTN.ap()[:, :], ps1.ap()[:, :], AF.Exp, scale=-1.0),
                 reads=[ps1.ap()[:, :]], writes=[eGcTN.ap()[:, :]])
            for mt in range(4):
                a.op("ACT", lambda e, mt=mt: e.activation(
                    ELcol.ap()[:, mt:mt + 1], ps1.ap()[:, mt * 128 + 127: mt * 128 + 128],
                    AF.Exp),
                    reads=[ps1.ap()[:, mt * 128 + 127: mt * 128 + 128]],
                    writes=[ELcol.ap()[:, mt:mt + 1]])
            a.mm(psR.ap()[0:1, 0:CH], onescA.ap()[:, :], lf, True, True)
            a.op("ACT", lambda e: e.activation(eGLr.ap()[:, :], psR.ap()[0:1, 0:CH], AF.Exp),
                 reads=[psR.ap()[0:1, 0:CH]], writes=[eGLr.ap()[:, :]])
            a.mm(psR.ap()[0:1, 0:CH], onescH.ap()[:, :], lf, True, True)
            a.op("ACT", lambda e: e.activation(eGcLr.ap()[:, :], psR.ap()[0:1, 0:CH], AF.Exp),
                 reads=[psR.ap()[0:1, 0:CH]], writes=[eGcLr.ap()[:, :]])
            a.mm(ps2.ap()[:, :], ones1.ap()[:, :], eGLr.ap()[:, :], True, True)
            a.op("ACT", lambda e: e.copy(PLb_s.ap()[:, :], ps2.ap()[:, :]),
                 reads=[ps2.ap()[:, :]], writes=[PLb_s.ap()[:, :]])
            a.mm(ps2.ap()[:, :], ones1.ap()[:, :], eGcLr.ap()[:, :], True, True)
            a.op("ACT", lambda e: e.copy(ELb_s.ap()[:, :], ps2.ap()[:, :]),
                 reads=[ps2.ap()[:, :]], writes=[ELb_s.ap()[:, :]])
            for mt in range(4):
                a.mm(ps2.ap()[:, mt:mt + 1], lf[:, mt * 128:(mt + 1) * 128],
                     onescA.ap()[:, :], True, True)
            a.op("ACT", lambda e: e.activation(PLcol.ap()[:, :], ps2.ap()[:, 0:4], AF.Exp),
                 reads=[ps2.ap()[:, 0:4]], writes=[PLcol.ap()[:, :]])
            a.op("DVE", lambda e: e.tensor_mul(
                Bt.ap()[:, :], s_tm.ap()[:, lc * CH:(lc + 1) * CH], eGcN.ap()[:, :]),
                reads=[s_tm.ap()[:, lc * CH:(lc + 1) * CH], eGcN.ap()[:, :]],
                writes=[Bt.ap()[:, :]])
            for mt in range(4):
                a.op("DVE", lambda e, mt=mt: e.tensor_mul(
                    BtT.ap()[:, mt * 128:(mt + 1) * 128],
                    sT.ap()[:, mt * CH + lc * 128: mt * CH + (lc + 1) * 128],
                    eGcTN.ap()[:, mt * 128:(mt + 1) * 128]),
                    reads=[sT.ap()[:, mt * CH + lc * 128: mt * CH + (lc + 1) * 128],
                           eGcTN.ap()[:, mt * 128:(mt + 1) * 128]],
                    writes=[BtT.ap()[:, mt * 128:(mt + 1) * 128]])
            for h in range(H):
                a.mm(ps3.ap()[:, h * 128:(h + 1) * 128],
                     kTs.ap()[:, h * CH + lc * 128: h * CH + (lc + 1) * 128],
                     qTs.ap()[:, h * CH + lc * 128: h * CH + (lc + 1) * 128],
                     True, True)
            for h in range(H):
                a.op("DVE", lambda e, h=h: e.tensor_mul(
                    AmT.ap()[:, h * 128:(h + 1) * 128],
                    ps3.ap()[:, h * 128:(h + 1) * 128], maskT.ap()[:, :]),
                    reads=[ps3.ap()[:, h * 128:(h + 1) * 128], maskT.ap()[:, :]],
                    writes=[AmT.ap()[:, h * 128:(h + 1) * 128]])
            for h in range(H):
                a.mm(ps1.ap()[:, h * 64:(h + 1) * 64],
                     AmT.ap()[:, h * 128:(h + 1) * 128], Bt.ap()[:, h * 64:(h + 1) * 64],
                     True, True)
                a.mm(ps2.ap()[:, h * 64:(h + 1) * 64],
                     qTs.ap()[:, h * CH + lc * 128: h * CH + (lc + 1) * 128],
                     hk.ap()[:, h * 64:(h + 1) * 64], True, True)
            a.op("DVE", lambda e: e.tensor_mul(okf.ap()[:, :], ps1.ap()[:, :], eGc.ap()[:, :]),
                 reads=[ps1.ap()[:, :], eGc.ap()[:, :]], writes=[okf.ap()[:, :]])
            a.op("DVE", lambda e: e.tensor_mul(qvU.ap()[:, :], ps2.ap()[:, :], P_bf.ap()[:, :]),
                 reads=[ps2.ap()[:, :], P_bf.ap()[:, :]], writes=[qvU.ap()[:, :]])
            a.op("DVE", lambda e: e.tensor_add(okf.ap()[:, :], okf.ap()[:, :], qvU.ap()[:, :]),
                 reads=[okf.ap()[:, :], qvU.ap()[:, :]], writes=[okf.ap()[:, :]])
            a.op("DVE", lambda e: e.tensor_reduce(
                mxs.ap()[:, :].unsqueeze(-1), okf.ap()[:, :].rearrange("p (h m) -> p h m", m=64),
                AX.X, ALU.max),
                reads=[okf.ap()[:, :]], writes=[mxs.ap()[:, :]])
            a.op("DVE", lambda e: e.tensor_scalar(nmx.ap()[:, :], mxs.ap()[:, :], -1.0, None, ALU.mult),
                 reads=[mxs.ap()[:, :]], writes=[nmx.ap()[:, :]])
            for h in range(H):
                a.op("ACT", lambda e, h=h: e.activation(
                    qvU.ap()[:, h * 64:(h + 1) * 64], okf.ap()[:, h * 64:(h + 1) * 64],
                    AF.Exp, bias=nmx.ap()[:, h:h + 1],
                    accum_out=ssum.ap()[:, h:h + 1]),
                    reads=[okf.ap()[:, h * 64:(h + 1) * 64], nmx.ap()[:, h:h + 1]],
                    writes=[qvU.ap()[:, h * 64:(h + 1) * 64], ssum.ap()[:, h:h + 1]])
            a.op("DVE", lambda e: e.reciprocal(rsum.ap()[:, :], ssum.ap()[:, :]),
                 reads=[ssum.ap()[:, :]], writes=[rsum.ap()[:, :]])
            a.op("DVE", lambda e: e.tensor_scalar(rsum.ap()[:, :], rsum.ap()[:, :],
                                                  5.421010862427522e-20, None, ALU.mult),
                 reads=[rsum.ap()[:, :]], writes=[rsum.ap()[:, :]])
            a.op("DVE", lambda e: e.tensor_mul(qvU.ap()[:, :], qvU.ap()[:, :], eGc.ap()[:, :]),
                 reads=[qvU.ap()[:, :], eGc.ap()[:, :]], writes=[qvU.ap()[:, :]])
            for h in range(H):
                a.op("DVE", lambda e, h=h: e.tensor_scalar_mul(
                    qvt.ap()[:, h * 64:(h + 1) * 64], qvU.ap()[:, h * 64:(h + 1) * 64],
                    rsum.ap()[:, h:h + 1]),
                    reads=[qvU.ap()[:, h * 64:(h + 1) * 64], rsum.ap()[:, h:h + 1]],
                    writes=[qvt.ap()[:, h * 64:(h + 1) * 64]])
            for mt in range(4):
                a.dma("SP", qvT.ap()[:, mt * 128:(mt + 1) * 128],
                      qvt.ap()[:, mt * 128:(mt + 1) * 128], transpose=True)
            a.op("DVE", lambda e: e.reciprocal(nmx.ap()[:, 0:4], ELcol.ap()[:, :]),
                 reads=[ELcol.ap()[:, :]], writes=[nmx.ap()[:, 0:4]])
            a.op("DVE", lambda e: e.tensor_mul(nmx.ap()[:, 0:4], PLcol.ap()[:, :], nmx.ap()[:, 0:4]),
                 reads=[PLcol.ap()[:, :], nmx.ap()[:, 0:4]], writes=[nmx.ap()[:, 0:4]])
            a.op("DVE", lambda e: e.tensor_scalar(nmx.ap()[:, 0:4], nmx.ap()[:, 0:4],
                                                  18446744073709551616.0, None, ALU.mult),
                 reads=[nmx.ap()[:, 0:4]], writes=[nmx.ap()[:, 0:4]])
            for mt in range(4):
                a.op("DVE", lambda e, mt=mt: e.tensor_scalar_mul(
                    qhT.ap()[:, mt * 128:(mt + 1) * 128], qvT.ap()[:, mt * 128:(mt + 1) * 128],
                    nmx.ap()[:, mt:mt + 1]),
                    reads=[qvT.ap()[:, mt * 128:(mt + 1) * 128], nmx.ap()[:, mt:mt + 1]],
                    writes=[qhT.ap()[:, mt * 128:(mt + 1) * 128]])
            for h in range(H):
                mt, poff = h // 2, 64 * (h % 2)
                a.mm(ps3.ap()[:, h * 128:(h + 1) * 128],
                     BtT.ap()[poff:poff + 64, mt * 128:(mt + 1) * 128],
                     qvT.ap()[poff:poff + 64, mt * 128:(mt + 1) * 128],
                     True, True)
            for h in range(H):
                a.op("DVE", lambda e, h=h: e.tensor_mul(
                    W2m.ap()[:, h * 128:(h + 1) * 128],
                    ps3.ap()[:, h * 128:(h + 1) * 128], maskT64.ap()[:, :]),
                    reads=[ps3.ap()[:, h * 128:(h + 1) * 128], maskT64.ap()[:, :]],
                    writes=[W2m.ap()[:, h * 128:(h + 1) * 128]])
            for h in range(H):
                mt, poff = h // 2, 64 * (h % 2)
                a.mm(ps3.ap()[:, h * 128:(h + 1) * 128],
                     vv.ap()[:, lc * C + h * 128: lc * C + (h + 1) * 128],
                     W2m.ap()[:, h * 128:(h + 1) * 128], True, False)
                a.mm(ps3.ap()[:, h * 128:(h + 1) * 128],
                     hv.ap()[poff:poff + 64, mt * 128:(mt + 1) * 128],
                     qhT.ap()[poff:poff + 64, mt * 128:(mt + 1) * 128], False, True)
            if out_sc:
                for h in range(H):
                    a.op("DVE", lambda e, h=h: e.tensor_mul(
                        og.ap()[:, h * 128:(h + 1) * 128],
                        ps3.ap()[:, h * 128:(h + 1) * 128],
                        sgT.ap()[:, h * CH + lc * 128: h * CH + (lc + 1) * 128]),
                        reads=[ps3.ap()[:, h * 128:(h + 1) * 128],
                               sgT.ap()[:, h * CH + lc * 128: h * CH + (lc + 1) * 128]],
                        writes=[og.ap()[:, h * 128:(h + 1) * 128]])
            for h in range(H):
                a.mm(ps1.ap()[:, h * 64:(h + 1) * 64],
                     kNT.ap()[:, lc * C + h * 128: lc * C + (h + 1) * 128],
                     Bt.ap()[:, h * 64:(h + 1) * 64], True, True)
            for h in range(H):
                mt, poff = h // 2, 64 * (h % 2)
                a.mm(ps2.ap()[poff:poff + 64, mt * 128:(mt + 1) * 128],
                     Bt.ap()[:, h * 64:(h + 1) * 64],
                     vv.ap()[:, lc * C + h * 128: lc * C + (h + 1) * 128], True, True)
            a.op("DVE", lambda e: e.tensor_mul(wkA.ap()[:, :], ps1.ap()[:, :], ELb_s.ap()[:, :]),
                 reads=[ps1.ap()[:, :], ELb_s.ap()[:, :]], writes=[wkA.ap()[:, :]])
            a.op("DVE", lambda e: e.tensor_mul(wkB.ap()[:, :], hk.ap()[:, :], PLb_s.ap()[:, :]),
                 reads=[hk.ap()[:, :], PLb_s.ap()[:, :]], writes=[wkB.ap()[:, :]])
            a.op("DVE", lambda e: e.tensor_add(hk.ap()[:, :], wkA.ap()[:, :], wkB.ap()[:, :]),
                 reads=[wkA.ap()[:, :], wkB.ap()[:, :]], writes=[hk.ap()[:, :]])
            for mt in range(4):
                a.op("DVE", lambda e, mt=mt: e.tensor_scalar_mul(
                    wkA.ap()[:, mt * 128:(mt + 1) * 128], ps2.ap()[:, mt * 128:(mt + 1) * 128],
                    ELcol.ap()[:, mt:mt + 1]),
                    reads=[ps2.ap()[:, mt * 128:(mt + 1) * 128], ELcol.ap()[:, mt:mt + 1]],
                    writes=[wkA.ap()[:, mt * 128:(mt + 1) * 128]])
                a.op("DVE", lambda e, mt=mt: e.scalar_tensor_tensor(
                    hv.ap()[:, mt * 128:(mt + 1) * 128], hv.ap()[:, mt * 128:(mt + 1) * 128],
                    PLcol.ap()[:, mt:mt + 1], wkA.ap()[:, mt * 128:(mt + 1) * 128],
                    ALU.mult, ALU.add),
                    reads=[hv.ap()[:, mt * 128:(mt + 1) * 128], PLcol.ap()[:, mt:mt + 1],
                           wkA.ap()[:, mt * 128:(mt + 1) * 128]],
                    writes=[hv.ap()[:, mt * 128:(mt + 1) * 128]])
            if out_sc:
                a.op("ACT", lambda e: e.activation(okf.ap()[:, :], og.ap()[:, 0:CH], AF.Square),
                     reads=[og.ap()[:, 0:CH]], writes=[okf.ap()[:, :]])
                a.op("ACT", lambda e: e.activation(qvU.ap()[:, :], og.ap()[:, CH:2 * CH], AF.Square),
                     reads=[og.ap()[:, CH:2 * CH]], writes=[qvU.ap()[:, :]])
                for half in range(2):
                    for hh4 in range(4):
                        h = half * 4 + hh4
                        srcq = (okf.ap()[:, h * 128:(h + 1) * 128] if h < 4
                                else qvU.ap()[:, (h - 4) * 128:(h - 3) * 128])
                        a.mm(psR.ap()[0:1, hh4 * 128:(hh4 + 1) * 128],
                             onesf.ap()[:, :], srcq, True, True)
                    a.op("ACT", lambda e, half=half: e.activation(
                        att_t.ap()[0:1, half * CH:(half + 1) * CH], psR.ap()[0:1, :],
                        AF.Ln, scale=1.0 / 128.0, bias=eps_col.ap()[0:1, 0:1]),
                        reads=[psR.ap()[0:1, :], eps_col.ap()[0:1, 0:1]],
                        writes=[att_t.ap()[0:1, half * CH:(half + 1) * CH]])
                a.op("ACT", lambda e: e.activation(att_t.ap()[0:1, :], att_t.ap()[0:1, :],
                                                   AF.Exp, scale=-0.5),
                     reads=[att_t.ap()[0:1, :]], writes=[att_t.ap()[0:1, :]])
                a.op("DVE", lambda e: e.tensor_copy(eGLr.ap()[:, :], att_t.ap()[0:1, 0:512]),
                     reads=[att_t.ap()[0:1, 0:512]], writes=[eGLr.ap()[:, :]])
                a.op("DVE", lambda e: e.tensor_copy(eGcLr.ap()[:, :], att_t.ap()[0:1, 512:1024]),
                     reads=[att_t.ap()[0:1, 512:1024]], writes=[eGcLr.ap()[:, :]])
                a.mm(ps3.ap()[:, 0:CH], ones1.ap()[:, :], eGLr.ap()[:, :], True, True)
                a.mm(ps3.ap()[:, CH:2 * CH], ones1.ap()[:, :], eGcLr.ap()[:, :], True, True)
                a.op("DVE", lambda e: e.tensor_mul(og.ap()[:, :], og.ap()[:, :], ps3.ap()[:, :]),
                     reads=[og.ap()[:, :], ps3.ap()[:, :]], writes=[og.ap()[:, :]])
                for co in range(8):
                    a.op("DVE", lambda e, co=co: e.tensor_scalar_mul(
                        outN.ap()[:, co * 128:(co + 1) * 128], og.ap()[:, co * 128:(co + 1) * 128],
                        nw8.ap()[:, co:co + 1]),
                        reads=[og.ap()[:, co * 128:(co + 1) * 128], nw8.ap()[:, co:co + 1]],
                        writes=[outN.ap()[:, co * 128:(co + 1) * 128]])
                for half in range(2):
                    for ci in range(8):
                        a.mm(psV.ap()[:, half * CH:(half + 1) * CH],
                             outN.ap()[:, ci * 128:(ci + 1) * 128],
                             rhs_slab(wob, ci, half * CH, (half + 1) * CH),
                             ci == 0, ci == 7)
                a.op("DVE", lambda e: e.tensor_copy(att_t.ap()[:, :], psV.ap()[:, :]),
                     reads=[psV.ap()[:, :]], writes=[att_t.ap()[:, :]])
                row0 = (sc - 1) * CH + lc * 128
                a.dma("POOL", at_d[row0:row0 + 128, :], att_t.ap()[:, :])
        if sc == 0:
            a.op("DVE", lambda e: e.tensor_scalar_mul(hk.ap()[:, :], hk.ap()[:, :], mask8.ap()[:, 0:1]),
                 reads=[hk.ap()[:, :], mask8.ap()[:, 0:1]], writes=[hk.ap()[:, :]])
            a.op("DVE", lambda e: e.tensor_scalar_mul(hv.ap()[:, :], hv.ap()[:, :], mask8.ap()[:, 0:1]),
                 reads=[hv.ap()[:, :], mask8.ap()[:, 0:1]], writes=[hv.ap()[:, :]])

    if ctx is None:
        a.finish()
    return nc


def _run_ph4(inp, x1, nsc=9):
    nc = build_ph4(nsc)
    base = {
        "Wq": np.asarray(inp["L1_attn_Wq"], np.float32),
        "Wk1": np.asarray(inp["L1_attn_Wk"], np.float32),
        "Wv1": np.asarray(inp["L1_attn_Wv"], np.float32),
        "Wo1": np.asarray(inp["L1_attn_Wo"], np.float32),
        "Wf1": np.asarray(inp["L1_attn_Wf1"], np.float32),
        "Wf2": np.asarray(inp["L1_attn_Wf2"], np.float32),
        "Wg1": np.asarray(inp["L1_attn_Wg1"], np.float32),
        "Wg2": np.asarray(inp["L1_attn_Wg2"], np.float32),
        "nw": _coltiles(inp["L1_attn_norm_w"]),
    }
    in_maps = []
    for core in range(N_CORES):
        m = dict(base)
        m["x1"] = x1[core]
        m["mask"] = np.full((128, 1), 0.0 if core % 4 == 0 else 1.0, np.float32)
        in_maps.append(m)
    res = run_bass_kernel_spmd(nc, in_maps, core_ids=list(range(N_CORES)))
    return res.results


def build_ph5(nch5=8, ctx=None):
    nc, a, mk_dram, sb_raw, ps_raw = ctx if ctx is not None else _own_ctx()
    HID = 3072
    n_out = nch5 * CH
    x1_d = mk_dram("x1", [n_out + CH, C], F32, "ExternalInput")
    at_d = mk_dram("attn1", [n_out, C], F32, "ExternalInput")
    wgT_d = mk_dram("wgT", [C, HID], I32, "ExternalInput")
    wuT_d = mk_dram("wuT", [C, HID], I32, "ExternalInput")
    wdT_d = mk_dram("wdT", [HID, C], I32, "ExternalInput")
    out_d = mk_dram("x2", [n_out, C], F32, "ExternalOutput")

    bufs = {}

    def sb(name, shape, dt=F32):
        t = sb_raw(name, shape, dt)
        bufs[name] = t
        return t

    wst = sb("wst", [128, 2 * 512], I32)
    wgb = load_grid(a, nc, "wg1t", wgT_d, 8, HID, st=wst, cw=512, alloc=sb_raw)
    wub = load_grid(a, nc, "wu1t", wuT_d, 8, HID, st=wst, cw=512, alloc=sb_raw)
    wd_sp = mk_dram("wd_sp", [24 * 128, C], BF16, "Internal")
    wd_stage = sb("wd_stage", [128, C], BF16)
    for hi in range(24):
        for half in range(2):
            p = half
            a.dma("SP", wst.ap()[:, p * 512:(p + 1) * 512],
                  wdT_d[128 * hi:128 * (hi + 1), half * 512:(half + 1) * 512])
            dsts = wd_stage.ap()[:, half * 512:(half + 1) * 512]
            a.op("DVE", lambda e, d=dsts, p=p: e.tensor_copy(
                d, wst.ap()[:, p * 512:(p + 1) * 512]),
                reads=[wst.ap()[:, p * 512:(p + 1) * 512]], writes=[dsts])
        a.dma("POOL", wd_sp[128 * hi:128 * (hi + 1), :], wd_stage.ap()[:, :])
    wd_sl = [sb(f"wd_sl{p}", [128, C], BF16) for p in range(2)]

    NT = CH // 128
    x1_c = sb("x1_c", [128, NT * C])
    xn1 = sb("xn1", [128, NT * C], BF16)
    xn1T = sb("xn1T", [128, 8 * CH], BF16)
    sgP = [sb(f"sg{i}", [128, CH], BF16) for i in range(2)]
    hT = sb("hT", [128, 24 * CH], BF16)
    resP = [sb(f"res{i}", [128, C]) for i in range(2)]
    class _TView:
        def __init__(self, apv):
            self._ap = apv

        def ap(self):
            return self._ap
    bufs["junk_shared"] = _TView(wst.ap().bitcast(F32)[:, 0:C])
    bufs["lnE_junk"] = bufs["junk_shared"]
    alloc_ln_bufs(a, nc, bufs, "lnE", NT, alloc=sb_raw)
    eps_col = sb("eps_col", [128, 1])
    a.op("DVE", lambda e: e.memset(eps_col.ap()[:, :], 1e-5), writes=[eps_col.ap()[:, :]])

    ps_gP = [ps_raw(f"ps_g{i}", [128, CH], F32) for i in range(2)]
    ps_uP = [ps_raw(f"ps_u{i}", [128, CH], F32) for i in range(2)]
    ps_dP = [ps_raw(f"ps_d{i}", [128, C], F32) for i in range(2)]

    for ch in range(nch5):
        for tt in range(NT):
            a.dma("SP", x1_c.ap()[:, tt * C:(tt + 1) * C],
                  x1_d[CH + ch * CH + tt * 128: CH + ch * CH + (tt + 1) * 128, :])
        x1_tiles = [x1_c.ap()[:, tt * C:(tt + 1) * C] for tt in range(NT)]
        xn_tiles = [xn1.ap()[:, tt * C:(tt + 1) * C] for tt in range(NT)]
        layernorm_tiles(a, nc, bufs, x1_tiles, xn_tiles, tag="lnE")
        for tt in range(NT):
            for ci in range(8):
                a.dma("SP", xn1T.ap()[:, ci * CH + tt * 128: ci * CH + (tt + 1) * 128],
                      xn1.ap()[:, tt * C + ci * 128: tt * C + (ci + 1) * 128],
                      transpose=True)
        for ht in range(24):
            ps_g = ps_gP[ht % 2]; ps_u = ps_uP[ht % 2]; sg = sgP[ht % 2]
            for ci in range(8):
                a.mm(ps_g.ap()[:, :], lhsT_tile(wgb, ci, ht, HID),
                     xn1T.ap()[:, ci * CH:(ci + 1) * CH], ci == 0, ci == 7)
            for ci in range(8):
                a.mm(ps_u.ap()[:, :], lhsT_tile(wub, ci, ht, HID),
                     xn1T.ap()[:, ci * CH:(ci + 1) * CH], ci == 0, ci == 7)
            a.op("ACT", lambda e, ps_g=ps_g, sg=sg: e.activation(
                sg.ap()[:, :], ps_g.ap()[:, :], AF.Silu, scale=1.0 / 127.0),
                 reads=[ps_g.ap()[:, :]], writes=[sg.ap()[:, :]])
            o = hT.ap()[:, ht * CH:(ht + 1) * CH]
            a.op("DVE", lambda e, o=o, sg=sg, ps_u=ps_u: e.tensor_mul(
                o, sg.ap()[:, :], ps_u.ap()[:, :]),
                 reads=[sg.ap()[:, :], ps_u.ap()[:, :]], writes=[o])
        for tt in range(NT):
            ps_d = ps_dP[tt % 2]; res_t = resP[tt % 2]
            for hi in range(24):
                slp = wd_sl[hi % 2]
                a.dma("SP", slp.ap()[:, :], wd_sp[128 * hi:128 * (hi + 1), :])
                lhs = hT.ap()[:, hi * CH + tt * 128: hi * CH + (tt + 1) * 128]
                a.mm(ps_d.ap()[:, 0:512], lhs, slp.ap()[:, 0:512],
                     hi == 0, hi == 23)
                a.mm(ps_d.ap()[:, 512:1024], lhs, slp.ap()[:, 512:1024],
                     hi == 0, hi == 23)
            row0 = ch * CH + tt * 128
            a.dma("SP", res_t.ap()[:, :], at_d[row0:row0 + 128, :])
            a.op("DVE", lambda e, ps_d=ps_d, res_t=res_t: e.scalar_tensor_tensor(
                res_t.ap()[:, :], ps_d.ap()[:, :], 1.0 / 16129.0,
                res_t.ap()[:, :], ALU.mult, ALU.add),
                reads=[ps_d.ap()[:, :], res_t.ap()[:, :]],
                writes=[res_t.ap()[:, :]])
            a.op("DVE", lambda e, tt=tt, res_t=res_t: e.tensor_add(
                res_t.ap()[:, :], res_t.ap()[:, :], x1_c.ap()[:, tt * C:(tt + 1) * C]),
                 reads=[res_t.ap()[:, :], x1_c.ap()[:, tt * C:(tt + 1) * C]],
                 writes=[res_t.ap()[:, :]])
            a.dma("POOL", out_d[row0:row0 + 128, :], res_t.ap()[:, :])

    if ctx is None:
        a.finish()
    return nc


def _run_ph5(inp, x1, attn1, nch5=8):
    nc = build_ph5(nch5)
    base = {
        "wgT": np.ascontiguousarray(np.asarray(inp["L1_ffn_w_gate"], np.int32).T),
        "wuT": np.ascontiguousarray(np.asarray(inp["L1_ffn_w_up"], np.int32).T),
        "wdT": np.ascontiguousarray(np.asarray(inp["L1_ffn_w_down"], np.int32).T),
    }
    in_maps = []
    for core in range(N_CORES):
        m = dict(base)
        m["x1"] = x1[core]
        m["attn1"] = attn1[core]
        in_maps.append(m)
    res = run_bass_kernel_spmd(nc, in_maps, core_ids=list(range(N_CORES)))
    return res.results
